# version 04
# baseline (speedup 1.0000x reference)
import os
import numpy as np
import ml_dtypes
import concourse.bass as bass
import concourse.mybir as mybir
from concourse.bass_utils import run_bass_kernel_spmd

F32 = mybir.dt.float32
BF16 = mybir.dt.bfloat16
I32 = mybir.dt.int32
AF = mybir.ActivationFunctionType
ALU = mybir.AluOpType
AX = mybir.AxisListType.X

D = 2048
T = 2048
H = 16
DK = 64
DV = 128
M = 64
C = 128
NCH = T // C
G = 512
NG = T // G
HID = 5632
NHC = HID // 128
NDC = D // 128
NCORES = 8

CONSTS = [0.0, 1.0, -1.0, -0.5, -32.0, 1e-5, 1.0 / 128.0, 0.125, 1.0 / 2048.0]
_BF = ml_dtypes.bfloat16

_KPHASE = os.environ.get("KPHASE", "all")
_KDBG = os.environ.get("KDBG", "0") == "1"
_PHASES = ["b1", "b2", "b3", "b4", "b5", "b6"]


def _upto(ph):
    return _KPHASE == "all" or _PHASES.index(ph) <= _PHASES.index(_KPHASE)


class Sem:
    def __init__(self, h):
        self.h = h
        self.n = 0

    def inc(self, instr, amt=1):
        instr.then_inc(self.h, amt)
        self.n += amt
        return instr


class Ctx:
    pass


def build():
    nc = bass.Bass("TRN2", target_bir_lowering=False)
    x = Ctx()
    x.nc = nc
    scr_kind = "ExternalOutput" if _KDBG else "Internal"

    x.x_fm = nc.dram_tensor("x_fm", [D, T], F32, kind="ExternalInput").ap()
    x.wq = nc.dram_tensor("wq", [D, H * DK], F32, kind="ExternalInput").ap()
    x.wk = nc.dram_tensor("wk", [D, H * DK], F32, kind="ExternalInput").ap()
    x.ws = nc.dram_tensor("ws", [D, H * M], F32, kind="ExternalInput").ap()
    x.wv = nc.dram_tensor("wv", [D, H * DV], F32, kind="ExternalInput").ap()
    x.wg = nc.dram_tensor("wg", [D, H * DV], F32, kind="ExternalInput").ap()
    x.wo = nc.dram_tensor("wo", [D, D], F32, kind="ExternalInput").ap()
    x.wgT = nc.dram_tensor("wgT", [D, HID], I32, kind="ExternalInput").ap()
    x.wuT = nc.dram_tensor("wuT", [D, HID], I32, kind="ExternalInput").ap()
    x.wdT = nc.dram_tensor("wdT", [HID, D], I32, kind="ExternalInput").ap()
    x.sgate = nc.dram_tensor("sgate", [128, NHC], F32, kind="ExternalInput").ap()
    x.sup = nc.dram_tensor("sup", [128, NHC], F32, kind="ExternalInput").ap()
    x.sdown = nc.dram_tensor("sdown", [128, NDC], F32, kind="ExternalInput").ap()
    x.gnw = nc.dram_tensor("gnw", [128, 1], F32, kind="ExternalInput").ap()
    x.m1u_i = nc.dram_tensor("m1u", [128, 128], BF16, kind="ExternalInput").ap()
    x.m1u8_i = nc.dram_tensor("m1u8", [128, 8 * 128], BF16, kind="ExternalInput").ap()
    x.ident_i = nc.dram_tensor("ident", [128, 128], BF16, kind="ExternalInput").ap()
    x.cst_i = nc.dram_tensor("cst", [128, len(CONSTS)], F32, kind="ExternalInput").ap()
    x.mask_i = nc.dram_tensor("maskpre", [128, NCORES], F32, kind="ExternalInput").ap()
    x.onescol_i = nc.dram_tensor("onescol", [128, 1], BF16, kind="ExternalInput").ap()
    x.onesrow_i = nc.dram_tensor("onesrow", [1, 128], F32, kind="ExternalInput").ap()

    x.out_fm = nc.dram_tensor("out_fm", [D, T], F32, kind="ExternalOutput").ap()

    x.q_scr = nc.dram_tensor("q_scr", [H, DK, T], BF16, kind=scr_kind).ap()
    x.k_scr = nc.dram_tensor("k_scr", [H, DK, T], BF16, kind=scr_kind).ap()
    x.e_scr = nc.dram_tensor("e_scr", [H, M, T], BF16, kind=scr_kind).ap()
    x.v_scr = nc.dram_tensor("v_scr", [H, DV, T], BF16, kind=scr_kind).ap()
    x.g_scr = nc.dram_tensor("g_scr", [H, DV, T], BF16, kind=scr_kind).ap()
    x.x2_scr = nc.dram_tensor("x2_scr", [NDC, 128, T], F32, kind=scr_kind).ap()
    x.hid_scr = nc.dram_tensor("hid_scr", [NHC, 128, T], BF16, kind=scr_kind).ap()
    x.xln2_scr = nc.dram_tensor("xln2_scr", [NDC, 128, T], BF16, kind=scr_kind).ap()
    x.seg_d = nc.dram_tensor("seg_d", [128, 4096], F32).ap()
    if _KDBG:
        x.dbg_denr = nc.dram_tensor("dbg_denr", [2, 128, 1024], F32, kind="ExternalOutput").ap()
        x.dbg_og = nc.dram_tensor("dbg_og", [128, H * 1024], mybir.dt.bfloat16, kind="ExternalOutput").ap()
        x.dbg_ov = nc.dram_tensor("dbg_ov", [128, H * DV], F32, kind="ExternalOutput").ap()
        x.dbg_pvt = nc.dram_tensor("dbg_pvt", [128, H * M], mybir.dt.bfloat16, kind="ExternalOutput").ap()
        x.dbg_rs = nc.dram_tensor("dbg_rs", [128, 16], F32, kind="ExternalOutput").ap()
        x.dbg_pvf = nc.dram_tensor("dbg_pvf", [64, H * C], mybir.dt.bfloat16, kind="ExternalOutput").ap()
        x.dbg_w2 = nc.dram_tensor("dbg_w2", [128, H * C], mybir.dt.bfloat16, kind="ExternalOutput").ap()
    x.gath_d = nc.dram_tensor("gath_d", [NCORES * 128, 4096], F32).ap()

    x.e_rows = x.e_scr.rearrange("h m t -> (h m) t")
    x.k_rows = x.k_scr.rearrange("h m t -> (h m) t")
    x.v_rows = x.v_scr.rearrange("h m t -> (h m) t")

    from contextlib import ExitStack
    es = ExitStack()
    with es:
        x.cst = es.enter_context(nc.sbuf_tensor("cst_sb", [128, len(CONSTS)], F32))
        x.m1u = es.enter_context(nc.sbuf_tensor("m1u_sb", [128, 128], BF16))
        x.m1u8 = es.enter_context(nc.sbuf_tensor("m1u8_sb", [128, 8 * 128], BF16))
        x.ident = es.enter_context(nc.sbuf_tensor("ident_sb", [128, 128], BF16))
        x.gnw_sb = es.enter_context(nc.sbuf_tensor("gnw_sb", [128, 1], F32))
        x.mask_sb = es.enter_context(nc.sbuf_tensor("mask_sb", [128, NCORES], F32))
        x.onescol = es.enter_context(nc.sbuf_tensor("onescol_sb", [128, 1], BF16))
        x.onesrow = es.enter_context(nc.sbuf_tensor("onesrow_sb", [1, 128], F32))
        x.sg_sc = es.enter_context(nc.sbuf_tensor("sg_sc", [128, NHC], F32))
        x.su_sc = es.enter_context(nc.sbuf_tensor("su_sc", [128, NHC], F32))
        x.sd_sc = es.enter_context(nc.sbuf_tensor("sd_sc", [128, NDC], F32))

        for i, vv in enumerate(CONSTS):
            nc.const_aps.register(F32, vv, x.cst[:, i:i + 1])

        x.s_ld = Sem(es.enter_context(nc.semaphore("s_ld")))
        x.s_wr = Sem(es.enter_context(nc.semaphore("s_wr")))
        x.s_gd = Sem(es.enter_context(nc.semaphore("s_gd")))
        x.s_cc = Sem(es.enter_context(nc.semaphore("s_cc")))
        x.s_pe = Sem(es.enter_context(nc.semaphore("s_pe")))
        x.s_v = Sem(es.enter_context(nc.semaphore("s_v")))
        x.s_s = Sem(es.enter_context(nc.semaphore("s_s")))
        x.s_g = Sem(es.enter_context(nc.semaphore("s_g")))

        with nc.Block() as blk:
            @blk.sync
            def _(sy):
                for dst, src in [(x.cst, x.cst_i), (x.m1u, x.m1u_i), (x.m1u8, x.m1u8_i),
                                 (x.ident, x.ident_i), (x.gnw_sb, x.gnw),
                                 (x.mask_sb, x.mask_i), (x.onescol, x.onescol_i),
                                 (x.onesrow, x.onesrow_i), (x.sg_sc, x.sgate),
                                 (x.su_sc, x.sup), (x.sd_sc, x.sdown)]:
                    x.s_ld.inc(sy.dma_start(out=dst[:, :], in_=src), 16)

            @blk.vector
            def _(v):
                v.wait_ge(x.s_ld.h, x.s_ld.n)

        if _upto("b1"):
            with nc.sbuf_tensor("xln", [128, NDC * T], BF16) as _xln_flat, \
                 nc.Block() as blk:
                xln = _xln_flat
                _emit_ln(x, blk, x.x_fm, xln, "ln1")
                _emit_proj(x, blk, xln)

        if _upto("b2"):
            with nc.sbuf_tensor("acc", [128, 4096], F32) as acc, \
                 nc.sbuf_tensor("hkb2", [64, 2 * H * M], BF16) as hkb2, \
                 nc.sbuf_tensor("hvb2", [64, 2 * H * DV], BF16) as hvb2, \
                 nc.sbuf_tensor("wsnap", [1, 2 * H * M], F32) as wsnap:
                x.acc, x.hkb2, x.hvb2, x.wsnap = acc, hkb2, hvb2, wsnap
                with nc.Block() as blk:
                    _emit_states_exchange(x, blk)
                if _upto("b3"):
                    with nc.Block() as blk:
                        _emit_attn(x, blk)

        if _upto("b4"):
            with nc.sbuf_tensor("xln2", [128, NDC * T], BF16) as xln2:
                x.xln2 = xln2
                with nc.Block() as blk:
                    _emit_ln(x, blk, x.x2_scr.rearrange("a p t -> (a p) t"), x.xln2, "ln2")

                    @blk.scalar
                    def _(sc):
                        sc.wait_ge(x.s_v.h, x.s_v.n)
                        sc.wait_ge(x.s_g.h, x.s_g.n)
                        for dc in range(NDC):
                            x.s_wr.inc(sc.dma_start(
                                out=x.xln2_scr[dc],
                                in_=_v3(x.xln2, NDC, T)[:, dc, :]), 16)
            if _upto("b5"):
                with nc.Block() as blk:
                    _emit_ffn_fused(x, blk)

        with nc.Block() as blk:
            @blk.sync
            def _(sy):
                sy.wait_ge(x.s_wr.h, x.s_wr.n)
                sy.wait_ge(x.s_ld.h, x.s_ld.n)
                sy.wait_ge(x.s_gd.h, x.s_gd.n)

    return nc


def _v3(t2d, a, b):
    return t2d[:, :].rearrange("p (a b) -> p a b", a=a)


def _emit_ln(x, blk, src_fm, dst, tag):
    nc = x.nc
    src3 = src_fm.rearrange("(a p) t -> p a t", p=128)
    dst3 = _v3(dst, NDC, T)
    with nc.sbuf_tensor(f"{tag}_xg", [128, NDC * G], F32) as xg2, \
         nc.sbuf_tensor(f"{tag}_xb", [128, NDC * G], BF16) as xb2, \
         nc.sbuf_tensor(f"{tag}_sq", [128, NDC * G], BF16) as sq2, \
         nc.sbuf_tensor(f"{tag}_rows", [1, G * 6], F32) as rows, \
         nc.sbuf_tensor(f"{tag}_bc", [128, 2 * G], F32) as bc, \
         nc.psum_tensor(f"{tag}_psS", [1, G], F32) as psS, \
         nc.psum_tensor(f"{tag}_psQ", [1, G], F32) as psQ, \
         nc.psum_tensor(f"{tag}_psB", [128, 2 * G], F32) as psB:
        xg = _v3(xg2, NDC, G)
        xb = _v3(xb2, NDC, G)
        sq = _v3(sq2, NDC, G)
        mu, msq, tt, var, rstd, mup = (rows[0:1, i * G:(i + 1) * G] for i in range(6))
        for g in range(NG):
            @blk.sync
            def _(sy, g=g):
                sy.wait_ge(x.s_v.h, x.s_v.n)
                sy.wait_ge(x.s_g.h, x.s_g.n)
                sy.wait_ge(x.s_wr.h, x.s_wr.n)
                x.s_ld.inc(sy.dma_start(out=xg[:, :, :],
                                        in_=src3[:, :, g * G:(g + 1) * G]), 16)

            @blk.vector
            def _(v, g=g):
                v.wait_ge(x.s_ld.h, x.s_ld.n)
                v.wait_ge(x.s_pe.h, x.s_pe.n)
                v.tensor_copy(xb2[:, :], xg2[:, :])
                x.s_v.inc(v.tensor_tensor(sq2[:, :], xb2[:, :], xb2[:, :], ALU.mult))

            @blk.tensor
            def _(t, g=g):
                t.wait_ge(x.s_v.h, x.s_v.n)
                for dc in range(NDC):
                    t.matmul(psS[:, :], x.onescol[:, :], xb[:, dc, :],
                             start=(dc == 0), stop=(dc == NDC - 1))
                for dc in range(NDC):
                    ins = t.matmul(psQ[:, :], x.onescol[:, :], sq[:, dc, :],
                                   start=(dc == 0), stop=(dc == NDC - 1))
                x.s_pe.inc(ins)

            @blk.vector
            def _(v, g=g):
                v.wait_ge(x.s_pe.h, x.s_pe.n)
                v.tensor_scalar(mu, psS[:, :], 1.0 / D, None, ALU.mult)
                v.tensor_scalar(msq, psQ[:, :], 1.0 / D, None, ALU.mult)
                v.tensor_tensor(tt, mu, mu, ALU.mult)
                x.s_v.inc(v.tensor_tensor(var, msq, tt, ALU.subtract))

            @blk.scalar
            def _(sc, g=g):
                sc.wait_ge(x.s_v.h, x.s_v.n)
                sc.activation(tt, var, AF.Ln, bias=1e-5, scale=1.0)
                x.s_s.inc(sc.activation(rstd, tt, AF.Exp, bias=0.0, scale=-0.5))

            @blk.vector
            def _(v, g=g):
                v.wait_ge(x.s_s.h, x.s_s.n)
                x.s_v.inc(v.scalar_tensor_tensor(mup, mu, -1.0, rstd, ALU.mult, ALU.mult))

            @blk.tensor
            def _(t, g=g):
                t.wait_ge(x.s_v.h, x.s_v.n)
                t.matmul(psB[:, 0:G], x.onesrow[:, :], rstd, start=True, stop=True)
                ins = t.matmul(psB[:, G:2 * G], x.onesrow[:, :], mup, start=True, stop=True)
                x.s_pe.inc(ins)

            @blk.vector
            def _(v, g=g):
                v.wait_ge(x.s_pe.h, x.s_pe.n)
                x.s_v.inc(v.tensor_copy(bc[:, :], psB[:, :]))

            NPOOL = 6

            @blk.gpsimd
            def _(gp, g=g):
                gp.wait_ge(x.s_v.h, x.s_v.n)
                for dc in range(NDC - NPOOL, NDC):
                    gp.tensor_tensor(xg[:, dc, :], xg[:, dc, :], bc[:, 0:G], ALU.mult)
                    ins = gp.tensor_tensor(dst3[:, dc, g * G:(g + 1) * G],
                                           xg[:, dc, :], bc[:, G:2 * G], ALU.add)
                x.s_g.inc(ins)

            @blk.vector
            def _(v, g=g):
                for dc in range(NDC - NPOOL):
                    v.tensor_tensor(xg[:, dc, :], xg[:, dc, :], bc[:, 0:G], ALU.mult)
                    ins = v.tensor_tensor(dst3[:, dc, g * G:(g + 1) * G],
                                          xg[:, dc, :], bc[:, G:2 * G], ALU.add)
                x.s_v.inc(ins)


def _emit_proj(x, blk, xln):
    nc = x.nc
    xln3 = _v3(xln, NDC, T)
    jobs = []
    for kind, w, scr in [("q", x.wq, x.q_scr), ("k", x.wk, x.k_scr),
                         ("s", x.ws, x.e_scr), ("v", x.wv, x.v_scr),
                         ("g", x.wg, x.g_scr)]:
        for cc in range(w.shape[1] // 128):
            jobs.append((kind, w, cc, scr))

    with nc.sbuf_tensor("pj_slab", [128, 2 * NDC * 128], BF16) as slab, \
         nc.sbuf_tensor("pj_ev", [128, 2 * T], BF16) as ev, \
         nc.sbuf_tensor("pj_sf", [128, T], F32) as sf, \
         nc.psum_tensor("pj_psA", [128, T], F32) as psA, \
         nc.psum_tensor("pj_psB", [128, T], F32) as psB:
        pss = [psA, psB]
        pe_evt, ev_evt, wr_evt, ld_evt = [], [], [], []

        for j, (kind, w, cc, scr) in enumerate(jobs):
            ps = pss[j % 2]
            slab_b = slab[:, (j % 2) * NDC * 128:((j % 2) + 1) * NDC * 128]
            sl3 = slab_b.rearrange("p (a c) -> p a c", a=NDC)
            ev_b = ev[:, (j % 2) * T:((j % 2) + 1) * T]

            @blk.gpsimd
            def _(gp, j=j, w=w, cc=cc, slab_b=slab_b):
                if j >= 2:
                    gp.wait_ge(x.s_pe.h, pe_evt[j - 2])
                ins = gp.dma_start(
                    out=slab_b.rearrange("p (a c) -> p a c", a=NDC),
                    in_=w[:, cc * 128:(cc + 1) * 128].rearrange("(a p) c -> p a c", p=128))
                x.s_gd.inc(ins, 16)
            ld_evt.append(x.s_gd.n)

            @blk.tensor
            def _(t, j=j, ps=ps, sl3=sl3):
                t.wait_ge(x.s_gd.h, ld_evt[j])
                if j >= 2:
                    t.wait_ge(x.s_v.h, ev_evt[j - 2][0])
                    t.wait_ge(x.s_s.h, ev_evt[j - 2][1])
                if j == 0:
                    t.wait_ge(x.s_v.h, x.s_v.n)
                    t.wait_ge(x.s_g.h, x.s_g.n)
                for g in range(NG):
                    for dc in range(NDC):
                        ins = t.matmul(ps[:, g * G:(g + 1) * G], sl3[:, dc, :],
                                       xln3[:, dc, g * G:(g + 1) * G],
                                       start=(dc == 0), stop=(dc == NDC - 1))
                x.s_pe.inc(ins)
            pe_evt.append(x.s_pe.n)

            if kind in ("k", "v"):
                @blk.vector
                def _(v, j=j, ps=ps, ev_b=ev_b):
                    v.wait_ge(x.s_pe.h, pe_evt[j])
                    if j >= 2:
                        v.wait_ge(x.s_wr.h, wr_evt[j - 2])
                    x.s_v.inc(v.tensor_copy(ev_b, ps[:, :]))
            elif kind == "g":
                @blk.scalar
                def _(sc, j=j, ps=ps, ev_b=ev_b):
                    sc.wait_ge(x.s_pe.h, pe_evt[j])
                    if j >= 2:
                        sc.wait_ge(x.s_wr.h, wr_evt[j - 2])
                    x.s_s.inc(sc.activation(ev_b, ps[:, :], AF.Silu,
                                            bias=0.0, scale=1.0))
            elif kind == "q":
                @blk.scalar
                def _(sc, j=j, ps=ps, ev_b=ev_b):
                    sc.wait_ge(x.s_pe.h, pe_evt[j])
                    if j >= 2:
                        sc.wait_ge(x.s_wr.h, wr_evt[j - 2])
                    x.s_s.inc(sc.activation(ev_b, ps[:, :], AF.Copy,
                                            bias=0.0, scale=0.125))
            else:
                @blk.vector
                def _(v, j=j, ps=ps):
                    v.wait_ge(x.s_pe.h, pe_evt[j])
                    v.wait_ge(x.s_s.h, x.s_s.n)
                    x.s_v.inc(v.tensor_scalar(sf[:, :], ps[:, :], 32.0, -32.0,
                                              ALU.min, ALU.max))

                @blk.scalar
                def _(sc, j=j, ev_b=ev_b):
                    sc.wait_ge(x.s_v.h, x.s_v.n)
                    if j >= 2:
                        sc.wait_ge(x.s_wr.h, wr_evt[j - 2])
                    x.s_s.inc(sc.activation(ev_b, sf[:, :], AF.Exp,
                                            bias=-32.0, scale=1.0))
            ev_evt.append((x.s_v.n, x.s_s.n))

            @blk.scalar
            def _(sc, j=j, cc=cc, scr=scr, ev_b=ev_b):
                sc.wait_ge(x.s_v.h, ev_evt[j][0])
                sc.wait_ge(x.s_s.h, ev_evt[j][1])
                if scr.shape[1] == 64:
                    x.s_wr.inc(sc.dma_start(out=scr[2 * cc], in_=ev_b[0:64, :]), 16)
                    x.s_wr.inc(sc.dma_start(out=scr[2 * cc + 1], in_=ev_b[64:128, :]), 16)
                else:
                    x.s_wr.inc(sc.dma_start(out=scr[cc], in_=ev_b[:, :]), 16)
            wr_evt.append(x.s_wr.n)


def _emit_chunk_states(x, blk, etm, ktm, vtm, hk_dst, hv2_dst, w_dst,
                       psF, psG1, psG2, wait_s=False):

    @blk.tensor
    def _(t):
        t.wait_ge(x.s_v.h, x.s_v.n)
        if wait_s:
            t.wait_ge(x.s_s.h, x.s_s.n)
        for h in range(H):
            t.matmul(psF[0:64, h * M:(h + 1) * M], ktm[:, h * DK:(h + 1) * DK],
                     etm[:, h * M:(h + 1) * M], start=True, stop=True)
        for h in range(H):
            ps = psG1 if h < 8 else psG2
            ins = t.matmul(ps[0:64, (h % 8) * DV:((h % 8) + 1) * DV],
                           etm[:, h * M:(h + 1) * M],
                           vtm[:, h * DV:(h + 1) * DV], start=True, stop=True)
        x.s_pe.inc(ins)

    @blk.vector
    def _(v):
        v.wait_ge(x.s_pe.h, x.s_pe.n)
        v.tensor_tensor(hk_dst, hk_dst, psF[0:64, 0:H * M], ALU.add)
        v.tensor_tensor(hv2_dst[0], hv2_dst[0], psG1[0:64, 0:8 * DV], ALU.add)
        x.s_v.inc(v.tensor_tensor(hv2_dst[1], hv2_dst[1], psG2[0:64, 0:8 * DV], ALU.add))


def _emit_states_exchange(x, blk):
    nc = x.nc
    with nc.sbuf_tensor("seg", [128, 4096], F32) as seg, \
         nc.sbuf_tensor("b2_etm", [128, 2 * H * M], BF16) as etm2, \
         nc.sbuf_tensor("b2_ktm", [128, 2 * H * DK], BF16) as ktm2, \
         nc.sbuf_tensor("b2_vtm", [128, 2 * H * DV], BF16) as vtm2, \
         nc.sbuf_tensor("b2_slab", [128, 2 * 4096], F32) as slabr, \
         nc.psum_tensor("b2_psF", [64, H * M], F32) as psF, \
         nc.psum_tensor("b2_psG1", [64, 8 * DV], F32) as psG1, \
         nc.psum_tensor("b2_psG2", [64, 8 * DV], F32) as psG2, \
         nc.psum_tensor("b2_psH", [1, H * M], F32) as psH:

        hk = seg[0:64, 0:H * M]
        hv1 = seg[0:64, 1024:1024 + 8 * DV]
        hv2 = seg[0:64, 1024 + 8 * DV:1024 + 16 * DV]
        wrow = seg[0:1, 3072:3072 + H * M]

        @blk.vector
        def _(v):
            x.s_v.inc(v.memset(seg[:, :], 0.0))

        for c in range(NCH):
            b = c % 2
            etm = etm2[:, b * H * M:(b + 1) * H * M]
            ktm = ktm2[:, b * H * DK:(b + 1) * H * DK]
            vtm = vtm2[:, b * H * DV:(b + 1) * H * DV]

            @blk.sync
            def _(sy, c=c, etm=etm, ktm=ktm, vtm=vtm):
                if c >= 2:
                    sy.wait_ge(x.s_pe.h, x.s_pe.n)
                sy.wait_ge(x.s_wr.h, x.s_wr.n)
                x.s_ld.inc(sy.dma_start(out=etm, in_=x.e_rows[:, c * C:(c + 1) * C],
                                        transpose=True), 16)
                x.s_ld.inc(sy.dma_start(out=ktm, in_=x.k_rows[:, c * C:(c + 1) * C],
                                        transpose=True), 16)
                x.s_ld.inc(sy.dma_start(out=vtm, in_=x.v_rows[:, c * C:(c + 1) * C],
                                        transpose=True), 16)

            @blk.tensor
            def _(t, c=c, etm=etm):
                t.wait_ge(x.s_ld.h, x.s_ld.n)
                t.wait_ge(x.s_v.h, x.s_v.n)
                t.matmul(psH[0:1, 0:512], x.onescol[:, :], etm[:, 0:512],
                         start=True, stop=True)
                ins = t.matmul(psH[0:1, 512:1024], x.onescol[:, :], etm[:, 512:1024],
                               start=True, stop=True)
                x.s_pe.inc(ins)

            @blk.vector
            def _(v, c=c):
                v.wait_ge(x.s_pe.h, x.s_pe.n)
                x.s_v.inc(v.tensor_tensor(wrow, wrow, psH[0:1, :], ALU.add))

            _emit_chunk_states(x, blk, etm, ktm, vtm, hk, (hv1, hv2), wrow,
                               psF, psG1, psG2)

        @blk.gpsimd
        def _(g):
            g.wait_ge(x.s_v.h, x.s_v.n)
            x.s_gd.inc(g.dma_start(out=x.seg_d, in_=seg[:, :]), 16)
            g.wait_ge(x.s_gd.h, x.s_gd.n)
            x.s_cc.inc(g.collective_compute(
                "AllGather", ALU.bypass,
                replica_groups=[list(range(NCORES))],
                ins=[x.seg_d.opt()],
                outs=[x.gath_d.opt()]))
            g.wait_ge(x.s_cc.h, x.s_cc.n)

        @blk.vector
        def _(v):
            x.s_v.inc(v.memset(x.acc[:, :], 0.0))

        for r in range(NCORES):
            sb = slabr[:, (r % 2) * 4096:((r % 2) + 1) * 4096]

            @blk.gpsimd
            def _(g, r=r, sb=sb):
                if r >= 2:
                    g.wait_ge(x.s_v.h, x.s_v.n)
                x.s_gd.inc(g.dma_start(out=sb, in_=x.gath_d[r * 128:(r + 1) * 128, :]), 16)

            @blk.vector
            def _(v, r=r, sb=sb):
                v.wait_ge(x.s_gd.h, x.s_gd.n)
                x.s_v.inc(v.scalar_tensor_tensor(x.acc[:, :], sb,
                                                 x.mask_sb[:, r:r + 1], x.acc[:, :],
                                                 ALU.mult, ALU.add))

        @blk.vector
        def _(v):
            v.tensor_copy(x.hkb2[:, 0:H * M], x.acc[0:64, 0:H * M])
            v.tensor_copy(x.hvb2[:, 0:H * DV], x.acc[0:64, 1024:1024 + H * DV])
            x.s_v.inc(v.tensor_copy(x.wsnap[:, 0:H * M], x.acc[0:1, 3072:3072 + H * M]))


def _emit_attn(x, blk):
    nc = x.nc
    from contextlib import ExitStack
    with ExitStack() as aes:
        def sb(name, shape, dt):
            return aes.enter_context(nc.sbuf_tensor(name, shape, dt))

        qc = sb("at_qc", [64, 2 * H * C], BF16)
        kc = sb("at_kc", [64, 2 * H * C], BF16)
        efc = sb("at_efc", [64, 2 * H * C], BF16)
        gc = sb("at_gc", [128, 2 * H * C], BF16)
        etm2 = sb("at_etm", [128, 2 * H * M], BF16)
        ktm2 = sb("at_ktm", [128, 2 * H * DK], BF16)
        vtm2 = sb("at_vtm", [128, 2 * H * DV], BF16)
        denr = sb("at_denr", [128, 2 * H * M], F32)
        sqkm = sb("at_sqk", [128, H * C], BF16)
        w2m = sb("at_w2", [128, H * C], BF16)
        t1 = sb("at_t1", [128, H * M], F32)
        eo = sb("at_eo", [128, H * M], F32)
        rs = sb("at_rs", [128, 16], F32)
        rs2 = sb("at_rs2", [128, 16], F32)
        pvt = sb("at_pvt", [128, H * M], BF16)
        pvf = sb("at_pvf", [64, H * C], BF16)
        ovsb = sb("at_ov", [128, H * DV], F32)
        sq_sb = sb("at_sq", [128, H * DV], BF16)
        rrow = sb("at_rrow", [1, H * C], F32)
        rg_sb = sb("at_rg", [128, H * DV], F32)
        on_sb = sb("at_on", [128, H * DV], F32)
        og2 = sb("at_og", [128, H * 1024], BF16)
        wos = sb("at_wos", [128, 2 * NDC * 128], BF16)
        xt = sb("at_xt", [128, 2 * G], F32)
        x2t = sb("at_x2", [128, 2 * G], F32)
        P1 = aes.enter_context(nc.psum_tensor("at_P1", [128, 2048], F32))
        P2 = aes.enter_context(nc.psum_tensor("at_P2", [128, 1024], F32))
        P3b = aes.enter_context(nc.psum_tensor("at_P3", [128, 2048], BF16))
        P3f = P3b[:, :].bitcast(F32)

        og3 = og2[:, :].rearrange("p (h t) -> p h t", h=H)

        ev = {}

        def rec(key, sem):
            ev[key] = (sem, sem.n)

        def wt(eng, *keys):
            for k in keys:
                if k in ev:
                    sm, n = ev[k]
                    eng.wait_ge(sm.h, n)

        def tile_loads(c):
            p = c % 2
            sl = slice(c * C, (c + 1) * C)

            @blk.sync
            def _(sy, c=c, p=p, sl=sl):
                wt(sy, f"tilerd{p}", f"qkrd{p}")
                x.s_ld.inc(sy.dma_start(out=etm2[:, p * H * M:(p + 1) * H * M],
                                        in_=x.e_rows[:, sl], transpose=True), 16)
                x.s_ld.inc(sy.dma_start(out=ktm2[:, p * H * DK:(p + 1) * H * DK],
                                        in_=x.k_rows[:, sl], transpose=True), 16)
                x.s_ld.inc(sy.dma_start(out=vtm2[:, p * H * DV:(p + 1) * H * DV],
                                        in_=x.v_rows[:, sl], transpose=True), 16)
                x.s_ld.inc(sy.dma_start(
                    out=qc[:, p * H * C:(p + 1) * H * C].rearrange("p (h t) -> p h t", h=H),
                    in_=x.q_scr.rearrange("h m t -> m h t")[:, :, sl]), 16)
                x.s_ld.inc(sy.dma_start(
                    out=kc[:, p * H * C:(p + 1) * H * C].rearrange("p (h t) -> p h t", h=H),
                    in_=x.k_scr.rearrange("h m t -> m h t")[:, :, sl]), 16)
                x.s_ld.inc(sy.dma_start(
                    out=efc[:, p * H * C:(p + 1) * H * C].rearrange("p (h t) -> p h t", h=H),
                    in_=x.e_scr.rearrange("h m t -> m h t")[:, :, sl]), 16)
                x.s_ld.inc(sy.dma_start(
                    out=gc[:, p * H * C:(p + 1) * H * C].rearrange("p (h t) -> p h t", h=H),
                    in_=x.g_scr.rearrange("h m t -> m h t")[:, :, sl]), 16)
            rec(f"tileld{p}", x.s_ld)

        def emit_denom(c):
            p = c % 2

            @blk.tensor
            def _(t, c=c, p=p):
                wt(t, "P3", f"tileld{p}", f"wsnap{p}")
                for half in range(2):
                    sl = slice(half * 512, (half + 1) * 512)
                    t.matmul(P3f[:, sl], x.m1u[:, :],
                             etm2[:, p * H * M + half * 512: p * H * M + half * 512 + 512],
                             start=True, stop=False)
                    ins = t.matmul(P3f[:, sl], x.onesrow[:, :],
                                   x.wsnap[:, p * H * M + half * 512: p * H * M + half * 512 + 512],
                                   start=False, stop=True)
                x.s_pe.inc(ins)
            rec("CPE", x.s_pe)

            @blk.scalar
            def _(sc, c=c, p=p):
                wt(sc, "CPE", "t1rd")
                sc.activation(t1[:, :], P3f[:, :], AF.Ln, bias=0.0, scale=1.0)
                x.s_s.inc(sc.activation(denr[:, p * H * M:(p + 1) * H * M],
                                        t1[:, :], AF.Exp, bias=0.0, scale=-1.0))
            rec("P3", x.s_s)
            rec(f"denr{p}", x.s_s)
            rec("t1wr", x.s_s)
            if _KDBG and c < 2:
                @blk.scalar
                def _(sc, c=c, p=p):
                    sc.wait_ge(x.s_s.h, x.s_s.n)
                    x.s_wr.inc(sc.dma_start(out=x.dbg_denr[c],
                                            in_=denr[:, p * H * M:(p + 1) * H * M]), 16)

        tile_loads(0)
        emit_denom(0)
        tile_loads(1)

        for c in range(NCH):
            p = c % 2
            np_ = 1 - p
            etm = etm2[:, p * H * M:(p + 1) * H * M]
            ktm = ktm2[:, p * H * DK:(p + 1) * H * DK]
            vtm = vtm2[:, p * H * DV:(p + 1) * H * DV]
            q3 = qc[:, p * H * C:(p + 1) * H * C]
            k3 = kc[:, p * H * C:(p + 1) * H * C]
            ef3 = efc[:, p * H * C:(p + 1) * H * C]
            g3 = gc[:, p * H * C:(p + 1) * H * C]
            dnr = denr[:, p * H * M:(p + 1) * H * M]
            acc_hk = x.acc[0:64, 0:H * M]
            acc_hv1 = x.acc[0:64, 1024:1024 + 8 * DV]
            acc_hv2 = x.acc[0:64, 1024 + 8 * DV:1024 + 16 * DV]
            cst = (c % 8) * C

            @blk.tensor
            def _(t, c=c, p=p, etm=etm, ktm=ktm, vtm=vtm):
                wt(t, f"tileld{p}", "P1free", "P2free", "P3", f"shadrd{1 - p}")
                for h in range(H):
                    t.matmul(P2[0:64, h * M:(h + 1) * M], ktm[:, h * DK:(h + 1) * DK],
                             etm[:, h * M:(h + 1) * M], start=True, stop=True)
                for h in range(H):
                    ps = P1[0:64, 0:1024] if h < 8 else P1[0:64, 1024:2048]
                    t.matmul(ps[:, (h % 8) * DV:((h % 8) + 1) * DV],
                             etm[:, h * M:(h + 1) * M],
                             vtm[:, h * DV:(h + 1) * DV], start=True, stop=True)
                t.matmul(P3f[0:1, 0:512], x.onescol[:, :], etm[:, 0:512],
                         start=True, stop=True)
                ins = t.matmul(P3f[0:1, 512:1024], x.onescol[:, :], etm[:, 512:1024],
                               start=True, stop=True)
                x.s_pe.inc(ins)
            rec("UPE", x.s_pe)

            @blk.vector
            def _(v, c=c, p=p, np_=np_):
                wt(v, "UPE")
                v.tensor_tensor(x.acc[0:64, 0:H * M], x.acc[0:64, 0:H * M],
                                P2[0:64, :], ALU.add)
                v.tensor_tensor(x.acc[0:64, 1024:1024 + 8 * DV],
                                x.acc[0:64, 1024:1024 + 8 * DV],
                                P1[0:64, 0:1024], ALU.add)
                v.tensor_tensor(x.acc[0:64, 1024 + 8 * DV:1024 + 16 * DV],
                                x.acc[0:64, 1024 + 8 * DV:1024 + 16 * DV],
                                P1[0:64, 1024:2048], ALU.add)
                v.tensor_tensor(x.wsnap[:, np_ * H * M:(np_ + 1) * H * M],
                                x.wsnap[:, p * H * M:(p + 1) * H * M],
                                P3f[0:1, 0:H * M], ALU.add)
                v.tensor_copy(x.hkb2[:, np_ * H * M:(np_ + 1) * H * M],
                              x.acc[0:64, 0:H * M])
                ins = v.tensor_copy(x.hvb2[:, np_ * H * DV:(np_ + 1) * H * DV],
                                    x.acc[0:64, 1024:1024 + H * DV])
                x.s_v.inc(ins)
            rec("UV", x.s_v)
            rec(f"wsnap{np_}", x.s_v)
            rec("P2free_afterF", x.s_v)

            @blk.tensor
            def _(t, c=c, p=p, k3=k3, q3=q3):
                wt(t, "UV", f"qkld{p}")
                for h in range(H):
                    ins = t.matmul(P1[:, h * C:(h + 1) * C],
                                   k3[:, h * C:(h + 1) * C], q3[:, h * C:(h + 1) * C],
                                   start=True, stop=True)
                x.s_pe.inc(ins)
            rec("APE", x.s_pe)

            @blk.vector
            def _(v, c=c):
                wt(v, "APE", "sqkmrd")
                v.tensor_tensor(sqkm[:, 0:1024], P1[:, 0:1024], x.m1u8[:, :], ALU.mult)
                ins = v.tensor_tensor(sqkm[:, 1024:2048], P1[:, 1024:2048],
                                      x.m1u8[:, :], ALU.mult)
                x.s_v.inc(ins)
            rec("sqkmwr", x.s_v)
            rec("P1free", x.s_v)

            @blk.tensor
            def _(t, c=c, p=p, etm=etm, q3=q3):
                wt(t, "sqkmwr", "P2free_afterF")
                for h in range(H):
                    t.matmul(P2[:, h * M:(h + 1) * M], sqkm[:, h * C:(h + 1) * C],
                             etm[:, h * M:(h + 1) * M], start=True, stop=False)
                    ins = t.matmul(P2[:, h * M:(h + 1) * M], q3[:, h * C:(h + 1) * C],
                                   x.hkb2[:, p * H * M + h * M: p * H * M + (h + 1) * M],
                                   start=False, stop=True)
                x.s_pe.inc(ins)
            rec("BPE", x.s_pe)
            rec("sqkmrd", x.s_pe)

            @blk.vector
            def _(v, c=c, p=p, dnr=dnr):
                wt(v, "BPE", f"denr{p}", "t1wr")
                x.s_v.inc(v.tensor_tensor(t1[:, :], P2[:, :], dnr, ALU.mult))
            rec("okV", x.s_v)

            @blk.scalar
            def _(sc, c=c):
                wt(sc, "okV", "eord")
                x.s_s.inc(sc.activation(eo[:, :], t1[:, :], AF.Exp, bias=0.0, scale=1.0))
            rec("eoS", x.s_s)
            rec("t1rd", x.s_s)

            @blk.vector
            def _(v, c=c, p=p, dnr=dnr):
                wt(v, "eoS", "pvtrd")
                v.tensor_reduce(rs2[:, :], eo[:, :].rearrange("p (i m) -> p i m", i=H),
                                AX, ALU.add)
                v.reciprocal(rs[:, 0:8], rs2[:, 0:8])
                v.reciprocal(rs[:, 8:16], rs2[:, 8:16])
                v.tensor_tensor(t1[:, :], eo[:, :], dnr, ALU.mult)
                ins = v.tensor_tensor(
                    pvt[:, :].rearrange("p (i m) -> p i m", i=H),
                    t1[:, :].rearrange("p (i m) -> p i m", i=H),
                    rs[:, :].rearrange("p (i o) -> p i o", o=1).broadcast_to([128, H, M]),
                    ALU.mult)
                x.s_v.inc(ins)
            rec("pvtwr", x.s_v)
            rec("P2free", x.s_v)
            rec("eord", x.s_v)

            if _KDBG and c == 0:
                @blk.scalar
                def _(sc):
                    sc.wait_ge(x.s_v.h, x.s_v.n)
                    x.s_wr.inc(sc.dma_start(out=x.dbg_pvt, in_=pvt[:, :]), 16)
                    x.s_wr.inc(sc.dma_start(out=x.dbg_rs, in_=rs[:, :]), 16)

            @blk.tensor
            def _(t, c=c):
                wt(t, "pvtwr", "P3")
                for i in range(H):
                    ins = t.transpose(P3b[0:64, i * C:(i + 1) * C],
                                      pvt[:, i * M:(i + 1) * M], x.ident[:, :])
                x.s_pe.inc(ins)
            rec("TPE", x.s_pe)
            rec("pvtrd", x.s_pe)

            @blk.vector
            def _(v, c=c):
                wt(v, "TPE", "pvfrd")
                x.s_v.inc(v.tensor_copy(pvf[:, :], P3b[0:64, :]))
            rec("pvfwr", x.s_v)
            rec("P3", x.s_v)

            if _KDBG and c == 0:
                @blk.scalar
                def _(sc):
                    sc.wait_ge(x.s_v.h, x.s_v.n)
                    x.s_wr.inc(sc.dma_start(out=x.dbg_pvf, in_=pvf[:, :]), 16)

            @blk.tensor
            def _(t, c=c, ef3=ef3):
                wt(t, "pvfwr", "P1free")
                for h in range(H):
                    ins = t.matmul(P1[:, h * C:(h + 1) * C], ef3[:, h * C:(h + 1) * C],
                                   pvf[:, h * C:(h + 1) * C], start=True, stop=True)
                x.s_pe.inc(ins)
            rec("DPE", x.s_pe)

            @blk.vector
            def _(v, c=c):
                wt(v, "DPE", "w2mrd")
                v.tensor_tensor(w2m[:, 0:1024], P1[:, 0:1024], x.m1u8[:, :], ALU.mult)
                ins = v.tensor_tensor(w2m[:, 1024:2048], P1[:, 1024:2048],
                                      x.m1u8[:, :], ALU.mult)
                x.s_v.inc(ins)
            rec("w2mwr", x.s_v)
            rec("P1free", x.s_v)

            if _KDBG and c == 0:
                @blk.scalar
                def _(sc):
                    sc.wait_ge(x.s_v.h, x.s_v.n)
                    x.s_wr.inc(sc.dma_start(out=x.dbg_w2, in_=w2m[:, :]), 16)

            @blk.tensor
            def _(t, c=c, p=p, vtm=vtm):
                wt(t, "w2mwr", "P1free", "ovrd")
                for h in range(H):
                    t.matmul(P1[:, h * C:(h + 1) * C], vtm[:, h * DV:(h + 1) * DV],
                             w2m[:, h * C:(h + 1) * C], start=True, stop=False)
                    ins = t.matmul(P1[:, h * C:(h + 1) * C],
                                   x.hvb2[:, p * H * DV + h * DV: p * H * DV + (h + 1) * DV],
                                   pvf[:, h * C:(h + 1) * C], start=False, stop=True)
                x.s_pe.inc(ins)
            rec("EPE", x.s_pe)
            rec("w2mrd", x.s_pe)
            rec("pvfrd", x.s_pe)
            rec(f"tilerd{p}", x.s_pe)
            rec(f"shadrd{p}", x.s_pe)

            @blk.vector
            def _(v, c=c):
                wt(v, "EPE")
                x.s_v.inc(v.tensor_copy(ovsb[:, :], P1[:, :]))
            rec("ovwr", x.s_v)
            rec("P1free", x.s_v)

            if _KDBG and c == 0:
                @blk.scalar
                def _(sc):
                    sc.wait_ge(x.s_v.h, x.s_v.n)
                    x.s_wr.inc(sc.dma_start(out=x.dbg_ov, in_=ovsb[:, :]), 16)

            @blk.scalar
            def _(sc, c=c):
                wt(sc, "ovwr", "sqrd")
                x.s_s.inc(sc.activation(sq_sb[:, :], ovsb[:, :], AF.Square,
                                        bias=0.0, scale=1.0))
            rec("sqwr", x.s_s)

            @blk.tensor
            def _(t, c=c):
                wt(t, "sqwr", "P2free", "P3")
                for i in range(8):
                    t.matmul(P2[0:1, i * C:(i + 1) * C], x.onescol[:, :],
                             sq_sb[:, i * DV:(i + 1) * DV], start=True, stop=True)
                for i in range(8):
                    ins = t.matmul(P3f[0:1, i * 128:(i + 1) * 128], x.onescol[:, :],
                                   sq_sb[:, (8 + i) * DV:(9 + i) * DV],
                                   start=True, stop=True)
                x.s_pe.inc(ins)
            rec("RPE", x.s_pe)
            rec("sqrd", x.s_pe)

            @blk.scalar
            def _(sc, c=c):
                wt(sc, "RPE")
                sc.activation(rrow[:, 0:1024], P2[0:1, :], AF.Ln, bias=1e-5, scale=1.0 / DV)
                sc.activation(rrow[:, 1024:2048], P3f[0:1, :], AF.Ln, bias=1e-5, scale=1.0 / DV)
                x.s_s.inc(sc.activation(rrow[:, :], rrow[:, :], AF.Exp, bias=0.0, scale=-0.5))
            rec("rrowS", x.s_s)
            rec("P2free", x.s_s)
            rec("P3", x.s_s)

            @blk.tensor
            def _(t, c=c):
                wt(t, "rrowS", "P1free")
                for i in range(H):
                    ins = t.matmul(P1[:, i * C:(i + 1) * C], x.onesrow[:, :],
                                   rrow[:, i * C:(i + 1) * C], start=True, stop=True)
                x.s_pe.inc(ins)
            rec("RBPE", x.s_pe)

            @blk.vector
            def _(v, c=c, g3=g3, cst=cst):
                wt(v, "RBPE", "rgrd", "ogrd")
                v.tensor_scalar(rg_sb[:, :], P1[:, :], x.gnw_sb[:, 0:1], None, ALU.mult)
                v.tensor_tensor(on_sb[:, :], ovsb[:, :], rg_sb[:, :], ALU.mult)
                ins = v.tensor_tensor(
                    og3[:, :, cst:cst + C],
                    on_sb[:, :].rearrange("p (h t) -> p h t", h=H),
                    g3.rearrange("p (h t) -> p h t", h=H), ALU.mult)
                x.s_v.inc(ins)
            rec("ogV", x.s_v)
            rec("ovrd", x.s_v)
            rec("rgrd", x.s_v)
            rec("P1free", x.s_v)
            rec(f"qkld{p}", x.s_v)
            rec(f"qkrd{p}", x.s_v)

            if _KDBG and c == 7:
                @blk.scalar
                def _(sc):
                    sc.wait_ge(x.s_v.h, x.s_v.n)
                    x.s_wr.inc(sc.dma_start(out=x.dbg_og, in_=og2[:, :]), 16)

            if c + 2 < NCH:
                tile_loads(c + 2)

            if c + 1 < NCH:
                emit_denom(c + 1)

            if c % 8 == 7:
                _emit_wo(x, blk, c // 8, og3, wos, xt, x2t, P1, P2, rec, wt)


def _emit_wo(x, blk, sup, og3, wos, xt, x2t, P1, P2, rec, wt):
    nc = x.nc
    pe_evt = []
    for dc in range(NDC):
        bb = dc % 2
        wslab = wos[:, bb * NDC * 128:(bb + 1) * NDC * 128]
        slw = wslab.rearrange("p (a c) -> p a c", a=NDC)

        @blk.gpsimd
        def _(gp, dc=dc, wslab=wslab):
            if dc >= 2:
                gp.wait_ge(x.s_pe.h, pe_evt[dc - 2])
            else:
                gp.wait_ge(x.s_pe.h, x.s_pe.n)
            ins = gp.dma_start(
                out=wslab.rearrange("p (a c) -> p a c", a=NDC),
                in_=x.wo[:, dc * 128:(dc + 1) * 128].rearrange("(a p) c -> p a c", p=128))
            x.s_gd.inc(ins, 16)

        for g2 in range(2):
            tok0 = sup * 1024 + g2 * G
            xtb = xt[:, bb * G:(bb + 1) * G]
            x2b = x2t[:, bb * G:(bb + 1) * G]
            psl = P1[:, (bb * 2 + g2) * G % 2048:((bb * 2 + g2) * G % 2048) + G]

            @blk.sync
            def _(sy, xtb=xtb, dc=dc, tok0=tok0):
                sy.wait_ge(x.s_v.h, x.s_v.n)
                x.s_ld.inc(sy.dma_start(out=xtb,
                                        in_=x.x_fm[dc * 128:(dc + 1) * 128,
                                                   tok0:tok0 + G]), 16)

            @blk.tensor
            def _(t, dc=dc, g2=g2, psl=psl, slw=slw):
                t.wait_ge(x.s_gd.h, x.s_gd.n)
                if dc < 2 and g2 == 0:
                    wt(t, "P1free", "ogV")
                t.wait_ge(x.s_v.h, x.s_v.n)
                for vd in range(H):
                    ins = t.matmul(psl, slw[:, vd, :],
                                   og3[:, vd, g2 * G:(g2 + 1) * G],
                                   start=(vd == 0), stop=(vd == H - 1))
                x.s_pe.inc(ins)
            if g2 == 1:
                pe_evt.append(x.s_pe.n)

            @blk.vector
            def _(v, dc=dc, g2=g2, psl=psl, xtb=xtb, x2b=x2b):
                v.wait_ge(x.s_pe.h, x.s_pe.n)
                v.wait_ge(x.s_ld.h, x.s_ld.n)
                v.wait_ge(x.s_wr.h, x.s_wr.n)
                x.s_v.inc(v.tensor_tensor(x2b, psl, xtb, ALU.add))

            @blk.scalar
            def _(sc, dc=dc, x2b=x2b, tok0=tok0):
                sc.wait_ge(x.s_v.h, x.s_v.n)
                x.s_wr.inc(sc.dma_start(out=x.x2_scr[dc, :, tok0:tok0 + G],
                                        in_=x2b), 16)
    rec("P1free", x.s_v)
    rec("ogrd", x.s_pe)


def _emit_ffn_fused(x, blk):
    nc = x.nc
    GT = 1024
    with nc.sbuf_tensor("ff_x", [128, NDC * GT], BF16) as xg2, \
         nc.sbuf_tensor("ff_hid", [128, NHC * GT], BF16) as hid2, \
         nc.sbuf_tensor("ff_sg", [128, 2 * NDC * 128], BF16) as slg, \
         nc.sbuf_tensor("ff_su", [128, 2 * NDC * 128], BF16) as slu, \
         nc.sbuf_tensor("ff_wd", [128, 2 * NHC * 128], BF16) as wds, \
         nc.sbuf_tensor("ff_act", [128, GT], F32) as actt, \
         nc.sbuf_tensor("ff_x2", [128, 2 * GT], F32) as x2t, \
         nc.sbuf_tensor("ff_o", [128, 2 * GT], F32) as ot, \
         nc.psum_tensor("ff_PA", [128, 2048], F32) as PA, \
         nc.psum_tensor("ff_PB", [128, 2048], F32) as PB:
        xg3 = xg2[:, :].rearrange("p (a t) -> p a t", a=NDC)
        hid3 = hid2[:, :].rearrange("p (h t) -> p h t", h=NHC)
        ev = {}

        def rec(key, sem):
            ev[key] = (sem, sem.n)

        def wt(eng, *keys):
            for k in keys:
                if k in ev:
                    sm, n = ev[k]
                    eng.wait_ge(sm.h, n)

        for grp in range(2):
            t0 = grp * GT

            @blk.sync
            def _(sy, grp=grp, t0=t0):
                wt(sy, "xgrd")
                sy.wait_ge(x.s_wr.h, x.s_wr.n)
                x.s_ld.inc(sy.dma_start(
                    out=xg3[:, :, :],
                    in_=x.xln2_scr.rearrange("a p t -> p a t")[:, :, t0:t0 + GT]), 16)
            rec("xgld", x.s_ld)

            for hc in range(NHC):
                b = hc % 2
                sgb = slg[:, b * NDC * 128:(b + 1) * NDC * 128]
                sub = slu[:, b * NDC * 128:(b + 1) * NDC * 128]
                pg = PA[:, b * 1024:(b + 1) * 1024]
                pu = PB[:, b * 1024:(b + 1) * 1024]

                @blk.gpsimd
                def _(gp, hc=hc, sgb=sgb, sub=sub):
                    wt(gp, f"slabrd{hc % 2}")
                    x.s_gd.inc(gp.dma_start(
                        out=sgb.rearrange("p (a c) -> p a c", a=NDC),
                        in_=x.wgT[:, hc * 128:(hc + 1) * 128].rearrange("(a p) c -> p a c", p=128)), 16)
                    x.s_gd.inc(gp.dma_start(
                        out=sub.rearrange("p (a c) -> p a c", a=NDC),
                        in_=x.wuT[:, hc * 128:(hc + 1) * 128].rearrange("(a p) c -> p a c", p=128)), 16)
                rec(f"slabld{hc % 2}", x.s_gd)

                @blk.vector
                def _(v, hc=hc, sgb=sgb, sub=sub):
                    wt(v, f"slabld{hc % 2}", f"slabrd{hc % 2}")
                    v.tensor_scalar(sgb, sgb, -128.0, None, ALU.add)
                    x.s_v.inc(v.tensor_scalar(sub, sub, -128.0, None, ALU.add))
                rec(f"deq{hc % 2}", x.s_v)

                sg3 = sgb.rearrange("p (a c) -> p a c", a=NDC)
                su3 = sub.rearrange("p (a c) -> p a c", a=NDC)

                @blk.tensor
                def _(t, hc=hc, sg3=sg3, su3=su3, pg=pg, pu=pu):
                    wt(t, f"deq{hc % 2}", "xgld", f"gu{hc % 2}")
                    if hc < 2:
                        wt(t, "pdrd0", "pdrd1")
                    for g2 in range(2):
                        for dc in range(NDC):
                            t.matmul(pg[:, g2 * G:(g2 + 1) * G], sg3[:, dc, :],
                                     xg3[:, dc, g2 * G:(g2 + 1) * G],
                                     start=(dc == 0), stop=(dc == NDC - 1))
                    for g2 in range(2):
                        for dc in range(NDC):
                            ins = t.matmul(pu[:, g2 * G:(g2 + 1) * G], su3[:, dc, :],
                                           xg3[:, dc, g2 * G:(g2 + 1) * G],
                                           start=(dc == 0), stop=(dc == NDC - 1))
                    x.s_pe.inc(ins)
                rec(f"slabrd{hc % 2}", x.s_pe)
                rec(f"gupe{hc}", x.s_pe)
                if hc == NHC - 1:
                    rec("xgrd", x.s_pe)

                @blk.scalar
                def _(sc, hc=hc, pg=pg):
                    wt(sc, f"gupe{hc}", "actrd")
                    x.s_s.inc(sc.activation(actt[:, :], pg, AF.Silu, bias=0.0,
                                            scale=x.sg_sc[:, hc:hc + 1]))
                rec(f"sil{hc}", x.s_s)

                @blk.vector
                def _(v, hc=hc, pu=pu):
                    wt(v, f"sil{hc}", "hidrd")
                    x.s_v.inc(v.scalar_tensor_tensor(hid3[:, hc, :], pu,
                                                     x.su_sc[:, hc:hc + 1],
                                                     actt[:, :], ALU.mult, ALU.mult))
                rec(f"gu{hc % 2}", x.s_v)
                rec("actrd", x.s_v)
                rec(f"hid{hc}", x.s_v)

            for dc in range(NDC):
                b = dc % 2
                wdb = wds[:, b * NHC * 128:(b + 1) * NHC * 128]
                pd = PA[:, b * 1024:(b + 1) * 1024]
                x2b = x2t[:, b * GT:(b + 1) * GT]
                otb = ot[:, b * GT:(b + 1) * GT]

                @blk.gpsimd
                def _(gp, dc=dc, wdb=wdb):
                    wt(gp, f"wdrd{dc % 2}")
                    x.s_gd.inc(gp.dma_start(
                        out=wdb.rearrange("p (a c) -> p a c", a=NHC),
                        in_=x.wdT[:, dc * 128:(dc + 1) * 128].rearrange("(a p) c -> p a c", p=128)), 16)
                rec(f"wdld{dc % 2}", x.s_gd)

                @blk.sync
                def _(sy, dc=dc, t0=t0, x2b=x2b):
                    wt(sy, f"x2rd{dc % 2}")
                    x.s_ld.inc(sy.dma_start(out=x2b,
                                            in_=x.x2_scr[dc, :, t0:t0 + GT]), 16)
                rec(f"x2ld{dc % 2}", x.s_ld)

                @blk.vector
                def _(v, dc=dc, wdb=wdb):
                    wt(v, f"wdld{dc % 2}", f"wdrd{dc % 2}")
                    x.s_v.inc(v.tensor_scalar(wdb, wdb, -128.0, None, ALU.add))
                rec(f"wdeq{dc % 2}", x.s_v)

                wd3 = wdb.rearrange("p (a c) -> p a c", a=NHC)

                @blk.tensor
                def _(t, dc=dc, wd3=wd3, pd=pd):
                    wt(t, f"wdeq{dc % 2}", f"pdrd{dc % 2}")
                    if dc < 2:
                        wt(t, f"sil{NHC - 2}", f"sil{NHC - 1}")
                    for g2 in range(2):
                        for hc in range(NHC):
                            ins = t.matmul(pd[:, g2 * G:(g2 + 1) * G], wd3[:, hc, :],
                                           hid3[:, hc, g2 * G:(g2 + 1) * G],
                                           start=(hc == 0), stop=(hc == NHC - 1))
                    x.s_pe.inc(ins)
                rec(f"wdrd{dc % 2}", x.s_pe)
                rec(f"dpe{dc}", x.s_pe)
                if dc == NDC - 1:
                    rec("hidrd", x.s_pe)

                @blk.vector
                def _(v, dc=dc, pd=pd, x2b=x2b, otb=otb):
                    wt(v, f"dpe{dc}", f"x2ld{dc % 2}", f"otwr{dc % 2}")
                    x.s_v.inc(v.scalar_tensor_tensor(otb, pd, x.sd_sc[:, dc:dc + 1],
                                                     x2b, ALU.mult, ALU.add))
                rec(f"otV{dc % 2}", x.s_v)
                rec(f"x2rd{dc % 2}", x.s_v)
                rec(f"pdrd{dc % 2}", x.s_v)

                @blk.scalar
                def _(sc, dc=dc, t0=t0, otb=otb):
                    wt(sc, f"otV{dc % 2}")
                    x.s_wr.inc(sc.dma_start(
                        out=x.out_fm[dc * 128:(dc + 1) * 128, t0:t0 + GT],
                        in_=otb), 16)
                rec(f"otwr{dc % 2}", x.s_wr)


_CACHE = {}


def _host_inputs(inputs):
    x = np.asarray(inputs["L0_attn_hidden_states"])
    shared = dict(
        wq=np.ascontiguousarray(np.asarray(inputs["L0_attn_Wq"])),
        wk=np.ascontiguousarray(np.asarray(inputs["L0_attn_Wk"])),
        ws=np.ascontiguousarray(np.asarray(inputs["L0_attn_Ws"])),
        wv=np.ascontiguousarray(np.asarray(inputs["L0_attn_Wv"])),
        wg=np.ascontiguousarray(np.asarray(inputs["L0_attn_Wg"])),
        wo=np.ascontiguousarray(np.asarray(inputs["L0_attn_Wo"])),
        wgT=np.ascontiguousarray(np.asarray(inputs["L0_ffn_wq_gate"]).T),
        wuT=np.ascontiguousarray(np.asarray(inputs["L0_ffn_wq_up"]).T),
        wdT=np.ascontiguousarray(np.asarray(inputs["L0_ffn_wq_down"]).T),
        sgate=np.ascontiguousarray(np.asarray(inputs["L0_ffn_s_gate"]).reshape(NHC, 128).T),
        sup=np.ascontiguousarray(np.asarray(inputs["L0_ffn_s_up"]).reshape(NHC, 128).T),
        sdown=np.ascontiguousarray(np.asarray(inputs["L0_ffn_s_down"]).reshape(NDC, 128).T),
        gnw=np.ascontiguousarray(np.asarray(inputs["L0_attn_g_norm_weight"]).reshape(128, 1)),
        m1u=np.ascontiguousarray(np.triu(np.ones((128, 128))).astype(_BF)),
        m1u8=np.ascontiguousarray(np.tile(np.triu(np.ones((128, 128))).astype(_BF), (1, 8))),
        ident=np.eye(128).astype(_BF),
        cst=np.ascontiguousarray(np.tile(np.array([CONSTS], np.float32), (128, 1))),
        onescol=np.ones((128, 1), dtype=_BF),
        onesrow=np.ones((1, 128), dtype=np.float32),
    )
    per_core = []
    for c in range(NCORES):
        b, qt = divmod(c, 4)
        mk = np.zeros((128, NCORES), np.float32)
        for r in range(b * 4, c):
            mk[:, r] = 1.0
        m = dict(shared)
        m["x_fm"] = np.ascontiguousarray(x[b, qt * T:(qt + 1) * T, :].T)
        m["maskpre"] = mk
        per_core.append(m)
    return per_core


def kernel(**inputs):
    if "nc" not in _CACHE:
        _CACHE["nc"] = build()
    nc = _CACHE["nc"]
    in_maps = _host_inputs(inputs)
    res = run_bass_kernel_spmd(nc, in_maps, core_ids=list(range(NCORES)))
    out = np.empty((2, 8192, D), np.float32)
    for c in range(NCORES):
        b, qt = divmod(c, 4)
        out[b, qt * T:(qt + 1) * T, :] = res.results[c]["out_fm"].T
    return out


# version 06
# speedup vs baseline: 1.0236x; 1.0236x over previous
import os
import numpy as np
import ml_dtypes
import concourse.bass as bass
import concourse.mybir as mybir
from concourse.bass_utils import run_bass_kernel_spmd

F32 = mybir.dt.float32
BF16 = mybir.dt.bfloat16
I32 = mybir.dt.int32
U8 = mybir.dt.uint8
AF = mybir.ActivationFunctionType
ALU = mybir.AluOpType
AX = mybir.AxisListType.X

D = 2048
T = 2048
H = 16
DK = 64
DV = 128
M = 64
C = 128
NCH = T // C
G = 512
NG = T // G
HID = 5632
NHC = HID // 128
NDC = D // 128
NCORES = 8

CONSTS = [0.0, 1.0, -1.0, -0.5, -32.0, 1e-5, 1.0 / 128.0, 0.125, 1.0 / 2048.0]
_BF = ml_dtypes.bfloat16

_KPHASE = os.environ.get("KPHASE", "all")
_KDBG = os.environ.get("KDBG", "0") == "1"
_PHASES = ["b1", "b2", "b3", "b4", "b5", "b6"]


def _upto(ph):
    return _KPHASE == "all" or _PHASES.index(ph) <= _PHASES.index(_KPHASE)


class Sem:
    def __init__(self, h):
        self.h = h
        self.n = 0

    def inc(self, instr, amt=1):
        instr.then_inc(self.h, amt)
        self.n += amt
        return instr


class Ctx:
    pass


def build():
    nc = bass.Bass("TRN2", target_bir_lowering=False)
    x = Ctx()
    x.nc = nc
    scr_kind = "ExternalOutput" if _KDBG else "Internal"

    x.x_fm = nc.dram_tensor("x_fm", [D, T], F32, kind="ExternalInput").ap()
    x.wpj = nc.dram_tensor("wpj", [56, 128, NDC * 128], F32, kind="ExternalInput").ap()
    x.wo_t = nc.dram_tensor("wo_t", [NDC, 128, NDC * 128], F32, kind="ExternalInput").ap()
    x.wg_t = nc.dram_tensor("wg_t", [NHC, 128, NDC * 128], U8, kind="ExternalInput").ap()
    x.wu_t = nc.dram_tensor("wu_t", [NHC, 128, NDC * 128], U8, kind="ExternalInput").ap()
    x.wd_t = nc.dram_tensor("wd_t", [NDC, 128, NHC * 128], U8, kind="ExternalInput").ap()
    x.sgate = nc.dram_tensor("sgate", [128, NHC], F32, kind="ExternalInput").ap()
    x.sup = nc.dram_tensor("sup", [128, NHC], F32, kind="ExternalInput").ap()
    x.sdown = nc.dram_tensor("sdown", [128, NDC], F32, kind="ExternalInput").ap()
    x.gnw = nc.dram_tensor("gnw", [128, 1], F32, kind="ExternalInput").ap()
    x.m1u_i = nc.dram_tensor("m1u", [128, 128], BF16, kind="ExternalInput").ap()
    x.m1u8_i = nc.dram_tensor("m1u8", [128, 8 * 128], BF16, kind="ExternalInput").ap()
    x.ident_i = nc.dram_tensor("ident", [128, 128], BF16, kind="ExternalInput").ap()
    x.cst_i = nc.dram_tensor("cst", [128, len(CONSTS)], F32, kind="ExternalInput").ap()
    x.mask_i = nc.dram_tensor("maskpre", [128, NCORES], F32, kind="ExternalInput").ap()
    x.onescol_i = nc.dram_tensor("onescol", [128, 1], BF16, kind="ExternalInput").ap()
    x.onesrow_i = nc.dram_tensor("onesrow", [1, 128], F32, kind="ExternalInput").ap()

    x.out_fm = nc.dram_tensor("out_fm", [D, T], F32, kind="ExternalOutput").ap()

    x.q_scr = nc.dram_tensor("q_scr", [H, DK, T], BF16, kind=scr_kind).ap()
    x.k_scr = nc.dram_tensor("k_scr", [H, DK, T], BF16, kind=scr_kind).ap()
    x.e_scr = nc.dram_tensor("e_scr", [H, M, T], BF16, kind=scr_kind).ap()
    x.v_scr = nc.dram_tensor("v_scr", [H, DV, T], BF16, kind=scr_kind).ap()
    x.g_scr = nc.dram_tensor("g_scr", [H, DV, T], BF16, kind=scr_kind).ap()
    x.x2_scr = nc.dram_tensor("x2_scr", [NDC, 128, T], F32, kind=scr_kind).ap()
    x.hid_scr = nc.dram_tensor("hid_scr", [NHC, 128, T], BF16, kind=scr_kind).ap()
    x.xln2_scr = nc.dram_tensor("xln2_scr", [NDC, 128, T], BF16, kind=scr_kind).ap()
    x.seg_d = nc.dram_tensor("seg_d", [128, 4096], F32).ap()
    if _KDBG:
        x.dbg_denr = nc.dram_tensor("dbg_denr", [2, 128, 1024], F32, kind="ExternalOutput").ap()
        x.dbg_og = nc.dram_tensor("dbg_og", [128, H * 1024], mybir.dt.bfloat16, kind="ExternalOutput").ap()
        x.dbg_ov = nc.dram_tensor("dbg_ov", [128, H * DV], F32, kind="ExternalOutput").ap()
        x.dbg_pvt = nc.dram_tensor("dbg_pvt", [128, H * M], mybir.dt.bfloat16, kind="ExternalOutput").ap()
        x.dbg_rs = nc.dram_tensor("dbg_rs", [128, 16], F32, kind="ExternalOutput").ap()
        x.dbg_pvf = nc.dram_tensor("dbg_pvf", [64, H * C], mybir.dt.bfloat16, kind="ExternalOutput").ap()
        x.dbg_w2 = nc.dram_tensor("dbg_w2", [128, H * C], mybir.dt.bfloat16, kind="ExternalOutput").ap()
    x.gath_d = nc.dram_tensor("gath_d", [NCORES * 128, 4096], F32).ap()

    x.e_rows = x.e_scr.rearrange("h m t -> (h m) t")
    x.k_rows = x.k_scr.rearrange("h m t -> (h m) t")
    x.v_rows = x.v_scr.rearrange("h m t -> (h m) t")

    from contextlib import ExitStack
    es = ExitStack()
    with es:
        x.cst = es.enter_context(nc.sbuf_tensor("cst_sb", [128, len(CONSTS)], F32))
        x.m1u = es.enter_context(nc.sbuf_tensor("m1u_sb", [128, 128], BF16))
        x.m1u8 = es.enter_context(nc.sbuf_tensor("m1u8_sb", [128, 8 * 128], BF16))
        x.ident = es.enter_context(nc.sbuf_tensor("ident_sb", [128, 128], BF16))
        x.gnw_sb = es.enter_context(nc.sbuf_tensor("gnw_sb", [128, 1], F32))
        x.mask_sb = es.enter_context(nc.sbuf_tensor("mask_sb", [128, NCORES], F32))
        x.onescol = es.enter_context(nc.sbuf_tensor("onescol_sb", [128, 1], BF16))
        x.onesrow = es.enter_context(nc.sbuf_tensor("onesrow_sb", [1, 128], F32))
        x.sg_sc = es.enter_context(nc.sbuf_tensor("sg_sc", [128, NHC], F32))
        x.su_sc = es.enter_context(nc.sbuf_tensor("su_sc", [128, NHC], F32))
        x.sd_sc = es.enter_context(nc.sbuf_tensor("sd_sc", [128, NDC], F32))

        for i, vv in enumerate(CONSTS):
            nc.const_aps.register(F32, vv, x.cst[:, i:i + 1])

        x.s_ld = Sem(es.enter_context(nc.semaphore("s_ld")))
        x.s_wr = Sem(es.enter_context(nc.semaphore("s_wr")))
        x.s_gd = Sem(es.enter_context(nc.semaphore("s_gd")))
        x.s_cc = Sem(es.enter_context(nc.semaphore("s_cc")))
        x.s_pe = Sem(es.enter_context(nc.semaphore("s_pe")))
        x.s_v = Sem(es.enter_context(nc.semaphore("s_v")))
        x.s_s = Sem(es.enter_context(nc.semaphore("s_s")))
        x.s_g = Sem(es.enter_context(nc.semaphore("s_g")))

        with nc.Block() as blk:
            @blk.sync
            def _(sy):
                for dst, src in [(x.cst, x.cst_i), (x.m1u, x.m1u_i), (x.m1u8, x.m1u8_i),
                                 (x.ident, x.ident_i), (x.gnw_sb, x.gnw),
                                 (x.mask_sb, x.mask_i), (x.onescol, x.onescol_i),
                                 (x.onesrow, x.onesrow_i), (x.sg_sc, x.sgate),
                                 (x.su_sc, x.sup), (x.sd_sc, x.sdown)]:
                    x.s_ld.inc(sy.dma_start(out=dst[:, :], in_=src), 16)

            @blk.vector
            def _(v):
                v.wait_ge(x.s_ld.h, x.s_ld.n)

        if _upto("b1"):
            with nc.sbuf_tensor("xln", [128, NDC * T], BF16) as _xln_flat, \
                 nc.Block() as blk:
                xln = _xln_flat
                _emit_ln(x, blk, x.x_fm, xln, "ln1")
                _emit_proj(x, blk, xln)

        if _upto("b2"):
            with nc.sbuf_tensor("acc", [128, 4096], F32) as acc, \
                 nc.sbuf_tensor("hkb2", [64, 2 * H * M], BF16) as hkb2, \
                 nc.sbuf_tensor("hvb2", [64, 2 * H * DV], BF16) as hvb2, \
                 nc.sbuf_tensor("wsnap", [1, 2 * H * M], F32) as wsnap:
                x.acc, x.hkb2, x.hvb2, x.wsnap = acc, hkb2, hvb2, wsnap
                with nc.Block() as blk:
                    _emit_states_exchange(x, blk)
                if _upto("b3"):
                    with nc.Block() as blk:
                        _emit_attn(x, blk)

        if _upto("b4"):
            with nc.sbuf_tensor("xln2", [128, NDC * T], BF16) as xln2:
                x.xln2 = xln2
                with nc.Block() as blk:
                    _emit_ln(x, blk, x.x2_scr.rearrange("a p t -> (a p) t"), x.xln2, "ln2")

                    @blk.scalar
                    def _(sc):
                        sc.wait_ge(x.s_v.h, x.s_v.n)
                        sc.wait_ge(x.s_g.h, x.s_g.n)
                        for dc in range(NDC):
                            x.s_wr.inc(sc.dma_start(
                                out=x.xln2_scr[dc],
                                in_=_v3(x.xln2, NDC, T)[:, dc, :]), 16)
            if _upto("b5"):
                with nc.Block() as blk:
                    _emit_ffn_fused(x, blk)

        with nc.Block() as blk:
            @blk.sync
            def _(sy):
                sy.wait_ge(x.s_wr.h, x.s_wr.n)
                sy.wait_ge(x.s_ld.h, x.s_ld.n)
                sy.wait_ge(x.s_gd.h, x.s_gd.n)

    return nc


def _v3(t2d, a, b):
    return t2d[:, :].rearrange("p (a b) -> p a b", a=a)


def _emit_ln(x, blk, src_fm, dst, tag):
    nc = x.nc
    src3 = src_fm.rearrange("(a p) t -> p a t", p=128)
    dst3 = _v3(dst, NDC, T)
    with nc.sbuf_tensor(f"{tag}_xg", [128, NDC * G], F32) as xg2, \
         nc.sbuf_tensor(f"{tag}_xb", [128, NDC * G], BF16) as xb2, \
         nc.sbuf_tensor(f"{tag}_sq", [128, NDC * G], BF16) as sq2, \
         nc.sbuf_tensor(f"{tag}_rows", [1, G * 6], F32) as rows, \
         nc.sbuf_tensor(f"{tag}_bc", [128, 2 * G], F32) as bc, \
         nc.psum_tensor(f"{tag}_psS", [1, G], F32) as psS, \
         nc.psum_tensor(f"{tag}_psQ", [1, G], F32) as psQ, \
         nc.psum_tensor(f"{tag}_psB", [128, 2 * G], F32) as psB:
        xg = _v3(xg2, NDC, G)
        xb = _v3(xb2, NDC, G)
        sq = _v3(sq2, NDC, G)
        mu, msq, tt, var, rstd, mup = (rows[0:1, i * G:(i + 1) * G] for i in range(6))
        for g in range(NG):
            @blk.sync
            def _(sy, g=g):
                sy.wait_ge(x.s_v.h, x.s_v.n)
                sy.wait_ge(x.s_g.h, x.s_g.n)
                sy.wait_ge(x.s_wr.h, x.s_wr.n)
                x.s_ld.inc(sy.dma_start(out=xg[:, :, :],
                                        in_=src3[:, :, g * G:(g + 1) * G]), 16)

            @blk.vector
            def _(v, g=g):
                v.wait_ge(x.s_ld.h, x.s_ld.n)
                v.wait_ge(x.s_pe.h, x.s_pe.n)
                v.tensor_copy(xb2[:, :], xg2[:, :])
                x.s_v.inc(v.tensor_tensor(sq2[:, :], xb2[:, :], xb2[:, :], ALU.mult))

            @blk.tensor
            def _(t, g=g):
                t.wait_ge(x.s_v.h, x.s_v.n)
                for dc in range(NDC):
                    t.matmul(psS[:, :], x.onescol[:, :], xb[:, dc, :],
                             start=(dc == 0), stop=(dc == NDC - 1))
                for dc in range(NDC):
                    ins = t.matmul(psQ[:, :], x.onescol[:, :], sq[:, dc, :],
                                   start=(dc == 0), stop=(dc == NDC - 1))
                x.s_pe.inc(ins)

            @blk.vector
            def _(v, g=g):
                v.wait_ge(x.s_pe.h, x.s_pe.n)
                v.tensor_scalar(mu, psS[:, :], 1.0 / D, None, ALU.mult)
                v.tensor_scalar(msq, psQ[:, :], 1.0 / D, None, ALU.mult)
                v.tensor_tensor(tt, mu, mu, ALU.mult)
                x.s_v.inc(v.tensor_tensor(var, msq, tt, ALU.subtract))

            @blk.scalar
            def _(sc, g=g):
                sc.wait_ge(x.s_v.h, x.s_v.n)
                sc.activation(tt, var, AF.Ln, bias=1e-5, scale=1.0)
                x.s_s.inc(sc.activation(rstd, tt, AF.Exp, bias=0.0, scale=-0.5))

            @blk.vector
            def _(v, g=g):
                v.wait_ge(x.s_s.h, x.s_s.n)
                x.s_v.inc(v.scalar_tensor_tensor(mup, mu, -1.0, rstd, ALU.mult, ALU.mult))

            @blk.tensor
            def _(t, g=g):
                t.wait_ge(x.s_v.h, x.s_v.n)
                t.matmul(psB[:, 0:G], x.onesrow[:, :], rstd, start=True, stop=True)
                ins = t.matmul(psB[:, G:2 * G], x.onesrow[:, :], mup, start=True, stop=True)
                x.s_pe.inc(ins)

            @blk.vector
            def _(v, g=g):
                v.wait_ge(x.s_pe.h, x.s_pe.n)
                x.s_v.inc(v.tensor_copy(bc[:, :], psB[:, :]))

            NPOOL = 6

            @blk.gpsimd
            def _(gp, g=g):
                gp.wait_ge(x.s_v.h, x.s_v.n)
                for dc in range(NDC - NPOOL, NDC):
                    gp.tensor_tensor(xg[:, dc, :], xg[:, dc, :], bc[:, 0:G], ALU.mult)
                    ins = gp.tensor_tensor(dst3[:, dc, g * G:(g + 1) * G],
                                           xg[:, dc, :], bc[:, G:2 * G], ALU.add)
                x.s_g.inc(ins)

            @blk.vector
            def _(v, g=g):
                for dc in range(NDC - NPOOL):
                    v.tensor_tensor(xg[:, dc, :], xg[:, dc, :], bc[:, 0:G], ALU.mult)
                    ins = v.tensor_tensor(dst3[:, dc, g * G:(g + 1) * G],
                                          xg[:, dc, :], bc[:, G:2 * G], ALU.add)
                x.s_v.inc(ins)


def _emit_proj(x, blk, xln):
    nc = x.nc
    xln3 = _v3(xln, NDC, T)
    jobs = []
    for kind, ncc, scr in [("q", 8, x.q_scr), ("k", 8, x.k_scr),
                           ("s", 8, x.e_scr), ("v", 16, x.v_scr),
                           ("g", 16, x.g_scr)]:
        for cc in range(ncc):
            jobs.append((kind, cc, scr))

    with nc.sbuf_tensor("pj_slab", [128, 2 * NDC * 128], BF16) as slab, \
         nc.sbuf_tensor("pj_ev", [128, 2 * T], BF16) as ev, \
         nc.sbuf_tensor("pj_sf", [128, T], F32) as sf, \
         nc.psum_tensor("pj_psA", [128, T], F32) as psA, \
         nc.psum_tensor("pj_psB", [128, T], F32) as psB:
        pss = [psA, psB]
        pe_evt, ev_evt, wr_evt, ld_evt = [], [], [], []

        for j, (kind, cc, scr) in enumerate(jobs):
            ps = pss[j % 2]
            slab_b = slab[:, (j % 2) * NDC * 128:((j % 2) + 1) * NDC * 128]
            sl3 = slab_b.rearrange("p (a c) -> p a c", a=NDC)
            ev_b = ev[:, (j % 2) * T:((j % 2) + 1) * T]

            @blk.gpsimd
            def _(gp, j=j, slab_b=slab_b):
                if j >= 2:
                    gp.wait_ge(x.s_pe.h, pe_evt[j - 2])
                ins = gp.dma_start(out=slab_b, in_=x.wpj[j])
                x.s_gd.inc(ins, 16)
            ld_evt.append(x.s_gd.n)

            @blk.tensor
            def _(t, j=j, ps=ps, sl3=sl3):
                t.wait_ge(x.s_gd.h, ld_evt[j])
                if j >= 2:
                    t.wait_ge(x.s_v.h, ev_evt[j - 2][0])
                    t.wait_ge(x.s_s.h, ev_evt[j - 2][1])
                if j == 0:
                    t.wait_ge(x.s_v.h, x.s_v.n)
                    t.wait_ge(x.s_g.h, x.s_g.n)
                for g in range(NG):
                    for dc in range(NDC):
                        ins = t.matmul(ps[:, g * G:(g + 1) * G], sl3[:, dc, :],
                                       xln3[:, dc, g * G:(g + 1) * G],
                                       start=(dc == 0), stop=(dc == NDC - 1))
                x.s_pe.inc(ins)
            pe_evt.append(x.s_pe.n)

            if kind in ("k", "v"):
                @blk.vector
                def _(v, j=j, ps=ps, ev_b=ev_b):
                    v.wait_ge(x.s_pe.h, pe_evt[j])
                    if j >= 2:
                        v.wait_ge(x.s_wr.h, wr_evt[j - 2])
                    x.s_v.inc(v.tensor_copy(ev_b, ps[:, :]))
            elif kind == "g":
                @blk.scalar
                def _(sc, j=j, ps=ps, ev_b=ev_b):
                    sc.wait_ge(x.s_pe.h, pe_evt[j])
                    if j >= 2:
                        sc.wait_ge(x.s_wr.h, wr_evt[j - 2])
                    x.s_s.inc(sc.activation(ev_b, ps[:, :], AF.Silu,
                                            bias=0.0, scale=1.0))
            elif kind == "q":
                @blk.scalar
                def _(sc, j=j, ps=ps, ev_b=ev_b):
                    sc.wait_ge(x.s_pe.h, pe_evt[j])
                    if j >= 2:
                        sc.wait_ge(x.s_wr.h, wr_evt[j - 2])
                    x.s_s.inc(sc.activation(ev_b, ps[:, :], AF.Copy,
                                            bias=0.0, scale=0.125))
            else:
                @blk.vector
                def _(v, j=j, ps=ps):
                    v.wait_ge(x.s_pe.h, pe_evt[j])
                    v.wait_ge(x.s_s.h, x.s_s.n)
                    x.s_v.inc(v.tensor_scalar(sf[:, :], ps[:, :], 32.0, -32.0,
                                              ALU.min, ALU.max))

                @blk.scalar
                def _(sc, j=j, ev_b=ev_b):
                    sc.wait_ge(x.s_v.h, x.s_v.n)
                    if j >= 2:
                        sc.wait_ge(x.s_wr.h, wr_evt[j - 2])
                    x.s_s.inc(sc.activation(ev_b, sf[:, :], AF.Exp,
                                            bias=-32.0, scale=1.0))
            ev_evt.append((x.s_v.n, x.s_s.n))

            @blk.scalar
            def _(sc, j=j, cc=cc, scr=scr, ev_b=ev_b):
                sc.wait_ge(x.s_v.h, ev_evt[j][0])
                sc.wait_ge(x.s_s.h, ev_evt[j][1])
                if scr.shape[1] == 64:
                    x.s_wr.inc(sc.dma_start(out=scr[2 * cc], in_=ev_b[0:64, :]), 16)
                    x.s_wr.inc(sc.dma_start(out=scr[2 * cc + 1], in_=ev_b[64:128, :]), 16)
                else:
                    x.s_wr.inc(sc.dma_start(out=scr[cc], in_=ev_b[:, :]), 16)
            wr_evt.append(x.s_wr.n)


def _emit_chunk_states(x, blk, etm, ktm, vtm, hk_dst, hv2_dst, w_dst,
                       psF, psG1, psG2, wait_s=False):

    @blk.tensor
    def _(t):
        t.wait_ge(x.s_v.h, x.s_v.n)
        if wait_s:
            t.wait_ge(x.s_s.h, x.s_s.n)
        for h in range(H):
            t.matmul(psF[0:64, h * M:(h + 1) * M], ktm[:, h * DK:(h + 1) * DK],
                     etm[:, h * M:(h + 1) * M], start=True, stop=True)
        for h in range(H):
            ps = psG1 if h < 8 else psG2
            ins = t.matmul(ps[0:64, (h % 8) * DV:((h % 8) + 1) * DV],
                           etm[:, h * M:(h + 1) * M],
                           vtm[:, h * DV:(h + 1) * DV], start=True, stop=True)
        x.s_pe.inc(ins)

    @blk.vector
    def _(v):
        v.wait_ge(x.s_pe.h, x.s_pe.n)
        v.tensor_tensor(hk_dst, hk_dst, psF[0:64, 0:H * M], ALU.add)
        v.tensor_tensor(hv2_dst[0], hv2_dst[0], psG1[0:64, 0:8 * DV], ALU.add)
        x.s_v.inc(v.tensor_tensor(hv2_dst[1], hv2_dst[1], psG2[0:64, 0:8 * DV], ALU.add))


def _emit_states_exchange(x, blk):
    nc = x.nc
    with nc.sbuf_tensor("seg", [128, 4096], F32) as seg, \
         nc.sbuf_tensor("b2_etm", [128, 2 * H * M], BF16) as etm2, \
         nc.sbuf_tensor("b2_ktm", [128, 2 * H * DK], BF16) as ktm2, \
         nc.sbuf_tensor("b2_vtm", [128, 2 * H * DV], BF16) as vtm2, \
         nc.sbuf_tensor("b2_slab", [128, 2 * 4096], F32) as slabr, \
         nc.psum_tensor("b2_psF", [64, H * M], F32) as psF, \
         nc.psum_tensor("b2_psG1", [64, 8 * DV], F32) as psG1, \
         nc.psum_tensor("b2_psG2", [64, 8 * DV], F32) as psG2, \
         nc.psum_tensor("b2_psH", [1, H * M], F32) as psH:

        hk = seg[0:64, 0:H * M]
        hv1 = seg[0:64, 1024:1024 + 8 * DV]
        hv2 = seg[0:64, 1024 + 8 * DV:1024 + 16 * DV]
        wrow = seg[0:1, 3072:3072 + H * M]

        @blk.vector
        def _(v):
            x.s_v.inc(v.memset(seg[:, :], 0.0))

        for c in range(NCH):
            b = c % 2
            etm = etm2[:, b * H * M:(b + 1) * H * M]
            ktm = ktm2[:, b * H * DK:(b + 1) * H * DK]
            vtm = vtm2[:, b * H * DV:(b + 1) * H * DV]

            @blk.sync
            def _(sy, c=c, etm=etm, ktm=ktm, vtm=vtm):
                if c >= 2:
                    sy.wait_ge(x.s_pe.h, x.s_pe.n)
                sy.wait_ge(x.s_wr.h, x.s_wr.n)
                x.s_ld.inc(sy.dma_start(out=etm, in_=x.e_rows[:, c * C:(c + 1) * C],
                                        transpose=True), 16)
                x.s_ld.inc(sy.dma_start(out=ktm, in_=x.k_rows[:, c * C:(c + 1) * C],
                                        transpose=True), 16)
                x.s_ld.inc(sy.dma_start(out=vtm, in_=x.v_rows[:, c * C:(c + 1) * C],
                                        transpose=True), 16)

            @blk.tensor
            def _(t, c=c, etm=etm):
                t.wait_ge(x.s_ld.h, x.s_ld.n)
                t.wait_ge(x.s_v.h, x.s_v.n)
                t.matmul(psH[0:1, 0:512], x.onescol[:, :], etm[:, 0:512],
                         start=True, stop=True)
                ins = t.matmul(psH[0:1, 512:1024], x.onescol[:, :], etm[:, 512:1024],
                               start=True, stop=True)
                x.s_pe.inc(ins)

            @blk.vector
            def _(v, c=c):
                v.wait_ge(x.s_pe.h, x.s_pe.n)
                x.s_v.inc(v.tensor_tensor(wrow, wrow, psH[0:1, :], ALU.add))

            _emit_chunk_states(x, blk, etm, ktm, vtm, hk, (hv1, hv2), wrow,
                               psF, psG1, psG2)

        @blk.gpsimd
        def _(g):
            g.wait_ge(x.s_v.h, x.s_v.n)
            x.s_gd.inc(g.dma_start(out=x.seg_d, in_=seg[:, :]), 16)
            g.wait_ge(x.s_gd.h, x.s_gd.n)
            x.s_cc.inc(g.collective_compute(
                "AllGather", ALU.bypass,
                replica_groups=[list(range(NCORES))],
                ins=[x.seg_d.opt()],
                outs=[x.gath_d.opt()]))
            g.wait_ge(x.s_cc.h, x.s_cc.n)

        @blk.vector
        def _(v):
            x.s_v.inc(v.memset(x.acc[:, :], 0.0))

        for r in range(NCORES):
            sb = slabr[:, (r % 2) * 4096:((r % 2) + 1) * 4096]

            @blk.gpsimd
            def _(g, r=r, sb=sb):
                if r >= 2:
                    g.wait_ge(x.s_v.h, x.s_v.n)
                x.s_gd.inc(g.dma_start(out=sb, in_=x.gath_d[r * 128:(r + 1) * 128, :]), 16)

            @blk.vector
            def _(v, r=r, sb=sb):
                v.wait_ge(x.s_gd.h, x.s_gd.n)
                x.s_v.inc(v.scalar_tensor_tensor(x.acc[:, :], sb,
                                                 x.mask_sb[:, r:r + 1], x.acc[:, :],
                                                 ALU.mult, ALU.add))

        @blk.vector
        def _(v):
            v.tensor_copy(x.hkb2[:, 0:H * M], x.acc[0:64, 0:H * M])
            v.tensor_copy(x.hvb2[:, 0:H * DV], x.acc[0:64, 1024:1024 + H * DV])
            x.s_v.inc(v.tensor_copy(x.wsnap[:, 0:H * M], x.acc[0:1, 3072:3072 + H * M]))


def _emit_attn(x, blk):
    nc = x.nc
    from contextlib import ExitStack
    with ExitStack() as aes:
        def sb(name, shape, dt):
            return aes.enter_context(nc.sbuf_tensor(name, shape, dt))

        qc = sb("at_qc", [64, 2 * H * C], BF16)
        kc = sb("at_kc", [64, 2 * H * C], BF16)
        efc = sb("at_efc", [64, 2 * H * C], BF16)
        gc = sb("at_gc", [128, 2 * H * C], BF16)
        etm2 = sb("at_etm", [128, 2 * H * M], BF16)
        ktm2 = sb("at_ktm", [128, 2 * H * DK], BF16)
        vtm2 = sb("at_vtm", [128, 2 * H * DV], BF16)
        denr = sb("at_denr", [128, 2 * H * M], F32)
        sqkm = sb("at_sqk", [128, H * C], BF16)
        w2m = sb("at_w2", [128, H * C], BF16)
        t1 = sb("at_t1", [128, H * M], F32)
        eo = sb("at_eo", [128, H * M], F32)
        rs = sb("at_rs", [128, 16], F32)
        rs2 = sb("at_rs2", [128, 16], F32)
        pvt = sb("at_pvt", [128, H * M], BF16)
        pvf = sb("at_pvf", [64, H * C], BF16)
        ovsb = sb("at_ov", [128, H * DV], F32)
        sq_sb = sb("at_sq", [128, H * DV], BF16)
        rrow = sb("at_rrow", [1, H * C], F32)
        rg_sb = sb("at_rg", [128, H * DV], F32)
        on_sb = sb("at_on", [128, H * DV], F32)
        og2 = sb("at_og", [128, H * 1024], BF16)
        wos = sb("at_wos", [128, 2 * NDC * 128], BF16)
        xt = sb("at_xt", [128, 2 * G], F32)
        x2t = sb("at_x2", [128, 2 * G], F32)
        P1 = aes.enter_context(nc.psum_tensor("at_P1", [128, 2048], F32))
        P2 = aes.enter_context(nc.psum_tensor("at_P2", [128, 1024], F32))
        P3b = aes.enter_context(nc.psum_tensor("at_P3", [128, 2048], BF16))
        P3f = P3b[:, :].bitcast(F32)

        og3 = og2[:, :].rearrange("p (h t) -> p h t", h=H)

        ev = {}

        def rec(key, sem):
            ev[key] = (sem, sem.n)

        def wt(eng, *keys):
            for k in keys:
                if k in ev:
                    sm, n = ev[k]
                    eng.wait_ge(sm.h, n)

        def tile_loads(c):
            p = c % 2
            sl = slice(c * C, (c + 1) * C)

            @blk.sync
            def _(sy, c=c, p=p, sl=sl):
                wt(sy, f"tilerd{p}", f"qkrd{p}")
                x.s_ld.inc(sy.dma_start(out=etm2[:, p * H * M:(p + 1) * H * M],
                                        in_=x.e_rows[:, sl], transpose=True), 16)
                x.s_ld.inc(sy.dma_start(out=ktm2[:, p * H * DK:(p + 1) * H * DK],
                                        in_=x.k_rows[:, sl], transpose=True), 16)
                x.s_ld.inc(sy.dma_start(out=vtm2[:, p * H * DV:(p + 1) * H * DV],
                                        in_=x.v_rows[:, sl], transpose=True), 16)
                x.s_ld.inc(sy.dma_start(
                    out=qc[:, p * H * C:(p + 1) * H * C].rearrange("p (h t) -> p h t", h=H),
                    in_=x.q_scr.rearrange("h m t -> m h t")[:, :, sl]), 16)
                x.s_ld.inc(sy.dma_start(
                    out=kc[:, p * H * C:(p + 1) * H * C].rearrange("p (h t) -> p h t", h=H),
                    in_=x.k_scr.rearrange("h m t -> m h t")[:, :, sl]), 16)
                x.s_ld.inc(sy.dma_start(
                    out=efc[:, p * H * C:(p + 1) * H * C].rearrange("p (h t) -> p h t", h=H),
                    in_=x.e_scr.rearrange("h m t -> m h t")[:, :, sl]), 16)
                x.s_ld.inc(sy.dma_start(
                    out=gc[:, p * H * C:(p + 1) * H * C].rearrange("p (h t) -> p h t", h=H),
                    in_=x.g_scr.rearrange("h m t -> m h t")[:, :, sl]), 16)
            rec(f"tileld{p}", x.s_ld)

        def emit_denom(c):
            p = c % 2

            @blk.tensor
            def _(t, c=c, p=p):
                wt(t, "P3", f"tileld{p}", f"wsnap{p}")
                for half in range(2):
                    sl = slice(half * 512, (half + 1) * 512)
                    t.matmul(P3f[:, sl], x.m1u[:, :],
                             etm2[:, p * H * M + half * 512: p * H * M + half * 512 + 512],
                             start=True, stop=False)
                    ins = t.matmul(P3f[:, sl], x.onesrow[:, :],
                                   x.wsnap[:, p * H * M + half * 512: p * H * M + half * 512 + 512],
                                   start=False, stop=True)
                x.s_pe.inc(ins)
            rec("CPE", x.s_pe)

            @blk.scalar
            def _(sc, c=c, p=p):
                wt(sc, "CPE", "t1rd")
                sc.activation(t1[:, :], P3f[:, :], AF.Ln, bias=0.0, scale=1.0)
                x.s_s.inc(sc.activation(denr[:, p * H * M:(p + 1) * H * M],
                                        t1[:, :], AF.Exp, bias=0.0, scale=-1.0))
            rec("P3", x.s_s)
            rec(f"denr{p}", x.s_s)
            rec("t1wr", x.s_s)
            if _KDBG and c < 2:
                @blk.scalar
                def _(sc, c=c, p=p):
                    sc.wait_ge(x.s_s.h, x.s_s.n)
                    x.s_wr.inc(sc.dma_start(out=x.dbg_denr[c],
                                            in_=denr[:, p * H * M:(p + 1) * H * M]), 16)

        tile_loads(0)
        emit_denom(0)
        tile_loads(1)

        for c in range(NCH):
            p = c % 2
            np_ = 1 - p
            etm = etm2[:, p * H * M:(p + 1) * H * M]
            ktm = ktm2[:, p * H * DK:(p + 1) * H * DK]
            vtm = vtm2[:, p * H * DV:(p + 1) * H * DV]
            q3 = qc[:, p * H * C:(p + 1) * H * C]
            k3 = kc[:, p * H * C:(p + 1) * H * C]
            ef3 = efc[:, p * H * C:(p + 1) * H * C]
            g3 = gc[:, p * H * C:(p + 1) * H * C]
            dnr = denr[:, p * H * M:(p + 1) * H * M]
            acc_hk = x.acc[0:64, 0:H * M]
            acc_hv1 = x.acc[0:64, 1024:1024 + 8 * DV]
            acc_hv2 = x.acc[0:64, 1024 + 8 * DV:1024 + 16 * DV]
            cst = (c % 8) * C

            @blk.tensor
            def _(t, c=c, p=p, etm=etm, ktm=ktm, vtm=vtm):
                wt(t, f"tileld{p}", "P1free", "P2free", "P3", f"shadrd{1 - p}")
                for h in range(H):
                    t.matmul(P2[0:64, h * M:(h + 1) * M], ktm[:, h * DK:(h + 1) * DK],
                             etm[:, h * M:(h + 1) * M], start=True, stop=True)
                for h in range(H):
                    ps = P1[0:64, 0:1024] if h < 8 else P1[0:64, 1024:2048]
                    t.matmul(ps[:, (h % 8) * DV:((h % 8) + 1) * DV],
                             etm[:, h * M:(h + 1) * M],
                             vtm[:, h * DV:(h + 1) * DV], start=True, stop=True)
                t.matmul(P3f[0:1, 0:512], x.onescol[:, :], etm[:, 0:512],
                         start=True, stop=True)
                ins = t.matmul(P3f[0:1, 512:1024], x.onescol[:, :], etm[:, 512:1024],
                               start=True, stop=True)
                x.s_pe.inc(ins)
            rec("UPE", x.s_pe)

            @blk.vector
            def _(v, c=c, p=p, np_=np_):
                wt(v, "UPE")
                v.tensor_tensor(x.acc[0:64, 0:H * M], x.acc[0:64, 0:H * M],
                                P2[0:64, :], ALU.add)
                v.tensor_tensor(x.acc[0:64, 1024:1024 + 8 * DV],
                                x.acc[0:64, 1024:1024 + 8 * DV],
                                P1[0:64, 0:1024], ALU.add)
                v.tensor_tensor(x.acc[0:64, 1024 + 8 * DV:1024 + 16 * DV],
                                x.acc[0:64, 1024 + 8 * DV:1024 + 16 * DV],
                                P1[0:64, 1024:2048], ALU.add)
                v.tensor_tensor(x.wsnap[:, np_ * H * M:(np_ + 1) * H * M],
                                x.wsnap[:, p * H * M:(p + 1) * H * M],
                                P3f[0:1, 0:H * M], ALU.add)
                v.tensor_copy(x.hkb2[:, np_ * H * M:(np_ + 1) * H * M],
                              x.acc[0:64, 0:H * M])
                ins = v.tensor_copy(x.hvb2[:, np_ * H * DV:(np_ + 1) * H * DV],
                                    x.acc[0:64, 1024:1024 + H * DV])
                x.s_v.inc(ins)
            rec("UV", x.s_v)
            rec(f"wsnap{np_}", x.s_v)
            rec("P2free_afterF", x.s_v)

            @blk.tensor
            def _(t, c=c, p=p, k3=k3, q3=q3):
                wt(t, "UV", f"qkld{p}")
                for h in range(H):
                    ins = t.matmul(P1[:, h * C:(h + 1) * C],
                                   k3[:, h * C:(h + 1) * C], q3[:, h * C:(h + 1) * C],
                                   start=True, stop=True)
                x.s_pe.inc(ins)
            rec("APE", x.s_pe)

            @blk.vector
            def _(v, c=c):
                wt(v, "APE", "sqkmrd")
                v.tensor_tensor(sqkm[:, 0:1024], P1[:, 0:1024], x.m1u8[:, :], ALU.mult)
                ins = v.tensor_tensor(sqkm[:, 1024:2048], P1[:, 1024:2048],
                                      x.m1u8[:, :], ALU.mult)
                x.s_v.inc(ins)
            rec("sqkmwr", x.s_v)
            rec("P1free", x.s_v)

            @blk.tensor
            def _(t, c=c, p=p, etm=etm, q3=q3):
                wt(t, "sqkmwr", "P2free_afterF")
                for h in range(H):
                    t.matmul(P2[:, h * M:(h + 1) * M], sqkm[:, h * C:(h + 1) * C],
                             etm[:, h * M:(h + 1) * M], start=True, stop=False)
                    ins = t.matmul(P2[:, h * M:(h + 1) * M], q3[:, h * C:(h + 1) * C],
                                   x.hkb2[:, p * H * M + h * M: p * H * M + (h + 1) * M],
                                   start=False, stop=True)
                x.s_pe.inc(ins)
            rec("BPE", x.s_pe)
            rec("sqkmrd", x.s_pe)

            @blk.vector
            def _(v, c=c, p=p, dnr=dnr):
                wt(v, "BPE", f"denr{p}", "t1wr")
                x.s_v.inc(v.tensor_tensor(t1[:, :], P2[:, :], dnr, ALU.mult))
            rec("okV", x.s_v)

            @blk.scalar
            def _(sc, c=c):
                wt(sc, "okV", "eord")
                x.s_s.inc(sc.activation(eo[:, :], t1[:, :], AF.Exp, bias=0.0, scale=1.0))
            rec("eoS", x.s_s)
            rec("t1rd", x.s_s)

            @blk.vector
            def _(v, c=c, p=p, dnr=dnr):
                wt(v, "eoS", "pvtrd")
                v.tensor_reduce(rs2[:, :], eo[:, :].rearrange("p (i m) -> p i m", i=H),
                                AX, ALU.add)
                v.reciprocal(rs[:, 0:8], rs2[:, 0:8])
                v.reciprocal(rs[:, 8:16], rs2[:, 8:16])
                v.tensor_tensor(t1[:, :], eo[:, :], dnr, ALU.mult)
                ins = v.tensor_tensor(
                    pvt[:, :].rearrange("p (i m) -> p i m", i=H),
                    t1[:, :].rearrange("p (i m) -> p i m", i=H),
                    rs[:, :].rearrange("p (i o) -> p i o", o=1).broadcast_to([128, H, M]),
                    ALU.mult)
                x.s_v.inc(ins)
            rec("pvtwr", x.s_v)
            rec("P2free", x.s_v)
            rec("eord", x.s_v)

            if _KDBG and c == 0:
                @blk.scalar
                def _(sc):
                    sc.wait_ge(x.s_v.h, x.s_v.n)
                    x.s_wr.inc(sc.dma_start(out=x.dbg_pvt, in_=pvt[:, :]), 16)
                    x.s_wr.inc(sc.dma_start(out=x.dbg_rs, in_=rs[:, :]), 16)

            @blk.tensor
            def _(t, c=c):
                wt(t, "pvtwr", "P3")
                for i in range(H):
                    ins = t.transpose(P3b[0:64, i * C:(i + 1) * C],
                                      pvt[:, i * M:(i + 1) * M], x.ident[:, :])
                x.s_pe.inc(ins)
            rec("TPE", x.s_pe)
            rec("pvtrd", x.s_pe)

            @blk.vector
            def _(v, c=c):
                wt(v, "TPE", "pvfrd")
                x.s_v.inc(v.tensor_copy(pvf[:, :], P3b[0:64, :]))
            rec("pvfwr", x.s_v)
            rec("P3", x.s_v)

            if _KDBG and c == 0:
                @blk.scalar
                def _(sc):
                    sc.wait_ge(x.s_v.h, x.s_v.n)
                    x.s_wr.inc(sc.dma_start(out=x.dbg_pvf, in_=pvf[:, :]), 16)

            @blk.tensor
            def _(t, c=c, ef3=ef3):
                wt(t, "pvfwr", "P1free")
                for h in range(H):
                    ins = t.matmul(P1[:, h * C:(h + 1) * C], ef3[:, h * C:(h + 1) * C],
                                   pvf[:, h * C:(h + 1) * C], start=True, stop=True)
                x.s_pe.inc(ins)
            rec("DPE", x.s_pe)

            @blk.vector
            def _(v, c=c):
                wt(v, "DPE", "w2mrd")
                v.tensor_tensor(w2m[:, 0:1024], P1[:, 0:1024], x.m1u8[:, :], ALU.mult)
                ins = v.tensor_tensor(w2m[:, 1024:2048], P1[:, 1024:2048],
                                      x.m1u8[:, :], ALU.mult)
                x.s_v.inc(ins)
            rec("w2mwr", x.s_v)
            rec("P1free", x.s_v)

            if _KDBG and c == 0:
                @blk.scalar
                def _(sc):
                    sc.wait_ge(x.s_v.h, x.s_v.n)
                    x.s_wr.inc(sc.dma_start(out=x.dbg_w2, in_=w2m[:, :]), 16)

            @blk.tensor
            def _(t, c=c, p=p, vtm=vtm):
                wt(t, "w2mwr", "P1free", "ovrd")
                for h in range(H):
                    t.matmul(P1[:, h * C:(h + 1) * C], vtm[:, h * DV:(h + 1) * DV],
                             w2m[:, h * C:(h + 1) * C], start=True, stop=False)
                    ins = t.matmul(P1[:, h * C:(h + 1) * C],
                                   x.hvb2[:, p * H * DV + h * DV: p * H * DV + (h + 1) * DV],
                                   pvf[:, h * C:(h + 1) * C], start=False, stop=True)
                x.s_pe.inc(ins)
            rec("EPE", x.s_pe)
            rec("w2mrd", x.s_pe)
            rec("pvfrd", x.s_pe)
            rec(f"tilerd{p}", x.s_pe)
            rec(f"shadrd{p}", x.s_pe)

            @blk.vector
            def _(v, c=c):
                wt(v, "EPE")
                x.s_v.inc(v.tensor_copy(ovsb[:, :], P1[:, :]))
            rec("ovwr", x.s_v)
            rec("P1free", x.s_v)

            if _KDBG and c == 0:
                @blk.scalar
                def _(sc):
                    sc.wait_ge(x.s_v.h, x.s_v.n)
                    x.s_wr.inc(sc.dma_start(out=x.dbg_ov, in_=ovsb[:, :]), 16)

            @blk.scalar
            def _(sc, c=c):
                wt(sc, "ovwr", "sqrd")
                x.s_s.inc(sc.activation(sq_sb[:, :], ovsb[:, :], AF.Square,
                                        bias=0.0, scale=1.0))
            rec("sqwr", x.s_s)

            @blk.tensor
            def _(t, c=c):
                wt(t, "sqwr", "P2free", "P3")
                for i in range(8):
                    t.matmul(P2[0:1, i * C:(i + 1) * C], x.onescol[:, :],
                             sq_sb[:, i * DV:(i + 1) * DV], start=True, stop=True)
                for i in range(8):
                    ins = t.matmul(P3f[0:1, i * 128:(i + 1) * 128], x.onescol[:, :],
                                   sq_sb[:, (8 + i) * DV:(9 + i) * DV],
                                   start=True, stop=True)
                x.s_pe.inc(ins)
            rec("RPE", x.s_pe)
            rec("sqrd", x.s_pe)

            @blk.scalar
            def _(sc, c=c):
                wt(sc, "RPE")
                sc.activation(rrow[:, 0:1024], P2[0:1, :], AF.Ln, bias=1e-5, scale=1.0 / DV)
                sc.activation(rrow[:, 1024:2048], P3f[0:1, :], AF.Ln, bias=1e-5, scale=1.0 / DV)
                x.s_s.inc(sc.activation(rrow[:, :], rrow[:, :], AF.Exp, bias=0.0, scale=-0.5))
            rec("rrowS", x.s_s)
            rec("P2free", x.s_s)
            rec("P3", x.s_s)

            @blk.tensor
            def _(t, c=c):
                wt(t, "rrowS", "P1free")
                for i in range(H):
                    ins = t.matmul(P1[:, i * C:(i + 1) * C], x.onesrow[:, :],
                                   rrow[:, i * C:(i + 1) * C], start=True, stop=True)
                x.s_pe.inc(ins)
            rec("RBPE", x.s_pe)

            @blk.vector
            def _(v, c=c, g3=g3, cst=cst):
                wt(v, "RBPE", "rgrd", "ogrd")
                v.tensor_scalar(rg_sb[:, :], P1[:, :], x.gnw_sb[:, 0:1], None, ALU.mult)
                v.tensor_tensor(on_sb[:, :], ovsb[:, :], rg_sb[:, :], ALU.mult)
                ins = v.tensor_tensor(
                    og3[:, :, cst:cst + C],
                    on_sb[:, :].rearrange("p (h t) -> p h t", h=H),
                    g3.rearrange("p (h t) -> p h t", h=H), ALU.mult)
                x.s_v.inc(ins)
            rec("ogV", x.s_v)
            rec("ovrd", x.s_v)
            rec("rgrd", x.s_v)
            rec("P1free", x.s_v)
            rec(f"qkld{p}", x.s_v)
            rec(f"qkrd{p}", x.s_v)

            if _KDBG and c == 7:
                @blk.scalar
                def _(sc):
                    sc.wait_ge(x.s_v.h, x.s_v.n)
                    x.s_wr.inc(sc.dma_start(out=x.dbg_og, in_=og2[:, :]), 16)

            if c + 2 < NCH:
                tile_loads(c + 2)

            if c + 1 < NCH:
                emit_denom(c + 1)

            if c % 8 == 7:
                _emit_wo(x, blk, c // 8, og3, wos, xt, x2t, P1, P2, rec, wt)


def _emit_wo(x, blk, sup, og3, wos, xt, x2t, P1, P2, rec, wt):
    nc = x.nc
    pe_evt = []
    for dc in range(NDC):
        bb = dc % 2
        wslab = wos[:, bb * NDC * 128:(bb + 1) * NDC * 128]
        slw = wslab.rearrange("p (a c) -> p a c", a=NDC)

        @blk.gpsimd
        def _(gp, dc=dc, wslab=wslab):
            if dc >= 2:
                gp.wait_ge(x.s_pe.h, pe_evt[dc - 2])
            else:
                gp.wait_ge(x.s_pe.h, x.s_pe.n)
            ins = gp.dma_start(out=wslab, in_=x.wo_t[dc])
            x.s_gd.inc(ins, 16)

        for g2 in range(2):
            tok0 = sup * 1024 + g2 * G
            xtb = xt[:, bb * G:(bb + 1) * G]
            x2b = x2t[:, bb * G:(bb + 1) * G]
            psl = P1[:, (bb * 2 + g2) * G % 2048:((bb * 2 + g2) * G % 2048) + G]

            @blk.sync
            def _(sy, xtb=xtb, dc=dc, tok0=tok0):
                sy.wait_ge(x.s_v.h, x.s_v.n)
                x.s_ld.inc(sy.dma_start(out=xtb,
                                        in_=x.x_fm[dc * 128:(dc + 1) * 128,
                                                   tok0:tok0 + G]), 16)

            @blk.tensor
            def _(t, dc=dc, g2=g2, psl=psl, slw=slw):
                t.wait_ge(x.s_gd.h, x.s_gd.n)
                if dc < 2 and g2 == 0:
                    wt(t, "P1free", "ogV")
                t.wait_ge(x.s_v.h, x.s_v.n)
                for vd in range(H):
                    ins = t.matmul(psl, slw[:, vd, :],
                                   og3[:, vd, g2 * G:(g2 + 1) * G],
                                   start=(vd == 0), stop=(vd == H - 1))
                x.s_pe.inc(ins)
            if g2 == 1:
                pe_evt.append(x.s_pe.n)

            @blk.vector
            def _(v, dc=dc, g2=g2, psl=psl, xtb=xtb, x2b=x2b):
                v.wait_ge(x.s_pe.h, x.s_pe.n)
                v.wait_ge(x.s_ld.h, x.s_ld.n)
                v.wait_ge(x.s_wr.h, x.s_wr.n)
                x.s_v.inc(v.tensor_tensor(x2b, psl, xtb, ALU.add))

            @blk.scalar
            def _(sc, dc=dc, x2b=x2b, tok0=tok0):
                sc.wait_ge(x.s_v.h, x.s_v.n)
                x.s_wr.inc(sc.dma_start(out=x.x2_scr[dc, :, tok0:tok0 + G],
                                        in_=x2b), 16)
    rec("P1free", x.s_v)
    rec("ogrd", x.s_pe)


def _emit_ffn_fused(x, blk):
    nc = x.nc
    GT = 1024
    with nc.sbuf_tensor("ff_x", [128, NDC * GT], BF16) as xg2, \
         nc.sbuf_tensor("ff_hid", [128, NHC * GT], BF16) as hid2, \
         nc.sbuf_tensor("ff_sg", [128, 2 * NDC * 128], BF16) as slg, \
         nc.sbuf_tensor("ff_su", [128, 2 * NDC * 128], BF16) as slu, \
         nc.sbuf_tensor("ff_wd", [128, 2 * NHC * 128], BF16) as wds, \
         nc.sbuf_tensor("ff_act", [128, GT], F32) as actt, \
         nc.sbuf_tensor("ff_x2", [128, 2 * GT], F32) as x2t, \
         nc.sbuf_tensor("ff_o", [128, 2 * GT], F32) as ot, \
         nc.psum_tensor("ff_PA", [128, 2048], F32) as PA, \
         nc.psum_tensor("ff_PB", [128, 2048], F32) as PB:
        xg3 = xg2[:, :].rearrange("p (a t) -> p a t", a=NDC)
        hid3 = hid2[:, :].rearrange("p (h t) -> p h t", h=NHC)
        ev = {}

        def rec(key, sem):
            ev[key] = (sem, sem.n)

        def wt(eng, *keys):
            for k in keys:
                if k in ev:
                    sm, n = ev[k]
                    eng.wait_ge(sm.h, n)

        for grp in range(2):
            t0 = grp * GT

            @blk.sync
            def _(sy, grp=grp, t0=t0):
                wt(sy, "xgrd")
                sy.wait_ge(x.s_wr.h, x.s_wr.n)
                x.s_ld.inc(sy.dma_start(
                    out=xg3[:, :, :],
                    in_=x.xln2_scr.rearrange("a p t -> p a t")[:, :, t0:t0 + GT]), 16)
            rec("xgld", x.s_ld)

            for hc in range(NHC):
                b = hc % 2
                sgb = slg[:, b * NDC * 128:(b + 1) * NDC * 128]
                sub = slu[:, b * NDC * 128:(b + 1) * NDC * 128]
                pg = PA[:, b * 1024:(b + 1) * 1024]
                pu = PB[:, b * 1024:(b + 1) * 1024]

                @blk.gpsimd
                def _(gp, hc=hc, sgb=sgb, sub=sub):
                    wt(gp, f"slabrd{hc % 2}")
                    x.s_gd.inc(gp.dma_start(out=sgb, in_=x.wg_t[hc]), 16)
                    x.s_gd.inc(gp.dma_start(out=sub, in_=x.wu_t[hc]), 16)
                rec(f"slabld{hc % 2}", x.s_gd)

                @blk.vector
                def _(v, hc=hc, sgb=sgb, sub=sub):
                    wt(v, f"slabld{hc % 2}", f"slabrd{hc % 2}")
                    v.tensor_scalar(sgb, sgb, -128.0, None, ALU.add)
                    x.s_v.inc(v.tensor_scalar(sub, sub, -128.0, None, ALU.add))
                rec(f"deq{hc % 2}", x.s_v)

                sg3 = sgb.rearrange("p (a c) -> p a c", a=NDC)
                su3 = sub.rearrange("p (a c) -> p a c", a=NDC)

                @blk.tensor
                def _(t, hc=hc, sg3=sg3, su3=su3, pg=pg, pu=pu):
                    wt(t, f"deq{hc % 2}", "xgld", f"gu{hc % 2}")
                    if hc < 2:
                        wt(t, "pdrd0", "pdrd1")
                    for g2 in range(2):
                        for dc in range(NDC):
                            t.matmul(pg[:, g2 * G:(g2 + 1) * G], sg3[:, dc, :],
                                     xg3[:, dc, g2 * G:(g2 + 1) * G],
                                     start=(dc == 0), stop=(dc == NDC - 1))
                    for g2 in range(2):
                        for dc in range(NDC):
                            ins = t.matmul(pu[:, g2 * G:(g2 + 1) * G], su3[:, dc, :],
                                           xg3[:, dc, g2 * G:(g2 + 1) * G],
                                           start=(dc == 0), stop=(dc == NDC - 1))
                    x.s_pe.inc(ins)
                rec(f"slabrd{hc % 2}", x.s_pe)
                rec(f"gupe{hc}", x.s_pe)
                if hc == NHC - 1:
                    rec("xgrd", x.s_pe)

                @blk.scalar
                def _(sc, hc=hc, pg=pg):
                    wt(sc, f"gupe{hc}", "actrd")
                    x.s_s.inc(sc.activation(actt[:, :], pg, AF.Silu, bias=0.0,
                                            scale=x.sg_sc[:, hc:hc + 1]))
                rec(f"sil{hc}", x.s_s)

                @blk.vector
                def _(v, hc=hc, pu=pu):
                    wt(v, f"sil{hc}", "hidrd")
                    x.s_v.inc(v.scalar_tensor_tensor(hid3[:, hc, :], pu,
                                                     x.su_sc[:, hc:hc + 1],
                                                     actt[:, :], ALU.mult, ALU.mult))
                rec(f"gu{hc % 2}", x.s_v)
                rec("actrd", x.s_v)
                rec(f"hid{hc}", x.s_v)

            for dc in range(NDC):
                b = dc % 2
                wdb = wds[:, b * NHC * 128:(b + 1) * NHC * 128]
                pd = PA[:, b * 1024:(b + 1) * 1024]
                x2b = x2t[:, b * GT:(b + 1) * GT]
                otb = ot[:, b * GT:(b + 1) * GT]

                @blk.gpsimd
                def _(gp, dc=dc, wdb=wdb):
                    wt(gp, f"wdrd{dc % 2}")
                    x.s_gd.inc(gp.dma_start(out=wdb, in_=x.wd_t[dc]), 16)
                rec(f"wdld{dc % 2}", x.s_gd)

                @blk.sync
                def _(sy, dc=dc, t0=t0, x2b=x2b):
                    wt(sy, f"x2rd{dc % 2}")
                    x.s_ld.inc(sy.dma_start(out=x2b,
                                            in_=x.x2_scr[dc, :, t0:t0 + GT]), 16)
                rec(f"x2ld{dc % 2}", x.s_ld)

                @blk.vector
                def _(v, dc=dc, wdb=wdb):
                    wt(v, f"wdld{dc % 2}", f"wdrd{dc % 2}")
                    x.s_v.inc(v.tensor_scalar(wdb, wdb, -128.0, None, ALU.add))
                rec(f"wdeq{dc % 2}", x.s_v)

                wd3 = wdb.rearrange("p (a c) -> p a c", a=NHC)

                @blk.tensor
                def _(t, dc=dc, wd3=wd3, pd=pd):
                    wt(t, f"wdeq{dc % 2}", f"pdrd{dc % 2}")
                    if dc < 2:
                        wt(t, f"sil{NHC - 2}", f"sil{NHC - 1}")
                    for g2 in range(2):
                        for hc in range(NHC):
                            ins = t.matmul(pd[:, g2 * G:(g2 + 1) * G], wd3[:, hc, :],
                                           hid3[:, hc, g2 * G:(g2 + 1) * G],
                                           start=(hc == 0), stop=(hc == NHC - 1))
                    x.s_pe.inc(ins)
                rec(f"wdrd{dc % 2}", x.s_pe)
                rec(f"dpe{dc}", x.s_pe)
                if dc == NDC - 1:
                    rec("hidrd", x.s_pe)

                @blk.vector
                def _(v, dc=dc, pd=pd, x2b=x2b, otb=otb):
                    wt(v, f"dpe{dc}", f"x2ld{dc % 2}", f"otwr{dc % 2}")
                    x.s_v.inc(v.scalar_tensor_tensor(otb, pd, x.sd_sc[:, dc:dc + 1],
                                                     x2b, ALU.mult, ALU.add))
                rec(f"otV{dc % 2}", x.s_v)
                rec(f"x2rd{dc % 2}", x.s_v)
                rec(f"pdrd{dc % 2}", x.s_v)

                @blk.scalar
                def _(sc, dc=dc, t0=t0, otb=otb):
                    wt(sc, f"otV{dc % 2}")
                    x.s_wr.inc(sc.dma_start(
                        out=x.out_fm[dc * 128:(dc + 1) * 128, t0:t0 + GT],
                        in_=otb), 16)
                rec(f"otwr{dc % 2}", x.s_wr)


_CACHE = {}


def _tile_slabs(w2d):
    r, c = w2d.shape
    return np.ascontiguousarray(
        w2d.reshape(r // 128, 128, c // 128, 128).transpose(2, 1, 0, 3)
        .reshape(c // 128, 128, (r // 128) * 128))


def _lowbyte(a_int32):
    import sys
    a = np.ascontiguousarray(np.asarray(a_int32))
    return a.view(np.uint8).reshape(a.shape + (4,))[..., 0 if sys.byteorder == "little" else 3]


def _host_inputs(inputs):
    x = np.asarray(inputs["L0_attn_hidden_states"])
    wpj = np.concatenate([_tile_slabs(np.asarray(inputs[k]))
                          for k in ("L0_attn_Wq", "L0_attn_Wk", "L0_attn_Ws",
                                    "L0_attn_Wv", "L0_attn_Wg")], axis=0)
    shared = dict(
        wpj=wpj,
        wo_t=_tile_slabs(np.asarray(inputs["L0_attn_Wo"])),
        wg_t=_tile_slabs(np.ascontiguousarray(_lowbyte(inputs["L0_ffn_wq_gate"]).T)),
        wu_t=_tile_slabs(np.ascontiguousarray(_lowbyte(inputs["L0_ffn_wq_up"]).T)),
        wd_t=_tile_slabs(np.ascontiguousarray(_lowbyte(inputs["L0_ffn_wq_down"]).T)),
        sgate=np.ascontiguousarray(np.asarray(inputs["L0_ffn_s_gate"]).reshape(NHC, 128).T),
        sup=np.ascontiguousarray(np.asarray(inputs["L0_ffn_s_up"]).reshape(NHC, 128).T),
        sdown=np.ascontiguousarray(np.asarray(inputs["L0_ffn_s_down"]).reshape(NDC, 128).T),
        gnw=np.ascontiguousarray(np.asarray(inputs["L0_attn_g_norm_weight"]).reshape(128, 1)),
        m1u=np.ascontiguousarray(np.triu(np.ones((128, 128))).astype(_BF)),
        m1u8=np.ascontiguousarray(np.tile(np.triu(np.ones((128, 128))).astype(_BF), (1, 8))),
        ident=np.eye(128).astype(_BF),
        cst=np.ascontiguousarray(np.tile(np.array([CONSTS], np.float32), (128, 1))),
        onescol=np.ones((128, 1), dtype=_BF),
        onesrow=np.ones((1, 128), dtype=np.float32),
    )
    per_core = []
    for c in range(NCORES):
        b, qt = divmod(c, 4)
        mk = np.zeros((128, NCORES), np.float32)
        for r in range(b * 4, c):
            mk[:, r] = 1.0
        m = dict(shared)
        m["x_fm"] = np.ascontiguousarray(x[b, qt * T:(qt + 1) * T, :].T)
        m["maskpre"] = mk
        per_core.append(m)
    return per_core


def kernel(**inputs):
    if "nc" not in _CACHE:
        _CACHE["nc"] = build()
    nc = _CACHE["nc"]
    in_maps = _host_inputs(inputs)
    res = run_bass_kernel_spmd(nc, in_maps, core_ids=list(range(NCORES)))
    out = np.empty((2, 8192, D), np.float32)
    for c in range(NCORES):
        b, qt = divmod(c, 4)
        out[b, qt * T:(qt + 1) * T, :] = res.results[c]["out_fm"].T
    return out


# version 07
# speedup vs baseline: 1.0242x; 1.0006x over previous
import os
import numpy as np
import ml_dtypes
import concourse.bass as bass
import concourse.mybir as mybir
from concourse.bass_utils import run_bass_kernel_spmd

F32 = mybir.dt.float32
BF16 = mybir.dt.bfloat16
I32 = mybir.dt.int32
U8 = mybir.dt.uint8
AF = mybir.ActivationFunctionType
ALU = mybir.AluOpType
AX = mybir.AxisListType.X

D = 2048
T = 2048
H = 16
DK = 64
DV = 128
M = 64
C = 128
NCH = T // C
G = 512
NG = T // G
HID = 5632
NHC = HID // 128
NDC = D // 128
NCORES = 8

CONSTS = [0.0, 1.0, -1.0, -0.5, -32.0, 1e-5, 1.0 / 128.0, 0.125, 1.0 / 2048.0]
_BF = ml_dtypes.bfloat16

_KPHASE = os.environ.get("KPHASE", "all")
_KDBG = os.environ.get("KDBG", "0") == "1"
_PHASES = ["b1", "b2", "b3", "b4", "b5", "b6"]


def _upto(ph):
    return _KPHASE == "all" or _PHASES.index(ph) <= _PHASES.index(_KPHASE)


class Sem:
    def __init__(self, h):
        self.h = h
        self.n = 0

    def inc(self, instr, amt=1):
        instr.then_inc(self.h, amt)
        self.n += amt
        return instr


class Ctx:
    pass


def build():
    nc = bass.Bass("TRN2", target_bir_lowering=False)
    x = Ctx()
    x.nc = nc
    scr_kind = "ExternalOutput" if _KDBG else "Internal"

    x.x_fm = nc.dram_tensor("x_fm", [D, T], F32, kind="ExternalInput").ap()
    x.wpj = nc.dram_tensor("wpj", [56, 128, NDC * 128], F32, kind="ExternalInput").ap()
    x.wo_t = nc.dram_tensor("wo_t", [NDC, 128, NDC * 128], F32, kind="ExternalInput").ap()
    x.wg_t = nc.dram_tensor("wg_t", [NHC, 128, NDC * 128], U8, kind="ExternalInput").ap()
    x.wu_t = nc.dram_tensor("wu_t", [NHC, 128, NDC * 128], U8, kind="ExternalInput").ap()
    x.wd_t = nc.dram_tensor("wd_t", [NDC, 128, NHC * 128], U8, kind="ExternalInput").ap()
    x.sgate = nc.dram_tensor("sgate", [128, NHC], F32, kind="ExternalInput").ap()
    x.sup = nc.dram_tensor("sup", [128, NHC], F32, kind="ExternalInput").ap()
    x.sdown = nc.dram_tensor("sdown", [128, NDC], F32, kind="ExternalInput").ap()
    x.gnw = nc.dram_tensor("gnw", [128, 1], F32, kind="ExternalInput").ap()
    x.m1u_i = nc.dram_tensor("m1u", [128, 128], BF16, kind="ExternalInput").ap()
    x.m1u8_i = nc.dram_tensor("m1u8", [128, 8 * 128], BF16, kind="ExternalInput").ap()
    x.ident_i = nc.dram_tensor("ident", [128, 128], BF16, kind="ExternalInput").ap()
    x.cst_i = nc.dram_tensor("cst", [128, len(CONSTS)], F32, kind="ExternalInput").ap()
    x.mask_i = nc.dram_tensor("maskpre", [128, NCORES], F32, kind="ExternalInput").ap()
    x.onescol_i = nc.dram_tensor("onescol", [128, 1], BF16, kind="ExternalInput").ap()
    x.onesrow_i = nc.dram_tensor("onesrow", [1, 128], F32, kind="ExternalInput").ap()

    x.out_fm = nc.dram_tensor("out_fm", [D, T], F32, kind="ExternalOutput").ap()

    x.q_scr = nc.dram_tensor("q_scr", [H, DK, T], BF16, kind=scr_kind).ap()
    x.k_scr = nc.dram_tensor("k_scr", [H, DK, T], BF16, kind=scr_kind).ap()
    x.e_scr = nc.dram_tensor("e_scr", [H, M, T], BF16, kind=scr_kind).ap()
    x.v_scr = nc.dram_tensor("v_scr", [H, DV, T], BF16, kind=scr_kind).ap()
    x.g_scr = nc.dram_tensor("g_scr", [H, DV, T], BF16, kind=scr_kind).ap()
    x.x2_scr = nc.dram_tensor("x2_scr", [NDC, 128, T], F32, kind=scr_kind).ap()
    x.hid_scr = nc.dram_tensor("hid_scr", [NHC, 128, T], BF16, kind=scr_kind).ap()
    x.xln2_scr = nc.dram_tensor("xln2_scr", [NDC, 128, T], BF16, kind=scr_kind).ap()
    x.seg_d = nc.dram_tensor("seg_d", [64, 4096], F32).ap()
    if _KDBG:
        x.dbg_denr = nc.dram_tensor("dbg_denr", [2, 128, 1024], F32, kind="ExternalOutput").ap()
        x.dbg_og = nc.dram_tensor("dbg_og", [128, H * 1024], mybir.dt.bfloat16, kind="ExternalOutput").ap()
        x.dbg_ov = nc.dram_tensor("dbg_ov", [128, H * DV], F32, kind="ExternalOutput").ap()
        x.dbg_pvt = nc.dram_tensor("dbg_pvt", [128, H * M], mybir.dt.bfloat16, kind="ExternalOutput").ap()
        x.dbg_rs = nc.dram_tensor("dbg_rs", [128, 16], F32, kind="ExternalOutput").ap()
        x.dbg_pvf = nc.dram_tensor("dbg_pvf", [64, H * C], mybir.dt.bfloat16, kind="ExternalOutput").ap()
        x.dbg_w2 = nc.dram_tensor("dbg_w2", [128, H * C], mybir.dt.bfloat16, kind="ExternalOutput").ap()
    x.gath_d = nc.dram_tensor("gath_d", [NCORES * 64, 4096], F32).ap()

    x.e_rows = x.e_scr.rearrange("h m t -> (h m) t")
    x.k_rows = x.k_scr.rearrange("h m t -> (h m) t")
    x.v_rows = x.v_scr.rearrange("h m t -> (h m) t")

    from contextlib import ExitStack
    es = ExitStack()
    with es:
        x.cst = es.enter_context(nc.sbuf_tensor("cst_sb", [128, len(CONSTS)], F32))
        x.m1u = es.enter_context(nc.sbuf_tensor("m1u_sb", [128, 128], BF16))
        x.m1u8 = es.enter_context(nc.sbuf_tensor("m1u8_sb", [128, 8 * 128], BF16))
        x.ident = es.enter_context(nc.sbuf_tensor("ident_sb", [128, 128], BF16))
        x.gnw_sb = es.enter_context(nc.sbuf_tensor("gnw_sb", [128, 1], F32))
        x.mask_sb = es.enter_context(nc.sbuf_tensor("mask_sb", [128, NCORES], F32))
        x.onescol = es.enter_context(nc.sbuf_tensor("onescol_sb", [128, 1], BF16))
        x.onesrow = es.enter_context(nc.sbuf_tensor("onesrow_sb", [1, 128], F32))
        x.sg_sc = es.enter_context(nc.sbuf_tensor("sg_sc", [128, NHC], F32))
        x.su_sc = es.enter_context(nc.sbuf_tensor("su_sc", [128, NHC], F32))
        x.sd_sc = es.enter_context(nc.sbuf_tensor("sd_sc", [128, NDC], F32))

        for i, vv in enumerate(CONSTS):
            nc.const_aps.register(F32, vv, x.cst[:, i:i + 1])

        x.s_ld = Sem(es.enter_context(nc.semaphore("s_ld")))
        x.s_wr = Sem(es.enter_context(nc.semaphore("s_wr")))
        x.s_gd = Sem(es.enter_context(nc.semaphore("s_gd")))
        x.s_cc = Sem(es.enter_context(nc.semaphore("s_cc")))
        x.s_pe = Sem(es.enter_context(nc.semaphore("s_pe")))
        x.s_v = Sem(es.enter_context(nc.semaphore("s_v")))
        x.s_s = Sem(es.enter_context(nc.semaphore("s_s")))
        x.s_g = Sem(es.enter_context(nc.semaphore("s_g")))

        with nc.Block() as blk:
            @blk.sync
            def _(sy):
                for dst, src in [(x.cst, x.cst_i), (x.m1u, x.m1u_i), (x.m1u8, x.m1u8_i),
                                 (x.ident, x.ident_i), (x.gnw_sb, x.gnw),
                                 (x.mask_sb, x.mask_i), (x.onescol, x.onescol_i),
                                 (x.onesrow, x.onesrow_i), (x.sg_sc, x.sgate),
                                 (x.su_sc, x.sup), (x.sd_sc, x.sdown)]:
                    x.s_ld.inc(sy.dma_start(out=dst[:, :], in_=src), 16)

            @blk.vector
            def _(v):
                v.wait_ge(x.s_ld.h, x.s_ld.n)

        if _upto("b1"):
            with nc.sbuf_tensor("xln", [128, NDC * T], BF16) as _xln_flat, \
                 nc.Block() as blk:
                xln = _xln_flat
                _emit_ln(x, blk, x.x_fm, xln, "ln1")
                _emit_proj(x, blk, xln)

        if _upto("b2"):
            with nc.sbuf_tensor("acc", [64, 4096], F32) as acc, \
                 nc.sbuf_tensor("hkb2", [64, 2 * H * M], BF16) as hkb2, \
                 nc.sbuf_tensor("hvb2", [64, 2 * H * DV], BF16) as hvb2, \
                 nc.sbuf_tensor("wsnap", [1, 2 * H * M], F32) as wsnap:
                x.acc, x.hkb2, x.hvb2, x.wsnap = acc, hkb2, hvb2, wsnap
                with nc.Block() as blk:
                    _emit_states_exchange(x, blk)
                if _upto("b3"):
                    with nc.Block() as blk:
                        _emit_attn(x, blk)

        if _upto("b4"):
            with nc.sbuf_tensor("xln2", [128, NDC * T], BF16) as xln2:
                x.xln2 = xln2
                with nc.Block() as blk:
                    _emit_ln(x, blk, x.x2_scr.rearrange("a p t -> (a p) t"), x.xln2, "ln2")

                    @blk.scalar
                    def _(sc):
                        sc.wait_ge(x.s_v.h, x.s_v.n)
                        sc.wait_ge(x.s_g.h, x.s_g.n)
                        for dc in range(NDC):
                            x.s_wr.inc(sc.dma_start(
                                out=x.xln2_scr[dc],
                                in_=_v3(x.xln2, NDC, T)[:, dc, :]), 16)
            if _upto("b5"):
                with nc.Block() as blk:
                    _emit_ffn_fused(x, blk)

        with nc.Block() as blk:
            @blk.sync
            def _(sy):
                sy.wait_ge(x.s_wr.h, x.s_wr.n)
                sy.wait_ge(x.s_ld.h, x.s_ld.n)
                sy.wait_ge(x.s_gd.h, x.s_gd.n)

    return nc


def _v3(t2d, a, b):
    return t2d[:, :].rearrange("p (a b) -> p a b", a=a)


def _emit_ln(x, blk, src_fm, dst, tag):
    nc = x.nc
    src3 = src_fm.rearrange("(a p) t -> p a t", p=128)
    dst3 = _v3(dst, NDC, T)
    with nc.sbuf_tensor(f"{tag}_xg", [128, NDC * G], F32) as xg2, \
         nc.sbuf_tensor(f"{tag}_xb", [128, NDC * G], BF16) as xb2, \
         nc.sbuf_tensor(f"{tag}_sq", [128, NDC * G], BF16) as sq2, \
         nc.sbuf_tensor(f"{tag}_rows", [1, G * 6], F32) as rows, \
         nc.sbuf_tensor(f"{tag}_bc", [128, 2 * G], F32) as bc, \
         nc.psum_tensor(f"{tag}_psS", [1, G], F32) as psS, \
         nc.psum_tensor(f"{tag}_psQ", [1, G], F32) as psQ, \
         nc.psum_tensor(f"{tag}_psB", [128, 2 * G], F32) as psB:
        xg = _v3(xg2, NDC, G)
        xb = _v3(xb2, NDC, G)
        sq = _v3(sq2, NDC, G)
        mu, msq, tt, var, rstd, mup = (rows[0:1, i * G:(i + 1) * G] for i in range(6))
        for g in range(NG):
            @blk.sync
            def _(sy, g=g):
                sy.wait_ge(x.s_v.h, x.s_v.n)
                sy.wait_ge(x.s_g.h, x.s_g.n)
                sy.wait_ge(x.s_wr.h, x.s_wr.n)
                x.s_ld.inc(sy.dma_start(out=xg[:, :, :],
                                        in_=src3[:, :, g * G:(g + 1) * G]), 16)

            @blk.vector
            def _(v, g=g):
                v.wait_ge(x.s_ld.h, x.s_ld.n)
                v.wait_ge(x.s_pe.h, x.s_pe.n)
                v.tensor_copy(xb2[:, :], xg2[:, :])
                x.s_v.inc(v.tensor_tensor(sq2[:, :], xb2[:, :], xb2[:, :], ALU.mult))

            @blk.tensor
            def _(t, g=g):
                t.wait_ge(x.s_v.h, x.s_v.n)
                for dc in range(NDC):
                    t.matmul(psS[:, :], x.onescol[:, :], xb[:, dc, :],
                             start=(dc == 0), stop=(dc == NDC - 1))
                for dc in range(NDC):
                    ins = t.matmul(psQ[:, :], x.onescol[:, :], sq[:, dc, :],
                                   start=(dc == 0), stop=(dc == NDC - 1))
                x.s_pe.inc(ins)

            @blk.vector
            def _(v, g=g):
                v.wait_ge(x.s_pe.h, x.s_pe.n)
                v.tensor_scalar(mu, psS[:, :], 1.0 / D, None, ALU.mult)
                v.tensor_scalar(msq, psQ[:, :], 1.0 / D, None, ALU.mult)
                v.tensor_tensor(tt, mu, mu, ALU.mult)
                x.s_v.inc(v.tensor_tensor(var, msq, tt, ALU.subtract))

            @blk.scalar
            def _(sc, g=g):
                sc.wait_ge(x.s_v.h, x.s_v.n)
                sc.activation(tt, var, AF.Ln, bias=1e-5, scale=1.0)
                x.s_s.inc(sc.activation(rstd, tt, AF.Exp, bias=0.0, scale=-0.5))

            @blk.vector
            def _(v, g=g):
                v.wait_ge(x.s_s.h, x.s_s.n)
                x.s_v.inc(v.scalar_tensor_tensor(mup, mu, -1.0, rstd, ALU.mult, ALU.mult))

            @blk.tensor
            def _(t, g=g):
                t.wait_ge(x.s_v.h, x.s_v.n)
                t.matmul(psB[:, 0:G], x.onesrow[:, :], rstd, start=True, stop=True)
                ins = t.matmul(psB[:, G:2 * G], x.onesrow[:, :], mup, start=True, stop=True)
                x.s_pe.inc(ins)

            @blk.vector
            def _(v, g=g):
                v.wait_ge(x.s_pe.h, x.s_pe.n)
                x.s_v.inc(v.tensor_copy(bc[:, :], psB[:, :]))

            NPOOL = 6

            @blk.gpsimd
            def _(gp, g=g):
                gp.wait_ge(x.s_v.h, x.s_v.n)
                for dc in range(NDC - NPOOL, NDC):
                    gp.tensor_tensor(xg[:, dc, :], xg[:, dc, :], bc[:, 0:G], ALU.mult)
                    ins = gp.tensor_tensor(dst3[:, dc, g * G:(g + 1) * G],
                                           xg[:, dc, :], bc[:, G:2 * G], ALU.add)
                x.s_g.inc(ins)

            @blk.vector
            def _(v, g=g):
                for dc in range(NDC - NPOOL):
                    v.tensor_tensor(xg[:, dc, :], xg[:, dc, :], bc[:, 0:G], ALU.mult)
                    ins = v.tensor_tensor(dst3[:, dc, g * G:(g + 1) * G],
                                          xg[:, dc, :], bc[:, G:2 * G], ALU.add)
                x.s_v.inc(ins)


def _emit_proj(x, blk, xln):
    nc = x.nc
    xln3 = _v3(xln, NDC, T)
    jobs = []
    for kind, ncc, scr in [("q", 8, x.q_scr), ("k", 8, x.k_scr),
                           ("s", 8, x.e_scr), ("v", 16, x.v_scr),
                           ("g", 16, x.g_scr)]:
        for cc in range(ncc):
            jobs.append((kind, cc, scr))

    with nc.sbuf_tensor("pj_slab", [128, 2 * NDC * 128], BF16) as slab, \
         nc.sbuf_tensor("pj_ev", [128, 2 * T], BF16) as ev, \
         nc.sbuf_tensor("pj_sf", [128, T], F32) as sf, \
         nc.psum_tensor("pj_psA", [128, T], F32) as psA, \
         nc.psum_tensor("pj_psB", [128, T], F32) as psB:
        pss = [psA, psB]
        pe_evt, ev_evt, wr_evt, ld_evt = [], [], [], []

        for j, (kind, cc, scr) in enumerate(jobs):
            ps = pss[j % 2]
            slab_b = slab[:, (j % 2) * NDC * 128:((j % 2) + 1) * NDC * 128]
            sl3 = slab_b.rearrange("p (a c) -> p a c", a=NDC)
            ev_b = ev[:, (j % 2) * T:((j % 2) + 1) * T]

            @blk.gpsimd
            def _(gp, j=j, slab_b=slab_b):
                if j >= 2:
                    gp.wait_ge(x.s_pe.h, pe_evt[j - 2])
                ins = gp.dma_start(out=slab_b, in_=x.wpj[j])
                x.s_gd.inc(ins, 16)
            ld_evt.append(x.s_gd.n)

            @blk.tensor
            def _(t, j=j, ps=ps, sl3=sl3):
                t.wait_ge(x.s_gd.h, ld_evt[j])
                if j >= 2:
                    t.wait_ge(x.s_v.h, ev_evt[j - 2][0])
                    t.wait_ge(x.s_s.h, ev_evt[j - 2][1])
                if j == 0:
                    t.wait_ge(x.s_v.h, x.s_v.n)
                    t.wait_ge(x.s_g.h, x.s_g.n)
                for g in range(NG):
                    for dc in range(NDC):
                        ins = t.matmul(ps[:, g * G:(g + 1) * G], sl3[:, dc, :],
                                       xln3[:, dc, g * G:(g + 1) * G],
                                       start=(dc == 0), stop=(dc == NDC - 1))
                x.s_pe.inc(ins)
            pe_evt.append(x.s_pe.n)

            if kind in ("k", "v"):
                @blk.vector
                def _(v, j=j, ps=ps, ev_b=ev_b):
                    v.wait_ge(x.s_pe.h, pe_evt[j])
                    if j >= 2:
                        v.wait_ge(x.s_wr.h, wr_evt[j - 2])
                    x.s_v.inc(v.tensor_copy(ev_b, ps[:, :]))
            elif kind == "g":
                @blk.scalar
                def _(sc, j=j, ps=ps, ev_b=ev_b):
                    sc.wait_ge(x.s_pe.h, pe_evt[j])
                    if j >= 2:
                        sc.wait_ge(x.s_wr.h, wr_evt[j - 2])
                    x.s_s.inc(sc.activation(ev_b, ps[:, :], AF.Silu,
                                            bias=0.0, scale=1.0))
            elif kind == "q":
                @blk.scalar
                def _(sc, j=j, ps=ps, ev_b=ev_b):
                    sc.wait_ge(x.s_pe.h, pe_evt[j])
                    if j >= 2:
                        sc.wait_ge(x.s_wr.h, wr_evt[j - 2])
                    x.s_s.inc(sc.activation(ev_b, ps[:, :], AF.Copy,
                                            bias=0.0, scale=0.125))
            else:
                @blk.vector
                def _(v, j=j, ps=ps):
                    v.wait_ge(x.s_pe.h, pe_evt[j])
                    v.wait_ge(x.s_s.h, x.s_s.n)
                    x.s_v.inc(v.tensor_scalar(sf[:, :], ps[:, :], 32.0, -32.0,
                                              ALU.min, ALU.max))

                @blk.scalar
                def _(sc, j=j, ev_b=ev_b):
                    sc.wait_ge(x.s_v.h, x.s_v.n)
                    if j >= 2:
                        sc.wait_ge(x.s_wr.h, wr_evt[j - 2])
                    x.s_s.inc(sc.activation(ev_b, sf[:, :], AF.Exp,
                                            bias=-32.0, scale=1.0))
            ev_evt.append((x.s_v.n, x.s_s.n))

            @blk.scalar
            def _(sc, j=j, cc=cc, scr=scr, ev_b=ev_b):
                sc.wait_ge(x.s_v.h, ev_evt[j][0])
                sc.wait_ge(x.s_s.h, ev_evt[j][1])
                if scr.shape[1] == 64:
                    x.s_wr.inc(sc.dma_start(out=scr[2 * cc], in_=ev_b[0:64, :]), 16)
                    x.s_wr.inc(sc.dma_start(out=scr[2 * cc + 1], in_=ev_b[64:128, :]), 16)
                else:
                    x.s_wr.inc(sc.dma_start(out=scr[cc], in_=ev_b[:, :]), 16)
            wr_evt.append(x.s_wr.n)
        x.pj_wr_evt = wr_evt


def _emit_states_exchange(x, blk):
    nc = x.nc
    with nc.sbuf_tensor("seg", [64, 4096], F32) as seg, \
         nc.sbuf_tensor("b2_etm", [128, 2 * H * M], BF16) as etm2, \
         nc.sbuf_tensor("b2_ktm", [128, 2 * H * DK], BF16) as ktm2, \
         nc.sbuf_tensor("b2_vtm", [128, 2 * H * DV], BF16) as vtm2, \
         nc.sbuf_tensor("b2_slab", [64, 2 * 4096], F32) as slabr, \
         nc.psum_tensor("b2_psF", [64, H * M], F32) as psF, \
         nc.psum_tensor("b2_psG1", [64, 8 * DV], F32) as psG1, \
         nc.psum_tensor("b2_psG2", [64, 8 * DV], F32) as psG2, \
         nc.psum_tensor("b2_psH", [1, H * M], F32) as psH:

        b2pe, b2ld = [], []
        for c in range(NCH):
            b = c % 2
            etm = etm2[:, b * H * M:(b + 1) * H * M]
            ktm = ktm2[:, b * H * DK:(b + 1) * H * DK]
            vtm = vtm2[:, b * H * DV:(b + 1) * H * DV]

            @blk.sync
            def _(sy, c=c, etm=etm, ktm=ktm, vtm=vtm):
                if c >= 2:
                    sy.wait_ge(x.s_pe.h, b2pe[c - 2])
                else:
                    sy.wait_ge(x.s_wr.h, x.pj_wr_evt[39])
                x.s_ld.inc(sy.dma_start(out=etm, in_=x.e_rows[:, c * C:(c + 1) * C],
                                        transpose=True), 16)
                x.s_ld.inc(sy.dma_start(out=ktm, in_=x.k_rows[:, c * C:(c + 1) * C],
                                        transpose=True), 16)
                x.s_ld.inc(sy.dma_start(out=vtm, in_=x.v_rows[:, c * C:(c + 1) * C],
                                        transpose=True), 16)
            b2ld.append(x.s_ld.n)

            @blk.tensor
            def _(t, c=c, etm=etm, ktm=ktm, vtm=vtm):
                t.wait_ge(x.s_ld.h, b2ld[c])
                if c == 0:
                    t.wait_ge(x.s_v.h, x.s_v.n)
                    t.wait_ge(x.s_s.h, x.s_s.n)
                st, sp = c == 0, c == NCH - 1
                t.matmul(psH[0:1, 0:512], x.onescol[:, :], etm[:, 0:512],
                         start=st, stop=sp, skip_group_check=True)
                t.matmul(psH[0:1, 512:1024], x.onescol[:, :], etm[:, 512:1024],
                         start=st, stop=sp, skip_group_check=True)
                for h in range(H):
                    t.matmul(psF[0:64, h * M:(h + 1) * M], ktm[:, h * DK:(h + 1) * DK],
                             etm[:, h * M:(h + 1) * M], start=st, stop=sp,
                             skip_group_check=True)
                for h in range(H):
                    ps = psG1 if h < 8 else psG2
                    ins = t.matmul(ps[0:64, (h % 8) * DV:((h % 8) + 1) * DV],
                                   etm[:, h * M:(h + 1) * M],
                                   vtm[:, h * DV:(h + 1) * DV], start=st, stop=sp,
                                   skip_group_check=True)
                x.s_pe.inc(ins)
            b2pe.append(x.s_pe.n)

        @blk.vector
        def _(v):
            v.wait_ge(x.s_pe.h, b2pe[NCH - 1])
            v.tensor_copy(seg[:, 0:H * M], psF[:, :])
            v.tensor_copy(seg[:, 1024:2048], psG1[:, :])
            v.tensor_copy(seg[:, 2048:3072], psG2[:, :])
            x.s_v.inc(v.tensor_copy(seg[0:1, 3072:4096], psH[:, :]))

        @blk.gpsimd
        def _(g):
            g.wait_ge(x.s_v.h, x.s_v.n)
            x.s_gd.inc(g.dma_start(out=x.seg_d, in_=seg[:, :]), 16)
            g.wait_ge(x.s_gd.h, x.s_gd.n)
            x.s_cc.inc(g.collective_compute(
                "AllGather", ALU.bypass,
                replica_groups=[list(range(NCORES))],
                ins=[x.seg_d.opt()],
                outs=[x.gath_d.opt()]))
            g.wait_ge(x.s_cc.h, x.s_cc.n)

        @blk.vector
        def _(v):
            x.s_v.inc(v.memset(x.acc[:, :], 0.0))

        for r in range(NCORES):
            sb = slabr[:, (r % 2) * 4096:((r % 2) + 1) * 4096]

            @blk.gpsimd
            def _(g, r=r, sb=sb):
                if r >= 2:
                    g.wait_ge(x.s_v.h, x.s_v.n)
                x.s_gd.inc(g.dma_start(out=sb, in_=x.gath_d[r * 64:(r + 1) * 64, :]), 16)

            @blk.vector
            def _(v, r=r, sb=sb):
                v.wait_ge(x.s_gd.h, x.s_gd.n)
                x.s_v.inc(v.scalar_tensor_tensor(x.acc[:, :], sb,
                                                 x.mask_sb[0:64, r:r + 1], x.acc[:, :],
                                                 ALU.mult, ALU.add))

        @blk.vector
        def _(v):
            v.tensor_copy(x.hkb2[:, 0:H * M], x.acc[0:64, 0:H * M])
            v.tensor_copy(x.hvb2[:, 0:H * DV], x.acc[0:64, 1024:1024 + H * DV])
            x.s_v.inc(v.tensor_copy(x.wsnap[:, 0:H * M], x.acc[0:1, 3072:3072 + H * M]))


def _emit_attn(x, blk):
    nc = x.nc
    from contextlib import ExitStack
    with ExitStack() as aes:
        def sb(name, shape, dt):
            return aes.enter_context(nc.sbuf_tensor(name, shape, dt))

        qc = sb("at_qc", [64, 2 * H * C], BF16)
        kc = sb("at_kc", [64, 2 * H * C], BF16)
        efc = sb("at_efc", [64, 2 * H * C], BF16)
        gc = sb("at_gc", [128, 2 * H * C], BF16)
        etm2 = sb("at_etm", [128, 2 * H * M], BF16)
        ktm2 = sb("at_ktm", [128, 2 * H * DK], BF16)
        vtm2 = sb("at_vtm", [128, 2 * H * DV], BF16)
        denr = sb("at_denr", [128, 2 * H * M], F32)
        sqkm = sb("at_sqk", [128, H * C], BF16)
        w2m = sb("at_w2", [128, H * C], BF16)
        t1 = sb("at_t1", [128, H * M], F32)
        eo = sb("at_eo", [128, H * M], F32)
        rs = sb("at_rs", [128, 16], F32)
        rs2 = sb("at_rs2", [128, 16], F32)
        pvt = sb("at_pvt", [128, H * M], BF16)
        pvf = sb("at_pvf", [64, H * C], BF16)
        ovsb = sb("at_ov", [128, H * DV], F32)
        sq_sb = sb("at_sq", [128, H * DV], BF16)
        rrow = sb("at_rrow", [1, H * C], F32)
        rg_sb = sb("at_rg", [128, H * DV], F32)
        on_sb = sb("at_on", [128, H * DV], F32)
        og2 = sb("at_og", [128, H * 1024], BF16)
        wos = sb("at_wos", [128, 2 * NDC * 128], BF16)
        xt = sb("at_xt", [128, 2 * G], F32)
        x2t = sb("at_x2", [128, 2 * G], F32)
        P1 = aes.enter_context(nc.psum_tensor("at_P1", [128, 2048], F32))
        P2 = aes.enter_context(nc.psum_tensor("at_P2", [128, 1024], F32))
        P3b = aes.enter_context(nc.psum_tensor("at_P3", [128, 2048], BF16))
        P3f = P3b[:, :].bitcast(F32)

        og3 = og2[:, :].rearrange("p (h t) -> p h t", h=H)

        ev = {}

        def rec(key, sem):
            ev[key] = (sem, sem.n)

        def wt(eng, *keys):
            for k in keys:
                if k in ev:
                    sm, n = ev[k]
                    eng.wait_ge(sm.h, n)

        def tile_loads(c):
            p = c % 2
            sl = slice(c * C, (c + 1) * C)

            @blk.sync
            def _(sy, c=c, p=p, sl=sl):
                wt(sy, f"tilerd{p}")
                x.s_ld.inc(sy.dma_start(out=etm2[:, p * H * M:(p + 1) * H * M],
                                        in_=x.e_rows[:, sl], transpose=True), 16)
                x.s_ld.inc(sy.dma_start(out=ktm2[:, p * H * DK:(p + 1) * H * DK],
                                        in_=x.k_rows[:, sl], transpose=True), 16)
                x.s_ld.inc(sy.dma_start(out=vtm2[:, p * H * DV:(p + 1) * H * DV],
                                        in_=x.v_rows[:, sl], transpose=True), 16)

            @blk.gpsimd
            def _(gp, c=c, p=p, sl=sl):
                wt(gp, f"qkrd{p}")
                x.s_ld.inc(gp.dma_start(
                    out=qc[:, p * H * C:(p + 1) * H * C].rearrange("p (h t) -> p h t", h=H),
                    in_=x.q_scr.rearrange("h m t -> m h t")[:, :, sl]), 16)
                x.s_ld.inc(gp.dma_start(
                    out=kc[:, p * H * C:(p + 1) * H * C].rearrange("p (h t) -> p h t", h=H),
                    in_=x.k_scr.rearrange("h m t -> m h t")[:, :, sl]), 16)
                x.s_ld.inc(gp.dma_start(
                    out=efc[:, p * H * C:(p + 1) * H * C].rearrange("p (h t) -> p h t", h=H),
                    in_=x.e_scr.rearrange("h m t -> m h t")[:, :, sl]), 16)
                x.s_ld.inc(gp.dma_start(
                    out=gc[:, p * H * C:(p + 1) * H * C].rearrange("p (h t) -> p h t", h=H),
                    in_=x.g_scr.rearrange("h m t -> m h t")[:, :, sl]), 16)
            rec(f"tileld{p}", x.s_ld)

        def emit_denom(c):
            p = c % 2

            @blk.tensor
            def _(t, c=c, p=p):
                wt(t, "P3", f"tileld{p}", f"wsnap{p}")
                for half in range(2):
                    sl = slice(half * 512, (half + 1) * 512)
                    t.matmul(P3f[:, sl], x.m1u[:, :],
                             etm2[:, p * H * M + half * 512: p * H * M + half * 512 + 512],
                             start=True, stop=False)
                    ins = t.matmul(P3f[:, sl], x.onesrow[:, :],
                                   x.wsnap[:, p * H * M + half * 512: p * H * M + half * 512 + 512],
                                   start=False, stop=True)
                x.s_pe.inc(ins)
            rec("CPE", x.s_pe)

            @blk.scalar
            def _(sc, c=c, p=p):
                wt(sc, "CPE", "t1rd")
                sc.activation(t1[:, :], P3f[:, :], AF.Ln, bias=0.0, scale=1.0)
                x.s_s.inc(sc.activation(denr[:, p * H * M:(p + 1) * H * M],
                                        t1[:, :], AF.Exp, bias=0.0, scale=-1.0))
            rec("P3", x.s_s)
            rec(f"denr{p}", x.s_s)
            rec("t1wr", x.s_s)
            if _KDBG and c < 2:
                @blk.scalar
                def _(sc, c=c, p=p):
                    sc.wait_ge(x.s_s.h, x.s_s.n)
                    x.s_wr.inc(sc.dma_start(out=x.dbg_denr[c],
                                            in_=denr[:, p * H * M:(p + 1) * H * M]), 16)

        tile_loads(0)
        emit_denom(0)
        tile_loads(1)

        for c in range(NCH):
            p = c % 2
            np_ = 1 - p
            etm = etm2[:, p * H * M:(p + 1) * H * M]
            ktm = ktm2[:, p * H * DK:(p + 1) * H * DK]
            vtm = vtm2[:, p * H * DV:(p + 1) * H * DV]
            q3 = qc[:, p * H * C:(p + 1) * H * C]
            k3 = kc[:, p * H * C:(p + 1) * H * C]
            ef3 = efc[:, p * H * C:(p + 1) * H * C]
            g3 = gc[:, p * H * C:(p + 1) * H * C]
            dnr = denr[:, p * H * M:(p + 1) * H * M]
            acc_hk = x.acc[0:64, 0:H * M]
            acc_hv1 = x.acc[0:64, 1024:1024 + 8 * DV]
            acc_hv2 = x.acc[0:64, 1024 + 8 * DV:1024 + 16 * DV]
            cst = (c % 8) * C

            @blk.tensor
            def _(t, c=c, p=p, etm=etm, ktm=ktm, vtm=vtm):
                wt(t, f"tileld{p}", "P1free", "P2free", "P3", f"shadrd{1 - p}")
                for h in range(H):
                    t.matmul(P2[0:64, h * M:(h + 1) * M], ktm[:, h * DK:(h + 1) * DK],
                             etm[:, h * M:(h + 1) * M], start=True, stop=True)
                for h in range(H):
                    ps = P1[0:64, 0:1024] if h < 8 else P1[0:64, 1024:2048]
                    t.matmul(ps[:, (h % 8) * DV:((h % 8) + 1) * DV],
                             etm[:, h * M:(h + 1) * M],
                             vtm[:, h * DV:(h + 1) * DV], start=True, stop=True)
                t.matmul(P3f[0:1, 0:512], x.onescol[:, :], etm[:, 0:512],
                         start=True, stop=True)
                ins = t.matmul(P3f[0:1, 512:1024], x.onescol[:, :], etm[:, 512:1024],
                               start=True, stop=True)
                x.s_pe.inc(ins)
            rec("UPE", x.s_pe)

            @blk.vector
            def _(v, c=c, p=p, np_=np_):
                wt(v, "UPE")
                v.tensor_tensor(x.acc[0:64, 0:H * M], x.acc[0:64, 0:H * M],
                                P2[0:64, :], ALU.add)
                v.tensor_tensor(x.acc[0:64, 1024:1024 + 8 * DV],
                                x.acc[0:64, 1024:1024 + 8 * DV],
                                P1[0:64, 0:1024], ALU.add)
                v.tensor_tensor(x.acc[0:64, 1024 + 8 * DV:1024 + 16 * DV],
                                x.acc[0:64, 1024 + 8 * DV:1024 + 16 * DV],
                                P1[0:64, 1024:2048], ALU.add)
                v.tensor_tensor(x.wsnap[:, np_ * H * M:(np_ + 1) * H * M],
                                x.wsnap[:, p * H * M:(p + 1) * H * M],
                                P3f[0:1, 0:H * M], ALU.add)
                v.tensor_copy(x.hkb2[:, np_ * H * M:(np_ + 1) * H * M],
                              x.acc[0:64, 0:H * M])
                ins = v.tensor_copy(x.hvb2[:, np_ * H * DV:(np_ + 1) * H * DV],
                                    x.acc[0:64, 1024:1024 + H * DV])
                x.s_v.inc(ins)
            rec("UV", x.s_v)
            rec(f"wsnap{np_}", x.s_v)
            rec("P2free_afterF", x.s_v)

            @blk.tensor
            def _(t, c=c, p=p, k3=k3, q3=q3):
                wt(t, "UV", f"qkld{p}")
                for h in range(H):
                    ins = t.matmul(P1[:, h * C:(h + 1) * C],
                                   k3[:, h * C:(h + 1) * C], q3[:, h * C:(h + 1) * C],
                                   start=True, stop=True)
                x.s_pe.inc(ins)
            rec("APE", x.s_pe)

            @blk.vector
            def _(v, c=c):
                wt(v, "APE", "sqkmrd")
                v.tensor_tensor(sqkm[:, 0:1024], P1[:, 0:1024], x.m1u8[:, :], ALU.mult)
                ins = v.tensor_tensor(sqkm[:, 1024:2048], P1[:, 1024:2048],
                                      x.m1u8[:, :], ALU.mult)
                x.s_v.inc(ins)
            rec("sqkmwr", x.s_v)
            rec("P1free", x.s_v)

            @blk.tensor
            def _(t, c=c, p=p, etm=etm, q3=q3):
                wt(t, "sqkmwr", "P2free_afterF")
                for h in range(H):
                    t.matmul(P2[:, h * M:(h + 1) * M], sqkm[:, h * C:(h + 1) * C],
                             etm[:, h * M:(h + 1) * M], start=True, stop=False)
                    ins = t.matmul(P2[:, h * M:(h + 1) * M], q3[:, h * C:(h + 1) * C],
                                   x.hkb2[:, p * H * M + h * M: p * H * M + (h + 1) * M],
                                   start=False, stop=True)
                x.s_pe.inc(ins)
            rec("BPE", x.s_pe)
            rec("sqkmrd", x.s_pe)

            @blk.vector
            def _(v, c=c, p=p, dnr=dnr):
                wt(v, "BPE", f"denr{p}", "t1wr")
                x.s_v.inc(v.tensor_tensor(t1[:, :], P2[:, :], dnr, ALU.mult))
            rec("okV", x.s_v)

            @blk.scalar
            def _(sc, c=c):
                wt(sc, "okV", "eord")
                x.s_s.inc(sc.activation(eo[:, :], t1[:, :], AF.Exp, bias=0.0, scale=1.0))
            rec("eoS", x.s_s)
            rec("t1rd", x.s_s)

            @blk.vector
            def _(v, c=c, p=p, dnr=dnr):
                wt(v, "eoS", "pvtrd")
                v.tensor_reduce(rs2[:, :], eo[:, :].rearrange("p (i m) -> p i m", i=H),
                                AX, ALU.add)
                v.reciprocal(rs[:, 0:8], rs2[:, 0:8])
                v.reciprocal(rs[:, 8:16], rs2[:, 8:16])
                v.tensor_tensor(t1[:, :], eo[:, :], dnr, ALU.mult)
                ins = v.tensor_tensor(
                    pvt[:, :].rearrange("p (i m) -> p i m", i=H),
                    t1[:, :].rearrange("p (i m) -> p i m", i=H),
                    rs[:, :].rearrange("p (i o) -> p i o", o=1).broadcast_to([128, H, M]),
                    ALU.mult)
                x.s_v.inc(ins)
            rec("pvtwr", x.s_v)
            rec("P2free", x.s_v)
            rec("eord", x.s_v)

            if _KDBG and c == 0:
                @blk.scalar
                def _(sc):
                    sc.wait_ge(x.s_v.h, x.s_v.n)
                    x.s_wr.inc(sc.dma_start(out=x.dbg_pvt, in_=pvt[:, :]), 16)
                    x.s_wr.inc(sc.dma_start(out=x.dbg_rs, in_=rs[:, :]), 16)

            @blk.tensor
            def _(t, c=c):
                wt(t, "pvtwr", "P3")
                for i in range(H):
                    ins = t.transpose(P3b[0:64, i * C:(i + 1) * C],
                                      pvt[:, i * M:(i + 1) * M], x.ident[:, :])
                x.s_pe.inc(ins)
            rec("TPE", x.s_pe)
            rec("pvtrd", x.s_pe)

            @blk.vector
            def _(v, c=c):
                wt(v, "TPE", "pvfrd")
                x.s_v.inc(v.tensor_copy(pvf[:, :], P3b[0:64, :]))
            rec("pvfwr", x.s_v)
            rec("P3", x.s_v)

            if _KDBG and c == 0:
                @blk.scalar
                def _(sc):
                    sc.wait_ge(x.s_v.h, x.s_v.n)
                    x.s_wr.inc(sc.dma_start(out=x.dbg_pvf, in_=pvf[:, :]), 16)

            @blk.tensor
            def _(t, c=c, ef3=ef3):
                wt(t, "pvfwr", "P1free")
                for h in range(H):
                    ins = t.matmul(P1[:, h * C:(h + 1) * C], ef3[:, h * C:(h + 1) * C],
                                   pvf[:, h * C:(h + 1) * C], start=True, stop=True)
                x.s_pe.inc(ins)
            rec("DPE", x.s_pe)

            @blk.vector
            def _(v, c=c):
                wt(v, "DPE", "w2mrd")
                v.tensor_tensor(w2m[:, 0:1024], P1[:, 0:1024], x.m1u8[:, :], ALU.mult)
                ins = v.tensor_tensor(w2m[:, 1024:2048], P1[:, 1024:2048],
                                      x.m1u8[:, :], ALU.mult)
                x.s_v.inc(ins)
            rec("w2mwr", x.s_v)
            rec("P1free", x.s_v)

            if _KDBG and c == 0:
                @blk.scalar
                def _(sc):
                    sc.wait_ge(x.s_v.h, x.s_v.n)
                    x.s_wr.inc(sc.dma_start(out=x.dbg_w2, in_=w2m[:, :]), 16)

            @blk.tensor
            def _(t, c=c, p=p, vtm=vtm):
                wt(t, "w2mwr", "P1free", "ovrd")
                for h in range(H):
                    t.matmul(P1[:, h * C:(h + 1) * C], vtm[:, h * DV:(h + 1) * DV],
                             w2m[:, h * C:(h + 1) * C], start=True, stop=False)
                    ins = t.matmul(P1[:, h * C:(h + 1) * C],
                                   x.hvb2[:, p * H * DV + h * DV: p * H * DV + (h + 1) * DV],
                                   pvf[:, h * C:(h + 1) * C], start=False, stop=True)
                x.s_pe.inc(ins)
            rec("EPE", x.s_pe)
            rec("w2mrd", x.s_pe)
            rec("pvfrd", x.s_pe)
            rec(f"tilerd{p}", x.s_pe)
            rec(f"shadrd{p}", x.s_pe)

            @blk.vector
            def _(v, c=c):
                wt(v, "EPE")
                x.s_v.inc(v.tensor_copy(ovsb[:, :], P1[:, :]))
            rec("ovwr", x.s_v)
            rec("P1free", x.s_v)

            if _KDBG and c == 0:
                @blk.scalar
                def _(sc):
                    sc.wait_ge(x.s_v.h, x.s_v.n)
                    x.s_wr.inc(sc.dma_start(out=x.dbg_ov, in_=ovsb[:, :]), 16)

            @blk.scalar
            def _(sc, c=c):
                wt(sc, "ovwr", "sqrd")
                x.s_s.inc(sc.activation(sq_sb[:, :], ovsb[:, :], AF.Square,
                                        bias=0.0, scale=1.0))
            rec("sqwr", x.s_s)

            @blk.tensor
            def _(t, c=c):
                wt(t, "sqwr", "P2free", "P3")
                for i in range(8):
                    t.matmul(P2[0:1, i * C:(i + 1) * C], x.onescol[:, :],
                             sq_sb[:, i * DV:(i + 1) * DV], start=True, stop=True)
                for i in range(8):
                    ins = t.matmul(P3f[0:1, i * 128:(i + 1) * 128], x.onescol[:, :],
                                   sq_sb[:, (8 + i) * DV:(9 + i) * DV],
                                   start=True, stop=True)
                x.s_pe.inc(ins)
            rec("RPE", x.s_pe)
            rec("sqrd", x.s_pe)

            @blk.scalar
            def _(sc, c=c):
                wt(sc, "RPE")
                sc.activation(rrow[:, 0:1024], P2[0:1, :], AF.Ln, bias=1e-5, scale=1.0 / DV)
                sc.activation(rrow[:, 1024:2048], P3f[0:1, :], AF.Ln, bias=1e-5, scale=1.0 / DV)
                x.s_s.inc(sc.activation(rrow[:, :], rrow[:, :], AF.Exp, bias=0.0, scale=-0.5))
            rec("rrowS", x.s_s)
            rec("P2free", x.s_s)
            rec("P3", x.s_s)

            @blk.tensor
            def _(t, c=c):
                wt(t, "rrowS", "P1free")
                for i in range(H):
                    ins = t.matmul(P1[:, i * C:(i + 1) * C], x.onesrow[:, :],
                                   rrow[:, i * C:(i + 1) * C], start=True, stop=True)
                x.s_pe.inc(ins)
            rec("RBPE", x.s_pe)

            @blk.vector
            def _(v, c=c, g3=g3, cst=cst):
                wt(v, "RBPE", "rgrd", "ogrd")
                v.tensor_scalar(rg_sb[:, :], P1[:, :], x.gnw_sb[:, 0:1], None, ALU.mult)
                v.tensor_tensor(on_sb[:, :], ovsb[:, :], rg_sb[:, :], ALU.mult)
                ins = v.tensor_tensor(
                    og3[:, :, cst:cst + C],
                    on_sb[:, :].rearrange("p (h t) -> p h t", h=H),
                    g3.rearrange("p (h t) -> p h t", h=H), ALU.mult)
                x.s_v.inc(ins)
            rec("ogV", x.s_v)
            rec("ovrd", x.s_v)
            rec("rgrd", x.s_v)
            rec("P1free", x.s_v)
            rec(f"qkld{p}", x.s_v)
            rec(f"qkrd{p}", x.s_v)

            if _KDBG and c == 7:
                @blk.scalar
                def _(sc):
                    sc.wait_ge(x.s_v.h, x.s_v.n)
                    x.s_wr.inc(sc.dma_start(out=x.dbg_og, in_=og2[:, :]), 16)

            if c + 2 < NCH:
                tile_loads(c + 2)

            if c + 1 < NCH:
                emit_denom(c + 1)

            if c % 8 == 7:
                _emit_wo(x, blk, c // 8, og3, wos, xt, x2t, P1, P2, rec, wt)


def _emit_wo(x, blk, sup, og3, wos, xt, x2t, P1, P2, rec, wt):
    nc = x.nc
    pe_evt = []
    for dc in range(NDC):
        bb = dc % 2
        wslab = wos[:, bb * NDC * 128:(bb + 1) * NDC * 128]
        slw = wslab.rearrange("p (a c) -> p a c", a=NDC)

        @blk.gpsimd
        def _(gp, dc=dc, wslab=wslab):
            if dc >= 2:
                gp.wait_ge(x.s_pe.h, pe_evt[dc - 2])
            else:
                gp.wait_ge(x.s_pe.h, x.s_pe.n)
            ins = gp.dma_start(out=wslab, in_=x.wo_t[dc])
            x.s_gd.inc(ins, 16)

        for g2 in range(2):
            tok0 = sup * 1024 + g2 * G
            xtb = xt[:, bb * G:(bb + 1) * G]
            x2b = x2t[:, bb * G:(bb + 1) * G]
            psl = P1[:, (bb * 2 + g2) * G % 2048:((bb * 2 + g2) * G % 2048) + G]

            @blk.sync
            def _(sy, xtb=xtb, dc=dc, tok0=tok0):
                sy.wait_ge(x.s_v.h, x.s_v.n)
                x.s_ld.inc(sy.dma_start(out=xtb,
                                        in_=x.x_fm[dc * 128:(dc + 1) * 128,
                                                   tok0:tok0 + G]), 16)

            @blk.tensor
            def _(t, dc=dc, g2=g2, psl=psl, slw=slw):
                t.wait_ge(x.s_gd.h, x.s_gd.n)
                if dc < 2 and g2 == 0:
                    wt(t, "P1free", "ogV")
                t.wait_ge(x.s_v.h, x.s_v.n)
                for vd in range(H):
                    ins = t.matmul(psl, slw[:, vd, :],
                                   og3[:, vd, g2 * G:(g2 + 1) * G],
                                   start=(vd == 0), stop=(vd == H - 1))
                x.s_pe.inc(ins)
            if g2 == 1:
                pe_evt.append(x.s_pe.n)

            @blk.vector
            def _(v, dc=dc, g2=g2, psl=psl, xtb=xtb, x2b=x2b):
                v.wait_ge(x.s_pe.h, x.s_pe.n)
                v.wait_ge(x.s_ld.h, x.s_ld.n)
                v.wait_ge(x.s_wr.h, x.s_wr.n)
                x.s_v.inc(v.tensor_tensor(x2b, psl, xtb, ALU.add))

            @blk.scalar
            def _(sc, dc=dc, x2b=x2b, tok0=tok0):
                sc.wait_ge(x.s_v.h, x.s_v.n)
                x.s_wr.inc(sc.dma_start(out=x.x2_scr[dc, :, tok0:tok0 + G],
                                        in_=x2b), 16)
    rec("P1free", x.s_v)
    rec("ogrd", x.s_pe)


def _emit_ffn_fused(x, blk):
    nc = x.nc
    GT = 1024
    with nc.sbuf_tensor("ff_x", [128, NDC * GT], BF16) as xg2, \
         nc.sbuf_tensor("ff_hid", [128, NHC * GT], BF16) as hid2, \
         nc.sbuf_tensor("ff_sg", [128, 2 * NDC * 128], BF16) as slg, \
         nc.sbuf_tensor("ff_su", [128, 2 * NDC * 128], BF16) as slu, \
         nc.sbuf_tensor("ff_wd", [128, 2 * NHC * 128], BF16) as wds, \
         nc.sbuf_tensor("ff_sgu", [128, 2 * NDC * 128], U8) as sgu, \
         nc.sbuf_tensor("ff_suu", [128, 2 * NDC * 128], U8) as suu, \
         nc.sbuf_tensor("ff_wdu", [128, 2 * NHC * 128], U8) as wdu, \
         nc.sbuf_tensor("ff_act", [128, GT], F32) as actt, \
         nc.sbuf_tensor("ff_x2", [128, 2 * GT], F32) as x2t, \
         nc.sbuf_tensor("ff_o", [128, 2 * GT], F32) as ot, \
         nc.psum_tensor("ff_PA", [128, 2048], F32) as PA, \
         nc.psum_tensor("ff_PB", [128, 2048], F32) as PB:
        xg3 = xg2[:, :].rearrange("p (a t) -> p a t", a=NDC)
        hid3 = hid2[:, :].rearrange("p (h t) -> p h t", h=NHC)
        ev = {}

        def rec(key, sem):
            ev[key] = (sem, sem.n)

        def wt(eng, *keys):
            for k in keys:
                if k in ev:
                    sm, n = ev[k]
                    eng.wait_ge(sm.h, n)

        for grp in range(2):
            t0 = grp * GT

            @blk.sync
            def _(sy, grp=grp, t0=t0):
                wt(sy, "xgrd")
                sy.wait_ge(x.s_wr.h, x.s_wr.n)
                x.s_ld.inc(sy.dma_start(
                    out=xg3[:, :, :],
                    in_=x.xln2_scr.rearrange("a p t -> p a t")[:, :, t0:t0 + GT]), 16)
            rec("xgld", x.s_ld)

            for hc in range(NHC):
                b = hc % 2
                sgb = slg[:, b * NDC * 128:(b + 1) * NDC * 128]
                sub = slu[:, b * NDC * 128:(b + 1) * NDC * 128]
                sgu_b = sgu[:, b * NDC * 128:(b + 1) * NDC * 128]
                suu_b = suu[:, b * NDC * 128:(b + 1) * NDC * 128]
                pg = PA[:, b * 1024:(b + 1) * 1024]
                pu = PB[:, b * 1024:(b + 1) * 1024]

                @blk.sync
                def _(sy, hc=hc, sgu_b=sgu_b, suu_b=suu_b):
                    wt(sy, f"slabu{hc % 2}")
                    x.s_ld.inc(sy.dma_start(out=sgu_b, in_=x.wg_t[hc]), 16)
                    x.s_ld.inc(sy.dma_start(out=suu_b, in_=x.wu_t[hc]), 16)
                rec(f"slabld{hc % 2}", x.s_ld)

                @blk.vector
                def _(v, hc=hc, sgb=sgb, sub=sub, sgu_b=sgu_b, suu_b=suu_b):
                    wt(v, f"slabld{hc % 2}", f"slabrd{hc % 2}")
                    v.tensor_scalar(sgb, sgu_b, -128.0, None, ALU.add)
                    x.s_v.inc(v.tensor_scalar(sub, suu_b, -128.0, None, ALU.add))
                rec(f"deq{hc % 2}", x.s_v)
                rec(f"slabu{hc % 2}", x.s_v)

                sg3 = sgb.rearrange("p (a c) -> p a c", a=NDC)
                su3 = sub.rearrange("p (a c) -> p a c", a=NDC)

                @blk.tensor
                def _(t, hc=hc, sg3=sg3, su3=su3, pg=pg, pu=pu):
                    wt(t, f"deq{hc % 2}", "xgld", f"gu{hc % 2}")
                    if hc < 2:
                        wt(t, "pdrd0", "pdrd1")
                    for g2 in range(2):
                        for dc in range(NDC):
                            t.matmul(pg[:, g2 * G:(g2 + 1) * G], sg3[:, dc, :],
                                     xg3[:, dc, g2 * G:(g2 + 1) * G],
                                     start=(dc == 0), stop=(dc == NDC - 1))
                    for g2 in range(2):
                        for dc in range(NDC):
                            ins = t.matmul(pu[:, g2 * G:(g2 + 1) * G], su3[:, dc, :],
                                           xg3[:, dc, g2 * G:(g2 + 1) * G],
                                           start=(dc == 0), stop=(dc == NDC - 1))
                    x.s_pe.inc(ins)
                rec(f"slabrd{hc % 2}", x.s_pe)
                rec(f"gupe{hc}", x.s_pe)
                if hc == NHC - 1:
                    rec("xgrd", x.s_pe)

                @blk.scalar
                def _(sc, hc=hc, pg=pg):
                    wt(sc, f"gupe{hc}", "actrd")
                    x.s_s.inc(sc.activation(actt[:, :], pg, AF.Silu, bias=0.0,
                                            scale=x.sg_sc[:, hc:hc + 1]))
                rec(f"sil{hc}", x.s_s)

                @blk.vector
                def _(v, hc=hc, pu=pu):
                    wt(v, f"sil{hc}", "hidrd")
                    x.s_v.inc(v.scalar_tensor_tensor(hid3[:, hc, :], pu,
                                                     x.su_sc[:, hc:hc + 1],
                                                     actt[:, :], ALU.mult, ALU.mult))
                rec(f"gu{hc % 2}", x.s_v)
                rec("actrd", x.s_v)
                rec(f"hid{hc}", x.s_v)

            for dc in range(NDC):
                b = dc % 2
                wdb = wds[:, b * NHC * 128:(b + 1) * NHC * 128]
                wdu_b = wdu[:, b * NHC * 128:(b + 1) * NHC * 128]
                pd = PA[:, b * 1024:(b + 1) * 1024]
                x2b = x2t[:, b * GT:(b + 1) * GT]
                otb = ot[:, b * GT:(b + 1) * GT]

                @blk.sync
                def _(sy, dc=dc, wdu_b=wdu_b):
                    wt(sy, f"wdu{dc % 2}")
                    x.s_ld.inc(sy.dma_start(out=wdu_b, in_=x.wd_t[dc]), 16)
                rec(f"wdld{dc % 2}", x.s_ld)

                @blk.sync
                def _(sy, dc=dc, t0=t0, x2b=x2b):
                    wt(sy, f"x2rd{dc % 2}")
                    x.s_ld.inc(sy.dma_start(out=x2b,
                                            in_=x.x2_scr[dc, :, t0:t0 + GT]), 16)
                rec(f"x2ld{dc % 2}", x.s_ld)

                @blk.vector
                def _(v, dc=dc, wdb=wdb, wdu_b=wdu_b):
                    wt(v, f"wdld{dc % 2}", f"wdrd{dc % 2}")
                    x.s_v.inc(v.tensor_scalar(wdb, wdu_b, -128.0, None, ALU.add))
                rec(f"wdeq{dc % 2}", x.s_v)
                rec(f"wdu{dc % 2}", x.s_v)

                wd3 = wdb.rearrange("p (a c) -> p a c", a=NHC)

                @blk.tensor
                def _(t, dc=dc, wd3=wd3, pd=pd):
                    wt(t, f"wdeq{dc % 2}", f"pdrd{dc % 2}")
                    if dc < 2:
                        wt(t, f"sil{NHC - 2}", f"sil{NHC - 1}")
                    for g2 in range(2):
                        for hc in range(NHC):
                            ins = t.matmul(pd[:, g2 * G:(g2 + 1) * G], wd3[:, hc, :],
                                           hid3[:, hc, g2 * G:(g2 + 1) * G],
                                           start=(hc == 0), stop=(hc == NHC - 1))
                    x.s_pe.inc(ins)
                rec(f"wdrd{dc % 2}", x.s_pe)
                rec(f"dpe{dc}", x.s_pe)
                if dc == NDC - 1:
                    rec("hidrd", x.s_pe)

                @blk.vector
                def _(v, dc=dc, pd=pd, x2b=x2b, otb=otb):
                    wt(v, f"dpe{dc}", f"x2ld{dc % 2}", f"otwr{dc % 2}")
                    x.s_v.inc(v.scalar_tensor_tensor(otb, pd, x.sd_sc[:, dc:dc + 1],
                                                     x2b, ALU.mult, ALU.add))
                rec(f"otV{dc % 2}", x.s_v)
                rec(f"x2rd{dc % 2}", x.s_v)
                rec(f"pdrd{dc % 2}", x.s_v)

                @blk.scalar
                def _(sc, dc=dc, t0=t0, otb=otb):
                    wt(sc, f"otV{dc % 2}")
                    x.s_wr.inc(sc.dma_start(
                        out=x.out_fm[dc * 128:(dc + 1) * 128, t0:t0 + GT],
                        in_=otb), 16)
                rec(f"otwr{dc % 2}", x.s_wr)


_CACHE = {}


def _tile_slabs(w2d):
    r, c = w2d.shape
    return np.ascontiguousarray(
        w2d.reshape(r // 128, 128, c // 128, 128).transpose(2, 1, 0, 3)
        .reshape(c // 128, 128, (r // 128) * 128))


def _lowbyte(a_int32):
    import sys
    a = np.ascontiguousarray(np.asarray(a_int32))
    return a.view(np.uint8).reshape(a.shape + (4,))[..., 0 if sys.byteorder == "little" else 3]


def _host_inputs(inputs):
    x = np.asarray(inputs["L0_attn_hidden_states"])
    wpj = np.concatenate([_tile_slabs(np.asarray(inputs[k]))
                          for k in ("L0_attn_Wq", "L0_attn_Wk", "L0_attn_Ws",
                                    "L0_attn_Wv", "L0_attn_Wg")], axis=0)
    shared = dict(
        wpj=wpj,
        wo_t=_tile_slabs(np.asarray(inputs["L0_attn_Wo"])),
        wg_t=_tile_slabs(np.ascontiguousarray(_lowbyte(inputs["L0_ffn_wq_gate"]).T)),
        wu_t=_tile_slabs(np.ascontiguousarray(_lowbyte(inputs["L0_ffn_wq_up"]).T)),
        wd_t=_tile_slabs(np.ascontiguousarray(_lowbyte(inputs["L0_ffn_wq_down"]).T)),
        sgate=np.ascontiguousarray(np.asarray(inputs["L0_ffn_s_gate"]).reshape(NHC, 128).T),
        sup=np.ascontiguousarray(np.asarray(inputs["L0_ffn_s_up"]).reshape(NHC, 128).T),
        sdown=np.ascontiguousarray(np.asarray(inputs["L0_ffn_s_down"]).reshape(NDC, 128).T),
        gnw=np.ascontiguousarray(np.asarray(inputs["L0_attn_g_norm_weight"]).reshape(128, 1)),
        m1u=np.ascontiguousarray(np.triu(np.ones((128, 128))).astype(_BF)),
        m1u8=np.ascontiguousarray(np.tile(np.triu(np.ones((128, 128))).astype(_BF), (1, 8))),
        ident=np.eye(128).astype(_BF),
        cst=np.ascontiguousarray(np.tile(np.array([CONSTS], np.float32), (128, 1))),
        onescol=np.ones((128, 1), dtype=_BF),
        onesrow=np.ones((1, 128), dtype=np.float32),
    )
    per_core = []
    for c in range(NCORES):
        b, qt = divmod(c, 4)
        mk = np.zeros((128, NCORES), np.float32)
        for r in range(b * 4, c):
            mk[:, r] = 1.0
        m = dict(shared)
        m["x_fm"] = np.ascontiguousarray(x[b, qt * T:(qt + 1) * T, :].T)
        m["maskpre"] = mk
        per_core.append(m)
    return per_core


def kernel(**inputs):
    if "nc" not in _CACHE:
        _CACHE["nc"] = build()
    nc = _CACHE["nc"]
    in_maps = _host_inputs(inputs)
    res = run_bass_kernel_spmd(nc, in_maps, core_ids=list(range(NCORES)))
    out = np.empty((2, 8192, D), np.float32)
    for c in range(NCORES):
        b, qt = divmod(c, 4)
        out[b, qt * T:(qt + 1) * T, :] = res.results[c]["out_fm"].T
    return out


# version 08
# speedup vs baseline: 1.0381x; 1.0136x over previous
import os
import numpy as np
import ml_dtypes
import concourse.bass as bass
import concourse.mybir as mybir
from concourse.bass_utils import run_bass_kernel_spmd

F32 = mybir.dt.float32
BF16 = mybir.dt.bfloat16
I32 = mybir.dt.int32
U8 = mybir.dt.uint8
AF = mybir.ActivationFunctionType
ALU = mybir.AluOpType
AX = mybir.AxisListType.X

D = 2048
T = 2048
H = 16
DK = 64
DV = 128
M = 64
C = 128
NCH = T // C
G = 512
NG = T // G
HID = 5632
NHC = HID // 128
NDC = D // 128
NCORES = 8

CONSTS = [0.0, 1.0, -1.0, -0.5, -32.0, 1e-5, 1.0 / 128.0, 0.125, 1.0 / 2048.0]
_BF = ml_dtypes.bfloat16

_KPHASE = os.environ.get("KPHASE", "all")
_KDBG = os.environ.get("KDBG", "0") == "1"
_PHASES = ["b1", "b2", "b3", "b4", "b5", "b6"]


def _upto(ph):
    return _KPHASE == "all" or _PHASES.index(ph) <= _PHASES.index(_KPHASE)


class Sem:
    def __init__(self, h):
        self.h = h
        self.n = 0

    def inc(self, instr, amt=1):
        instr.then_inc(self.h, amt)
        self.n += amt
        return instr


class Ctx:
    pass


def build():
    nc = bass.Bass("TRN2", target_bir_lowering=False)
    x = Ctx()
    x.nc = nc
    scr_kind = "ExternalOutput" if _KDBG else "Internal"

    x.x_fm = nc.dram_tensor("x_fm", [D, T], F32, kind="ExternalInput").ap()
    x.wpj = nc.dram_tensor("wpj", [56, 128, NDC * 128], F32, kind="ExternalInput").ap()
    x.wo_t = nc.dram_tensor("wo_t", [NDC, 128, NDC * 128], F32, kind="ExternalInput").ap()
    x.wg_t = nc.dram_tensor("wg_t", [NHC, 128, NDC * 128], U8, kind="ExternalInput").ap()
    x.wu_t = nc.dram_tensor("wu_t", [NHC, 128, NDC * 128], U8, kind="ExternalInput").ap()
    x.wd_t = nc.dram_tensor("wd_t", [NDC, 128, NHC * 128], U8, kind="ExternalInput").ap()
    x.sgate = nc.dram_tensor("sgate", [128, NHC], F32, kind="ExternalInput").ap()
    x.sup = nc.dram_tensor("sup", [128, NHC], F32, kind="ExternalInput").ap()
    x.sdown = nc.dram_tensor("sdown", [128, NDC], F32, kind="ExternalInput").ap()
    x.gnw = nc.dram_tensor("gnw", [128, 1], F32, kind="ExternalInput").ap()
    x.m1u_i = nc.dram_tensor("m1u", [128, 128], BF16, kind="ExternalInput").ap()
    x.m1u8_i = nc.dram_tensor("m1u8", [128, 8 * 128], BF16, kind="ExternalInput").ap()
    x.ident_i = nc.dram_tensor("ident", [128, 128], BF16, kind="ExternalInput").ap()
    x.cst_i = nc.dram_tensor("cst", [128, len(CONSTS)], F32, kind="ExternalInput").ap()
    x.mask_i = nc.dram_tensor("maskpre", [128, NCORES], F32, kind="ExternalInput").ap()
    x.onescol_i = nc.dram_tensor("onescol", [128, 1], BF16, kind="ExternalInput").ap()
    x.onesrow_i = nc.dram_tensor("onesrow", [1, 128], F32, kind="ExternalInput").ap()

    x.out_fm = nc.dram_tensor("out_fm", [D, T], F32, kind="ExternalOutput").ap()

    x.q_scr = nc.dram_tensor("q_scr", [H, DK, T], BF16, kind=scr_kind).ap()
    x.k_scr = nc.dram_tensor("k_scr", [H, DK, T], BF16, kind=scr_kind).ap()
    x.e_scr = nc.dram_tensor("e_scr", [H, M, T], BF16, kind=scr_kind).ap()
    x.v_scr = nc.dram_tensor("v_scr", [H, DV, T], BF16, kind=scr_kind).ap()
    x.g_scr = nc.dram_tensor("g_scr", [H, DV, T], BF16, kind=scr_kind).ap()
    x.x2_scr = nc.dram_tensor("x2_scr", [NDC, 128, T], F32, kind=scr_kind).ap()
    x.hid_scr = nc.dram_tensor("hid_scr", [NHC, 128, T], BF16, kind=scr_kind).ap()
    x.xln2_scr = nc.dram_tensor("xln2_scr", [NDC, 128, T], BF16, kind=scr_kind).ap()
    x.seg_d = nc.dram_tensor("seg_d", [64, 4096], F32).ap()
    if _KDBG:
        x.dbg_denr = nc.dram_tensor("dbg_denr", [2, 128, 1024], F32, kind="ExternalOutput").ap()
        x.dbg_og = nc.dram_tensor("dbg_og", [128, H * 1024], mybir.dt.bfloat16, kind="ExternalOutput").ap()
        x.dbg_ov = nc.dram_tensor("dbg_ov", [128, H * DV], F32, kind="ExternalOutput").ap()
        x.dbg_pvt = nc.dram_tensor("dbg_pvt", [128, H * M], mybir.dt.bfloat16, kind="ExternalOutput").ap()
        x.dbg_rs = nc.dram_tensor("dbg_rs", [128, 16], F32, kind="ExternalOutput").ap()
        x.dbg_pvf = nc.dram_tensor("dbg_pvf", [64, H * C], mybir.dt.bfloat16, kind="ExternalOutput").ap()
        x.dbg_w2 = nc.dram_tensor("dbg_w2", [128, H * C], mybir.dt.bfloat16, kind="ExternalOutput").ap()
    x.gath_d = nc.dram_tensor("gath_d", [NCORES * 64, 4096], F32).ap()

    x.e_rows = x.e_scr.rearrange("h m t -> (h m) t")
    x.k_rows = x.k_scr.rearrange("h m t -> (h m) t")
    x.v_rows = x.v_scr.rearrange("h m t -> (h m) t")

    from contextlib import ExitStack
    es = ExitStack()
    with es:
        x.cst = es.enter_context(nc.sbuf_tensor("cst_sb", [128, len(CONSTS)], F32))
        x.m1u = es.enter_context(nc.sbuf_tensor("m1u_sb", [128, 128], BF16))
        x.m1u8 = es.enter_context(nc.sbuf_tensor("m1u8_sb", [128, 8 * 128], BF16))
        x.ident = es.enter_context(nc.sbuf_tensor("ident_sb", [128, 128], BF16))
        x.gnw_sb = es.enter_context(nc.sbuf_tensor("gnw_sb", [128, 1], F32))
        x.mask_sb = es.enter_context(nc.sbuf_tensor("mask_sb", [128, NCORES], F32))
        x.onescol = es.enter_context(nc.sbuf_tensor("onescol_sb", [128, 1], BF16))
        x.onesrow = es.enter_context(nc.sbuf_tensor("onesrow_sb", [1, 128], F32))
        x.sg_sc = es.enter_context(nc.sbuf_tensor("sg_sc", [128, NHC], F32))
        x.su_sc = es.enter_context(nc.sbuf_tensor("su_sc", [128, NHC], F32))
        x.sd_sc = es.enter_context(nc.sbuf_tensor("sd_sc", [128, NDC], F32))

        for i, vv in enumerate(CONSTS):
            nc.const_aps.register(F32, vv, x.cst[:, i:i + 1])

        x.s_ld = Sem(es.enter_context(nc.semaphore("s_ld")))
        x.s_wr = Sem(es.enter_context(nc.semaphore("s_wr")))
        x.s_gd = Sem(es.enter_context(nc.semaphore("s_gd")))
        x.s_cc = Sem(es.enter_context(nc.semaphore("s_cc")))
        x.s_pe = Sem(es.enter_context(nc.semaphore("s_pe")))
        x.s_v = Sem(es.enter_context(nc.semaphore("s_v")))
        x.s_s = Sem(es.enter_context(nc.semaphore("s_s")))
        x.s_g = Sem(es.enter_context(nc.semaphore("s_g")))

        with nc.Block() as blk:
            @blk.sync
            def _(sy):
                for dst, src in [(x.cst, x.cst_i), (x.m1u, x.m1u_i), (x.m1u8, x.m1u8_i),
                                 (x.ident, x.ident_i), (x.gnw_sb, x.gnw),
                                 (x.mask_sb, x.mask_i), (x.onescol, x.onescol_i),
                                 (x.onesrow, x.onesrow_i), (x.sg_sc, x.sgate),
                                 (x.su_sc, x.sup), (x.sd_sc, x.sdown)]:
                    x.s_ld.inc(sy.dma_start(out=dst[:, :], in_=src), 16)

            @blk.vector
            def _(v):
                v.wait_ge(x.s_ld.h, x.s_ld.n)

        if _upto("b1"):
            with nc.sbuf_tensor("xln", [128, NDC * T], BF16) as _xln_flat, \
                 nc.Block() as blk:
                xln = _xln_flat
                _emit_ln(x, blk, x.x_fm, xln, "ln1")
                _emit_proj(x, blk, xln)

        if _upto("b2"):
            with nc.sbuf_tensor("acc", [64, 4096], F32) as acc, \
                 nc.sbuf_tensor("hkb2", [64, 2 * H * M], BF16) as hkb2, \
                 nc.sbuf_tensor("hvb2", [64, 2 * H * DV], BF16) as hvb2, \
                 nc.sbuf_tensor("wsnap", [1, 2 * H * M], F32) as wsnap:
                x.acc, x.hkb2, x.hvb2, x.wsnap = acc, hkb2, hvb2, wsnap
                with nc.Block() as blk:
                    _emit_states_exchange(x, blk)
                if _upto("b3"):
                    with nc.Block() as blk:
                        _emit_attn(x, blk)

        if _upto("b4"):
            with nc.sbuf_tensor("xln2", [128, NDC * T], BF16) as xln2:
                x.xln2 = xln2
                with nc.Block() as blk:
                    _emit_ln(x, blk, x.x2_scr.rearrange("a p t -> (a p) t"), x.xln2, "ln2")

                    @blk.scalar
                    def _(sc):
                        sc.wait_ge(x.s_v.h, x.s_v.n)
                        sc.wait_ge(x.s_g.h, x.s_g.n)
                        for dc in range(NDC):
                            x.s_wr.inc(sc.dma_start(
                                out=x.xln2_scr[dc],
                                in_=_v3(x.xln2, NDC, T)[:, dc, :]), 16)
            if _upto("b5"):
                with nc.Block() as blk:
                    _emit_ffn_fused(x, blk)

        with nc.Block() as blk:
            @blk.sync
            def _(sy):
                sy.wait_ge(x.s_wr.h, x.s_wr.n)
                sy.wait_ge(x.s_ld.h, x.s_ld.n)
                sy.wait_ge(x.s_gd.h, x.s_gd.n)

    return nc


def _v3(t2d, a, b):
    return t2d[:, :].rearrange("p (a b) -> p a b", a=a)


def _emit_ln(x, blk, src_fm, dst, tag):
    nc = x.nc
    src3 = src_fm.rearrange("(a p) t -> p a t", p=128)
    dst3 = _v3(dst, NDC, T)
    with nc.sbuf_tensor(f"{tag}_xg", [128, NDC * G], F32) as xg2, \
         nc.sbuf_tensor(f"{tag}_xb", [128, NDC * G], BF16) as xb2, \
         nc.sbuf_tensor(f"{tag}_sq", [128, NDC * G], BF16) as sq2, \
         nc.sbuf_tensor(f"{tag}_rows", [1, G * 6], F32) as rows, \
         nc.sbuf_tensor(f"{tag}_bc", [128, 2 * G], F32) as bc, \
         nc.psum_tensor(f"{tag}_psS", [1, G], F32) as psS, \
         nc.psum_tensor(f"{tag}_psQ", [1, G], F32) as psQ, \
         nc.psum_tensor(f"{tag}_psB", [128, 2 * G], F32) as psB:
        xg = _v3(xg2, NDC, G)
        xb = _v3(xb2, NDC, G)
        sq = _v3(sq2, NDC, G)
        mu, msq, tt, var, rstd, mup = (rows[0:1, i * G:(i + 1) * G] for i in range(6))
        for g in range(NG):
            @blk.sync
            def _(sy, g=g):
                sy.wait_ge(x.s_v.h, x.s_v.n)
                sy.wait_ge(x.s_g.h, x.s_g.n)
                sy.wait_ge(x.s_wr.h, x.s_wr.n)
                x.s_ld.inc(sy.dma_start(out=xg[:, :, :],
                                        in_=src3[:, :, g * G:(g + 1) * G]), 16)

            @blk.vector
            def _(v, g=g):
                v.wait_ge(x.s_ld.h, x.s_ld.n)
                v.wait_ge(x.s_pe.h, x.s_pe.n)
                v.tensor_copy(xb2[:, :], xg2[:, :])
                x.s_v.inc(v.tensor_tensor(sq2[:, :], xb2[:, :], xb2[:, :], ALU.mult))

            @blk.tensor
            def _(t, g=g):
                t.wait_ge(x.s_v.h, x.s_v.n)
                for dc in range(NDC):
                    t.matmul(psS[:, :], x.onescol[:, :], xb[:, dc, :],
                             start=(dc == 0), stop=(dc == NDC - 1))
                for dc in range(NDC):
                    ins = t.matmul(psQ[:, :], x.onescol[:, :], sq[:, dc, :],
                                   start=(dc == 0), stop=(dc == NDC - 1))
                x.s_pe.inc(ins)

            @blk.vector
            def _(v, g=g):
                v.wait_ge(x.s_pe.h, x.s_pe.n)
                v.tensor_scalar(mu, psS[:, :], 1.0 / D, None, ALU.mult)
                v.tensor_scalar(msq, psQ[:, :], 1.0 / D, None, ALU.mult)
                v.tensor_tensor(tt, mu, mu, ALU.mult)
                x.s_v.inc(v.tensor_tensor(var, msq, tt, ALU.subtract))

            @blk.scalar
            def _(sc, g=g):
                sc.wait_ge(x.s_v.h, x.s_v.n)
                sc.activation(tt, var, AF.Ln, bias=1e-5, scale=1.0)
                x.s_s.inc(sc.activation(rstd, tt, AF.Exp, bias=0.0, scale=-0.5))

            @blk.vector
            def _(v, g=g):
                v.wait_ge(x.s_s.h, x.s_s.n)
                x.s_v.inc(v.scalar_tensor_tensor(mup, mu, -1.0, rstd, ALU.mult, ALU.mult))

            @blk.tensor
            def _(t, g=g):
                t.wait_ge(x.s_v.h, x.s_v.n)
                t.matmul(psB[:, 0:G], x.onesrow[:, :], rstd, start=True, stop=True)
                ins = t.matmul(psB[:, G:2 * G], x.onesrow[:, :], mup, start=True, stop=True)
                x.s_pe.inc(ins)

            @blk.vector
            def _(v, g=g):
                v.wait_ge(x.s_pe.h, x.s_pe.n)
                x.s_v.inc(v.tensor_copy(bc[:, :], psB[:, :]))

            NPOOL = 6

            @blk.gpsimd
            def _(gp, g=g):
                gp.wait_ge(x.s_v.h, x.s_v.n)
                for dc in range(NDC - NPOOL, NDC):
                    gp.tensor_tensor(xg[:, dc, :], xg[:, dc, :], bc[:, 0:G], ALU.mult)
                    ins = gp.tensor_tensor(dst3[:, dc, g * G:(g + 1) * G],
                                           xg[:, dc, :], bc[:, G:2 * G], ALU.add)
                x.s_g.inc(ins)

            @blk.vector
            def _(v, g=g):
                for dc in range(NDC - NPOOL):
                    v.tensor_tensor(xg[:, dc, :], xg[:, dc, :], bc[:, 0:G], ALU.mult)
                    ins = v.tensor_tensor(dst3[:, dc, g * G:(g + 1) * G],
                                          xg[:, dc, :], bc[:, G:2 * G], ALU.add)
                x.s_v.inc(ins)


def _emit_proj(x, blk, xln):
    nc = x.nc
    xln3 = _v3(xln, NDC, T)
    jobs = []
    for kind, ncc, scr in [("q", 8, x.q_scr), ("k", 8, x.k_scr),
                           ("s", 8, x.e_scr), ("v", 16, x.v_scr),
                           ("g", 16, x.g_scr)]:
        for cc in range(ncc):
            jobs.append((kind, cc, scr))

    with nc.sbuf_tensor("pj_slab", [128, 2 * NDC * 128], BF16) as slab, \
         nc.sbuf_tensor("pj_ev", [128, 2 * T], BF16) as ev, \
         nc.sbuf_tensor("pj_sf", [128, T], F32) as sf, \
         nc.psum_tensor("pj_psA", [128, T], F32) as psA, \
         nc.psum_tensor("pj_psB", [128, T], F32) as psB:
        pss = [psA, psB]
        pe_evt, ev_evt, wr_evt, ld_evt = [], [], [], []

        for j, (kind, cc, scr) in enumerate(jobs):
            ps = pss[j % 2]
            slab_b = slab[:, (j % 2) * NDC * 128:((j % 2) + 1) * NDC * 128]
            sl3 = slab_b.rearrange("p (a c) -> p a c", a=NDC)
            ev_b = ev[:, (j % 2) * T:((j % 2) + 1) * T]

            @blk.gpsimd
            def _(gp, j=j, slab_b=slab_b):
                if j >= 2:
                    gp.wait_ge(x.s_pe.h, pe_evt[j - 2])
                ins = gp.dma_start(out=slab_b, in_=x.wpj[j])
                x.s_gd.inc(ins, 16)
            ld_evt.append(x.s_gd.n)

            @blk.tensor
            def _(t, j=j, ps=ps, sl3=sl3):
                t.wait_ge(x.s_gd.h, ld_evt[j])
                if j >= 2:
                    t.wait_ge(x.s_v.h, ev_evt[j - 2][0])
                    t.wait_ge(x.s_s.h, ev_evt[j - 2][1])
                if j == 0:
                    t.wait_ge(x.s_v.h, x.s_v.n)
                    t.wait_ge(x.s_g.h, x.s_g.n)
                for g in range(NG):
                    for dc in range(NDC):
                        ins = t.matmul(ps[:, g * G:(g + 1) * G], sl3[:, dc, :],
                                       xln3[:, dc, g * G:(g + 1) * G],
                                       start=(dc == 0), stop=(dc == NDC - 1))
                x.s_pe.inc(ins)
            pe_evt.append(x.s_pe.n)

            if kind in ("k", "v"):
                @blk.vector
                def _(v, j=j, ps=ps, ev_b=ev_b):
                    v.wait_ge(x.s_pe.h, pe_evt[j])
                    if j >= 2:
                        v.wait_ge(x.s_wr.h, wr_evt[j - 2])
                    x.s_v.inc(v.tensor_copy(ev_b, ps[:, :]))
            elif kind == "g":
                @blk.scalar
                def _(sc, j=j, ps=ps, ev_b=ev_b):
                    sc.wait_ge(x.s_pe.h, pe_evt[j])
                    if j >= 2:
                        sc.wait_ge(x.s_wr.h, wr_evt[j - 2])
                    x.s_s.inc(sc.activation(ev_b, ps[:, :], AF.Silu,
                                            bias=0.0, scale=1.0))
            elif kind == "q":
                @blk.scalar
                def _(sc, j=j, ps=ps, ev_b=ev_b):
                    sc.wait_ge(x.s_pe.h, pe_evt[j])
                    if j >= 2:
                        sc.wait_ge(x.s_wr.h, wr_evt[j - 2])
                    x.s_s.inc(sc.activation(ev_b, ps[:, :], AF.Copy,
                                            bias=0.0, scale=0.125))
            else:
                @blk.vector
                def _(v, j=j, ps=ps):
                    v.wait_ge(x.s_pe.h, pe_evt[j])
                    v.wait_ge(x.s_s.h, x.s_s.n)
                    x.s_v.inc(v.tensor_scalar(sf[:, :], ps[:, :], 32.0, -32.0,
                                              ALU.min, ALU.max))

                @blk.scalar
                def _(sc, j=j, ev_b=ev_b):
                    sc.wait_ge(x.s_v.h, x.s_v.n)
                    if j >= 2:
                        sc.wait_ge(x.s_wr.h, wr_evt[j - 2])
                    x.s_s.inc(sc.activation(ev_b, sf[:, :], AF.Exp,
                                            bias=-32.0, scale=1.0))
            ev_evt.append((x.s_v.n, x.s_s.n))

            @blk.scalar
            def _(sc, j=j, cc=cc, scr=scr, ev_b=ev_b):
                sc.wait_ge(x.s_v.h, ev_evt[j][0])
                sc.wait_ge(x.s_s.h, ev_evt[j][1])
                if scr.shape[1] == 64:
                    x.s_wr.inc(sc.dma_start(out=scr[2 * cc], in_=ev_b[0:64, :]), 16)
                    x.s_wr.inc(sc.dma_start(out=scr[2 * cc + 1], in_=ev_b[64:128, :]), 16)
                else:
                    x.s_wr.inc(sc.dma_start(out=scr[cc], in_=ev_b[:, :]), 16)
            wr_evt.append(x.s_wr.n)
        x.pj_wr_evt = wr_evt


def _emit_states_exchange(x, blk):
    nc = x.nc
    with nc.sbuf_tensor("seg", [64, 4096], F32) as seg, \
         nc.sbuf_tensor("b2_etm", [128, 2 * H * M], BF16) as etm2, \
         nc.sbuf_tensor("b2_ktm", [128, 2 * H * DK], BF16) as ktm2, \
         nc.sbuf_tensor("b2_vtm", [128, 2 * H * DV], BF16) as vtm2, \
         nc.sbuf_tensor("b2_slab", [64, 2 * 4096], F32) as slabr, \
         nc.psum_tensor("b2_psF", [64, H * M], F32) as psF, \
         nc.psum_tensor("b2_psG1", [64, 8 * DV], F32) as psG1, \
         nc.psum_tensor("b2_psG2", [64, 8 * DV], F32) as psG2, \
         nc.psum_tensor("b2_psH", [1, H * M], F32) as psH:

        b2pe, b2ld = [], []
        for c in range(NCH):
            b = c % 2
            etm = etm2[:, b * H * M:(b + 1) * H * M]
            ktm = ktm2[:, b * H * DK:(b + 1) * H * DK]
            vtm = vtm2[:, b * H * DV:(b + 1) * H * DV]

            @blk.sync
            def _(sy, c=c, etm=etm, ktm=ktm, vtm=vtm):
                if c >= 2:
                    sy.wait_ge(x.s_pe.h, b2pe[c - 2])
                else:
                    sy.wait_ge(x.s_wr.h, x.pj_wr_evt[39])
                x.s_ld.inc(sy.dma_start(out=etm, in_=x.e_rows[:, c * C:(c + 1) * C],
                                        transpose=True), 16)
                x.s_ld.inc(sy.dma_start(out=ktm, in_=x.k_rows[:, c * C:(c + 1) * C],
                                        transpose=True), 16)
                x.s_ld.inc(sy.dma_start(out=vtm, in_=x.v_rows[:, c * C:(c + 1) * C],
                                        transpose=True), 16)
            b2ld.append(x.s_ld.n)

            @blk.tensor
            def _(t, c=c, etm=etm, ktm=ktm, vtm=vtm):
                t.wait_ge(x.s_ld.h, b2ld[c])
                if c == 0:
                    t.wait_ge(x.s_v.h, x.s_v.n)
                    t.wait_ge(x.s_s.h, x.s_s.n)
                st, sp = c == 0, c == NCH - 1
                t.matmul(psH[0:1, 0:512], x.onescol[:, :], etm[:, 0:512],
                         start=st, stop=sp, skip_group_check=True)
                t.matmul(psH[0:1, 512:1024], x.onescol[:, :], etm[:, 512:1024],
                         start=st, stop=sp, skip_group_check=True)
                for h in range(H):
                    t.matmul(psF[0:64, h * M:(h + 1) * M], ktm[:, h * DK:(h + 1) * DK],
                             etm[:, h * M:(h + 1) * M], start=st, stop=sp,
                             skip_group_check=True)
                for h in range(H):
                    ps = psG1 if h < 8 else psG2
                    ins = t.matmul(ps[0:64, (h % 8) * DV:((h % 8) + 1) * DV],
                                   etm[:, h * M:(h + 1) * M],
                                   vtm[:, h * DV:(h + 1) * DV], start=st, stop=sp,
                                   skip_group_check=True)
                x.s_pe.inc(ins)
            b2pe.append(x.s_pe.n)

        @blk.vector
        def _(v):
            v.wait_ge(x.s_pe.h, b2pe[NCH - 1])
            v.tensor_copy(seg[:, 0:H * M], psF[:, :])
            v.tensor_copy(seg[:, 1024:2048], psG1[:, :])
            v.tensor_copy(seg[:, 2048:3072], psG2[:, :])
            x.s_v.inc(v.tensor_copy(seg[0:1, 3072:4096], psH[:, :]))

        @blk.gpsimd
        def _(g):
            g.wait_ge(x.s_v.h, x.s_v.n)
            x.s_gd.inc(g.dma_start(out=x.seg_d, in_=seg[:, :]), 16)
            g.wait_ge(x.s_gd.h, x.s_gd.n)
            x.s_cc.inc(g.collective_compute(
                "AllGather", ALU.bypass,
                replica_groups=[list(range(NCORES))],
                ins=[x.seg_d.opt()],
                outs=[x.gath_d.opt()]))
            g.wait_ge(x.s_cc.h, x.s_cc.n)

        @blk.vector
        def _(v):
            x.s_v.inc(v.memset(x.acc[:, :], 0.0))

        for r in range(NCORES):
            sb = slabr[:, (r % 2) * 4096:((r % 2) + 1) * 4096]

            @blk.gpsimd
            def _(g, r=r, sb=sb):
                if r >= 2:
                    g.wait_ge(x.s_v.h, x.s_v.n)
                x.s_gd.inc(g.dma_start(out=sb, in_=x.gath_d[r * 64:(r + 1) * 64, :]), 16)

            @blk.vector
            def _(v, r=r, sb=sb):
                v.wait_ge(x.s_gd.h, x.s_gd.n)
                x.s_v.inc(v.scalar_tensor_tensor(x.acc[:, :], sb,
                                                 x.mask_sb[0:64, r:r + 1], x.acc[:, :],
                                                 ALU.mult, ALU.add))

        @blk.vector
        def _(v):
            v.tensor_copy(x.hkb2[:, 0:H * M], x.acc[0:64, 0:H * M])
            v.tensor_copy(x.hvb2[:, 0:H * DV], x.acc[0:64, 1024:1024 + H * DV])
            x.s_v.inc(v.tensor_copy(x.wsnap[:, 0:H * M], x.acc[0:1, 3072:3072 + H * M]))


def _emit_attn(x, blk):
    nc = x.nc
    from contextlib import ExitStack
    with ExitStack() as aes:
        def sb(name, shape, dt):
            return aes.enter_context(nc.sbuf_tensor(name, shape, dt))

        qc = sb("at_qc", [64, 2 * H * C], BF16)
        kc = sb("at_kc", [64, 2 * H * C], BF16)
        efc = sb("at_efc", [64, 2 * H * C], BF16)
        gc = sb("at_gc", [128, 2 * H * C], BF16)
        etm2 = sb("at_etm", [128, 2 * H * M], BF16)
        ktm2 = sb("at_ktm", [128, 2 * H * DK], BF16)
        vtm2 = sb("at_vtm", [128, 2 * H * DV], BF16)
        denr = sb("at_denr", [128, 2 * H * M], F32)
        sqkm = sb("at_sqk", [128, H * C], BF16)
        w2m = sb("at_w2", [128, H * C], BF16)
        t1 = sb("at_t1", [128, H * M], F32)
        eo = sb("at_eo", [128, H * M], F32)
        rs = sb("at_rs", [128, 16], F32)
        rs2 = sb("at_rs2", [128, 16], F32)
        pvt = sb("at_pvt", [128, H * M], BF16)
        pvf = sb("at_pvf", [64, H * C], BF16)
        ovsb = sb("at_ov", [128, H * DV], F32)
        sq_sb = sb("at_sq", [128, H * DV], BF16)
        rrow = sb("at_rrow", [1, H * C], F32)
        rg_sb = sb("at_rg", [128, H * DV], F32)
        on_sb = sb("at_on", [128, H * DV], F32)
        og2 = sb("at_og", [128, H * 1024], BF16)
        wos = sb("at_wos", [128, 2 * NDC * 128], BF16)
        xt = sb("at_xt", [128, 2 * G], F32)
        x2t = sb("at_x2", [128, 2 * G], F32)
        P1 = aes.enter_context(nc.psum_tensor("at_P1", [128, 2048], F32))
        P2 = aes.enter_context(nc.psum_tensor("at_P2", [128, 1024], F32))
        P3b = aes.enter_context(nc.psum_tensor("at_P3", [128, 2048], BF16))
        P3f = P3b[:, :].bitcast(F32)

        og3 = og2[:, :].rearrange("p (h t) -> p h t", h=H)

        ev = {}

        def rec(key, sem):
            ev[key] = (sem, sem.n)

        def wt(eng, *keys):
            for k in keys:
                if k in ev:
                    sm, n = ev[k]
                    eng.wait_ge(sm.h, n)

        def tile_loads(c):
            p = c % 2
            sl = slice(c * C, (c + 1) * C)

            @blk.sync
            def _(sy, c=c, p=p, sl=sl):
                wt(sy, f"tilerd{p}")
                x.s_ld.inc(sy.dma_start(out=etm2[:, p * H * M:(p + 1) * H * M],
                                        in_=x.e_rows[:, sl], transpose=True), 16)
                x.s_ld.inc(sy.dma_start(out=ktm2[:, p * H * DK:(p + 1) * H * DK],
                                        in_=x.k_rows[:, sl], transpose=True), 16)
                x.s_ld.inc(sy.dma_start(out=vtm2[:, p * H * DV:(p + 1) * H * DV],
                                        in_=x.v_rows[:, sl], transpose=True), 16)

            @blk.gpsimd
            def _(gp, c=c, p=p, sl=sl):
                wt(gp, f"qkrd{p}")
                x.s_ld.inc(gp.dma_start(
                    out=qc[:, p * H * C:(p + 1) * H * C].rearrange("p (h t) -> p h t", h=H),
                    in_=x.q_scr.rearrange("h m t -> m h t")[:, :, sl]), 16)
                x.s_ld.inc(gp.dma_start(
                    out=kc[:, p * H * C:(p + 1) * H * C].rearrange("p (h t) -> p h t", h=H),
                    in_=x.k_scr.rearrange("h m t -> m h t")[:, :, sl]), 16)
                x.s_ld.inc(gp.dma_start(
                    out=efc[:, p * H * C:(p + 1) * H * C].rearrange("p (h t) -> p h t", h=H),
                    in_=x.e_scr.rearrange("h m t -> m h t")[:, :, sl]), 16)
                x.s_ld.inc(gp.dma_start(
                    out=gc[:, p * H * C:(p + 1) * H * C].rearrange("p (h t) -> p h t", h=H),
                    in_=x.g_scr.rearrange("h m t -> m h t")[:, :, sl]), 16)
            rec(f"tileld{p}", x.s_ld)

        def emit_denom(c):
            p = c % 2

            @blk.tensor
            def _(t, c=c, p=p):
                wt(t, "P3", f"tileld{p}", f"wsnap{p}")
                for half in range(2):
                    sl = slice(half * 512, (half + 1) * 512)
                    t.matmul(P3f[:, sl], x.m1u[:, :],
                             etm2[:, p * H * M + half * 512: p * H * M + half * 512 + 512],
                             start=True, stop=False)
                    ins = t.matmul(P3f[:, sl], x.onesrow[:, :],
                                   x.wsnap[:, p * H * M + half * 512: p * H * M + half * 512 + 512],
                                   start=False, stop=True)
                x.s_pe.inc(ins)
            rec("CPE", x.s_pe)

            @blk.scalar
            def _(sc, c=c, p=p):
                wt(sc, "CPE", "t1rd")
                sc.activation(t1[:, :], P3f[:, :], AF.Ln, bias=0.0, scale=1.0)
                x.s_s.inc(sc.activation(denr[:, p * H * M:(p + 1) * H * M],
                                        t1[:, :], AF.Exp, bias=0.0, scale=-1.0))
            rec("P3", x.s_s)
            rec(f"denr{p}", x.s_s)
            rec("t1wr", x.s_s)
            if _KDBG and c < 2:
                @blk.scalar
                def _(sc, c=c, p=p):
                    sc.wait_ge(x.s_s.h, x.s_s.n)
                    x.s_wr.inc(sc.dma_start(out=x.dbg_denr[c],
                                            in_=denr[:, p * H * M:(p + 1) * H * M]), 16)

        tile_loads(0)
        emit_denom(0)
        tile_loads(1)

        for c in range(NCH):
            p = c % 2
            np_ = 1 - p
            etm = etm2[:, p * H * M:(p + 1) * H * M]
            ktm = ktm2[:, p * H * DK:(p + 1) * H * DK]
            vtm = vtm2[:, p * H * DV:(p + 1) * H * DV]
            q3 = qc[:, p * H * C:(p + 1) * H * C]
            k3 = kc[:, p * H * C:(p + 1) * H * C]
            ef3 = efc[:, p * H * C:(p + 1) * H * C]
            g3 = gc[:, p * H * C:(p + 1) * H * C]
            dnr = denr[:, p * H * M:(p + 1) * H * M]
            acc_hk = x.acc[0:64, 0:H * M]
            acc_hv1 = x.acc[0:64, 1024:1024 + 8 * DV]
            acc_hv2 = x.acc[0:64, 1024 + 8 * DV:1024 + 16 * DV]
            cst = (c % 8) * C

            @blk.tensor
            def _(t, c=c, p=p, etm=etm, ktm=ktm, vtm=vtm):
                wt(t, f"tileld{p}", "P1free", "P2free", "P3", f"shadrd{1 - p}")
                for h in range(H):
                    t.matmul(P2[0:64, h * M:(h + 1) * M], ktm[:, h * DK:(h + 1) * DK],
                             etm[:, h * M:(h + 1) * M], start=True, stop=True)
                for h in range(H):
                    ps = P1[0:64, 0:1024] if h < 8 else P1[0:64, 1024:2048]
                    t.matmul(ps[:, (h % 8) * DV:((h % 8) + 1) * DV],
                             etm[:, h * M:(h + 1) * M],
                             vtm[:, h * DV:(h + 1) * DV], start=True, stop=True)
                t.matmul(P3f[0:1, 0:512], x.onescol[:, :], etm[:, 0:512],
                         start=True, stop=True)
                ins = t.matmul(P3f[0:1, 512:1024], x.onescol[:, :], etm[:, 512:1024],
                               start=True, stop=True)
                x.s_pe.inc(ins)
            rec("UPE", x.s_pe)

            @blk.vector
            def _(v, c=c, p=p, np_=np_):
                wt(v, "UPE")
                v.tensor_tensor(x.acc[0:64, 0:H * M], x.acc[0:64, 0:H * M],
                                P2[0:64, :], ALU.add)
                v.tensor_tensor(x.acc[0:64, 1024:1024 + 8 * DV],
                                x.acc[0:64, 1024:1024 + 8 * DV],
                                P1[0:64, 0:1024], ALU.add)
                v.tensor_tensor(x.acc[0:64, 1024 + 8 * DV:1024 + 16 * DV],
                                x.acc[0:64, 1024 + 8 * DV:1024 + 16 * DV],
                                P1[0:64, 1024:2048], ALU.add)
                v.tensor_tensor(x.wsnap[:, np_ * H * M:(np_ + 1) * H * M],
                                x.wsnap[:, p * H * M:(p + 1) * H * M],
                                P3f[0:1, 0:H * M], ALU.add)
                v.tensor_copy(x.hkb2[:, np_ * H * M:(np_ + 1) * H * M],
                              x.acc[0:64, 0:H * M])
                ins = v.tensor_copy(x.hvb2[:, np_ * H * DV:(np_ + 1) * H * DV],
                                    x.acc[0:64, 1024:1024 + H * DV])
                x.s_v.inc(ins)
            rec("UV", x.s_v)
            rec(f"wsnap{np_}", x.s_v)
            rec("P2free_afterF", x.s_v)

            @blk.tensor
            def _(t, c=c, p=p, k3=k3, q3=q3):
                wt(t, "UV", f"qkld{p}")
                for h in range(H):
                    ins = t.matmul(P1[:, h * C:(h + 1) * C],
                                   k3[:, h * C:(h + 1) * C], q3[:, h * C:(h + 1) * C],
                                   start=True, stop=True)
                x.s_pe.inc(ins)
            rec("APE", x.s_pe)

            @blk.vector
            def _(v, c=c):
                wt(v, "APE", "sqkmrd")
                v.tensor_tensor(sqkm[:, 0:1024], P1[:, 0:1024], x.m1u8[:, :], ALU.mult)
                ins = v.tensor_tensor(sqkm[:, 1024:2048], P1[:, 1024:2048],
                                      x.m1u8[:, :], ALU.mult)
                x.s_v.inc(ins)
            rec("sqkmwr", x.s_v)
            rec("P1free", x.s_v)

            @blk.tensor
            def _(t, c=c, p=p, etm=etm, q3=q3):
                wt(t, "sqkmwr", "P2free_afterF")
                for h in range(H):
                    t.matmul(P2[:, h * M:(h + 1) * M], sqkm[:, h * C:(h + 1) * C],
                             etm[:, h * M:(h + 1) * M], start=True, stop=False)
                    ins = t.matmul(P2[:, h * M:(h + 1) * M], q3[:, h * C:(h + 1) * C],
                                   x.hkb2[:, p * H * M + h * M: p * H * M + (h + 1) * M],
                                   start=False, stop=True)
                x.s_pe.inc(ins)
            rec("BPE", x.s_pe)
            rec("sqkmrd", x.s_pe)

            @blk.vector
            def _(v, c=c, p=p, dnr=dnr):
                wt(v, "BPE", f"denr{p}", "t1wr")
                x.s_v.inc(v.tensor_tensor(t1[:, :], P2[:, :], dnr, ALU.mult))
            rec("okV", x.s_v)

            @blk.scalar
            def _(sc, c=c):
                wt(sc, "okV", "eord")
                x.s_s.inc(sc.activation(eo[:, :], t1[:, :], AF.Exp, bias=0.0, scale=1.0))
            rec("eoS", x.s_s)
            rec("t1rd", x.s_s)

            @blk.vector
            def _(v, c=c, p=p, dnr=dnr):
                wt(v, "eoS", "pvtrd")
                v.tensor_reduce(rs2[:, :], eo[:, :].rearrange("p (i m) -> p i m", i=H),
                                AX, ALU.add)
                v.reciprocal(rs[:, 0:8], rs2[:, 0:8])
                v.reciprocal(rs[:, 8:16], rs2[:, 8:16])
                v.tensor_tensor(t1[:, :], eo[:, :], dnr, ALU.mult)
                ins = v.tensor_tensor(
                    pvt[:, :].rearrange("p (i m) -> p i m", i=H),
                    t1[:, :].rearrange("p (i m) -> p i m", i=H),
                    rs[:, :].rearrange("p (i o) -> p i o", o=1).broadcast_to([128, H, M]),
                    ALU.mult)
                x.s_v.inc(ins)
            rec("pvtwr", x.s_v)
            rec("P2free", x.s_v)
            rec("eord", x.s_v)

            if _KDBG and c == 0:
                @blk.scalar
                def _(sc):
                    sc.wait_ge(x.s_v.h, x.s_v.n)
                    x.s_wr.inc(sc.dma_start(out=x.dbg_pvt, in_=pvt[:, :]), 16)
                    x.s_wr.inc(sc.dma_start(out=x.dbg_rs, in_=rs[:, :]), 16)

            @blk.tensor
            def _(t, c=c):
                wt(t, "pvtwr", "P3")
                for i in range(H):
                    ins = t.transpose(P3b[0:64, i * C:(i + 1) * C],
                                      pvt[:, i * M:(i + 1) * M], x.ident[:, :])
                x.s_pe.inc(ins)
            rec("TPE", x.s_pe)
            rec("pvtrd", x.s_pe)

            @blk.vector
            def _(v, c=c):
                wt(v, "TPE", "pvfrd")
                x.s_v.inc(v.tensor_copy(pvf[:, :], P3b[0:64, :]))
            rec("pvfwr", x.s_v)
            rec("P3", x.s_v)

            if _KDBG and c == 0:
                @blk.scalar
                def _(sc):
                    sc.wait_ge(x.s_v.h, x.s_v.n)
                    x.s_wr.inc(sc.dma_start(out=x.dbg_pvf, in_=pvf[:, :]), 16)

            @blk.tensor
            def _(t, c=c, ef3=ef3):
                wt(t, "pvfwr", "P1free")
                for h in range(H):
                    ins = t.matmul(P1[:, h * C:(h + 1) * C], ef3[:, h * C:(h + 1) * C],
                                   pvf[:, h * C:(h + 1) * C], start=True, stop=True)
                x.s_pe.inc(ins)
            rec("DPE", x.s_pe)

            @blk.vector
            def _(v, c=c):
                wt(v, "DPE", "w2mrd")
                v.tensor_tensor(w2m[:, 0:1024], P1[:, 0:1024], x.m1u8[:, :], ALU.mult)
                ins = v.tensor_tensor(w2m[:, 1024:2048], P1[:, 1024:2048],
                                      x.m1u8[:, :], ALU.mult)
                x.s_v.inc(ins)
            rec("w2mwr", x.s_v)
            rec("P1free", x.s_v)

            if _KDBG and c == 0:
                @blk.scalar
                def _(sc):
                    sc.wait_ge(x.s_v.h, x.s_v.n)
                    x.s_wr.inc(sc.dma_start(out=x.dbg_w2, in_=w2m[:, :]), 16)

            @blk.tensor
            def _(t, c=c, p=p, vtm=vtm):
                wt(t, "w2mwr", "P1free", "ovrd")
                for h in range(H):
                    t.matmul(P1[:, h * C:(h + 1) * C], vtm[:, h * DV:(h + 1) * DV],
                             w2m[:, h * C:(h + 1) * C], start=True, stop=False)
                    ins = t.matmul(P1[:, h * C:(h + 1) * C],
                                   x.hvb2[:, p * H * DV + h * DV: p * H * DV + (h + 1) * DV],
                                   pvf[:, h * C:(h + 1) * C], start=False, stop=True)
                x.s_pe.inc(ins)
            rec("EPE", x.s_pe)
            rec("w2mrd", x.s_pe)
            rec("pvfrd", x.s_pe)
            rec(f"tilerd{p}", x.s_pe)
            rec(f"shadrd{p}", x.s_pe)

            @blk.vector
            def _(v, c=c):
                wt(v, "EPE")
                x.s_v.inc(v.tensor_copy(ovsb[:, :], P1[:, :]))
            rec("ovwr", x.s_v)
            rec("P1free", x.s_v)

            if _KDBG and c == 0:
                @blk.scalar
                def _(sc):
                    sc.wait_ge(x.s_v.h, x.s_v.n)
                    x.s_wr.inc(sc.dma_start(out=x.dbg_ov, in_=ovsb[:, :]), 16)

            @blk.scalar
            def _(sc, c=c):
                wt(sc, "ovwr", "sqrd")
                x.s_s.inc(sc.activation(sq_sb[:, :], ovsb[:, :], AF.Square,
                                        bias=0.0, scale=1.0))
            rec("sqwr", x.s_s)

            @blk.tensor
            def _(t, c=c):
                wt(t, "sqwr", "P2free", "P3")
                for i in range(8):
                    t.matmul(P2[0:1, i * C:(i + 1) * C], x.onescol[:, :],
                             sq_sb[:, i * DV:(i + 1) * DV], start=True, stop=True)
                for i in range(8):
                    ins = t.matmul(P3f[0:1, i * 128:(i + 1) * 128], x.onescol[:, :],
                                   sq_sb[:, (8 + i) * DV:(9 + i) * DV],
                                   start=True, stop=True)
                x.s_pe.inc(ins)
            rec("RPE", x.s_pe)
            rec("sqrd", x.s_pe)

            @blk.scalar
            def _(sc, c=c):
                wt(sc, "RPE")
                sc.activation(rrow[:, 0:1024], P2[0:1, :], AF.Ln, bias=1e-5, scale=1.0 / DV)
                sc.activation(rrow[:, 1024:2048], P3f[0:1, :], AF.Ln, bias=1e-5, scale=1.0 / DV)
                x.s_s.inc(sc.activation(rrow[:, :], rrow[:, :], AF.Exp, bias=0.0, scale=-0.5))
            rec("rrowS", x.s_s)
            rec("P2free", x.s_s)
            rec("P3", x.s_s)

            @blk.tensor
            def _(t, c=c):
                wt(t, "rrowS", "P1free")
                for i in range(H):
                    ins = t.matmul(P1[:, i * C:(i + 1) * C], x.onesrow[:, :],
                                   rrow[:, i * C:(i + 1) * C], start=True, stop=True)
                x.s_pe.inc(ins)
            rec("RBPE", x.s_pe)

            @blk.vector
            def _(v, c=c, g3=g3, cst=cst):
                wt(v, "RBPE", "rgrd", "ogrd")
                v.tensor_scalar(rg_sb[:, :], P1[:, :], x.gnw_sb[:, 0:1], None, ALU.mult)
                v.tensor_tensor(on_sb[:, :], ovsb[:, :], rg_sb[:, :], ALU.mult)
                ins = v.tensor_tensor(
                    og3[:, :, cst:cst + C],
                    on_sb[:, :].rearrange("p (h t) -> p h t", h=H),
                    g3.rearrange("p (h t) -> p h t", h=H), ALU.mult)
                x.s_v.inc(ins)
            rec("ogV", x.s_v)
            rec("ovrd", x.s_v)
            rec("rgrd", x.s_v)
            rec("P1free", x.s_v)
            rec(f"qkld{p}", x.s_v)
            rec(f"qkrd{p}", x.s_v)

            if _KDBG and c == 7:
                @blk.scalar
                def _(sc):
                    sc.wait_ge(x.s_v.h, x.s_v.n)
                    x.s_wr.inc(sc.dma_start(out=x.dbg_og, in_=og2[:, :]), 16)

            if c + 2 < NCH:
                tile_loads(c + 2)

            if c + 1 < NCH:
                emit_denom(c + 1)

            if c % 8 == 7:
                _emit_wo(x, blk, c // 8, og3, wos, xt, x2t, P1, P2, rec, wt)


def _emit_wo(x, blk, sup, og3, wos, xt, x2t, P1, P2, rec, wt):
    nc = x.nc
    pe_evt = []
    for dc in range(NDC):
        bb = dc % 2
        wslab = wos[:, bb * NDC * 128:(bb + 1) * NDC * 128]
        slw = wslab.rearrange("p (a c) -> p a c", a=NDC)

        @blk.gpsimd
        def _(gp, dc=dc, wslab=wslab):
            if dc >= 2:
                gp.wait_ge(x.s_pe.h, pe_evt[dc - 2])
            else:
                gp.wait_ge(x.s_pe.h, x.s_pe.n)
            ins = gp.dma_start(out=wslab, in_=x.wo_t[dc])
            x.s_gd.inc(ins, 16)

        for g2 in range(2):
            tok0 = sup * 1024 + g2 * G
            xtb = xt[:, bb * G:(bb + 1) * G]
            x2b = x2t[:, bb * G:(bb + 1) * G]
            psl = P1[:, (bb * 2 + g2) * G % 2048:((bb * 2 + g2) * G % 2048) + G]

            @blk.sync
            def _(sy, xtb=xtb, dc=dc, tok0=tok0):
                sy.wait_ge(x.s_v.h, x.s_v.n)
                x.s_ld.inc(sy.dma_start(out=xtb,
                                        in_=x.x_fm[dc * 128:(dc + 1) * 128,
                                                   tok0:tok0 + G]), 16)

            @blk.tensor
            def _(t, dc=dc, g2=g2, psl=psl, slw=slw):
                t.wait_ge(x.s_gd.h, x.s_gd.n)
                if dc < 2 and g2 == 0:
                    wt(t, "P1free", "ogV")
                t.wait_ge(x.s_v.h, x.s_v.n)
                for vd in range(H):
                    ins = t.matmul(psl, slw[:, vd, :],
                                   og3[:, vd, g2 * G:(g2 + 1) * G],
                                   start=(vd == 0), stop=(vd == H - 1))
                x.s_pe.inc(ins)
            if g2 == 1:
                pe_evt.append(x.s_pe.n)

            @blk.vector
            def _(v, dc=dc, g2=g2, psl=psl, xtb=xtb, x2b=x2b):
                v.wait_ge(x.s_pe.h, x.s_pe.n)
                v.wait_ge(x.s_ld.h, x.s_ld.n)
                v.wait_ge(x.s_wr.h, x.s_wr.n)
                x.s_v.inc(v.tensor_tensor(x2b, psl, xtb, ALU.add))

            @blk.scalar
            def _(sc, dc=dc, x2b=x2b, tok0=tok0):
                sc.wait_ge(x.s_v.h, x.s_v.n)
                x.s_wr.inc(sc.dma_start(out=x.x2_scr[dc, :, tok0:tok0 + G],
                                        in_=x2b), 16)
    rec("P1free", x.s_v)
    rec("ogrd", x.s_pe)


def _emit_ffn_fused(x, blk):
    nc = x.nc
    GT = 1024
    with nc.sbuf_tensor("ff_x", [128, NDC * GT], BF16) as xg2, \
         nc.sbuf_tensor("ff_hid", [128, NHC * GT], BF16) as hid2, \
         nc.sbuf_tensor("ff_sg", [128, 2 * NDC * 128], BF16) as slg, \
         nc.sbuf_tensor("ff_su", [128, 2 * NDC * 128], BF16) as slu, \
         nc.sbuf_tensor("ff_wd", [128, 2 * NHC * 128], BF16) as wds, \
         nc.sbuf_tensor("ff_act", [128, GT], F32) as actt, \
         nc.sbuf_tensor("ff_x2", [128, 2 * GT], F32) as x2t, \
         nc.sbuf_tensor("ff_o", [128, 2 * GT], F32) as ot, \
         nc.psum_tensor("ff_PA", [128, 2048], F32) as PA, \
         nc.psum_tensor("ff_PB", [128, 2048], F32) as PB:
        xg3 = xg2[:, :].rearrange("p (a t) -> p a t", a=NDC)
        hid3 = hid2[:, :].rearrange("p (h t) -> p h t", h=NHC)
        ev = {}

        def rec(key, sem):
            ev[key] = (sem, sem.n)

        def wt(eng, *keys):
            for k in keys:
                if k in ev:
                    sm, n = ev[k]
                    eng.wait_ge(sm.h, n)

        for grp in range(2):
            t0 = grp * GT

            @blk.sync
            def _(sy, grp=grp, t0=t0):
                wt(sy, "xgrd")
                sy.wait_ge(x.s_wr.h, x.s_wr.n)
                x.s_ld.inc(sy.dma_start(
                    out=xg3[:, :, :],
                    in_=x.xln2_scr.rearrange("a p t -> p a t")[:, :, t0:t0 + GT]), 16)
            rec("xgld", x.s_ld)

            for hc in range(NHC):
                b = hc % 2
                sgb = slg[:, b * NDC * 128:(b + 1) * NDC * 128]
                sub = slu[:, b * NDC * 128:(b + 1) * NDC * 128]
                pg = PA[:, b * 1024:(b + 1) * 1024]
                pu = PB[:, b * 1024:(b + 1) * 1024]

                @blk.gpsimd
                def _(gp, hc=hc, sgb=sgb, sub=sub):
                    wt(gp, f"slabrd{hc % 2}")
                    x.s_gd.inc(gp.dma_start(out=sgb, in_=x.wg_t[hc]), 16)
                    x.s_gd.inc(gp.dma_start(out=sub, in_=x.wu_t[hc]), 16)
                rec(f"slabld{hc % 2}", x.s_gd)

                @blk.vector
                def _(v, hc=hc, sgb=sgb, sub=sub):
                    wt(v, f"slabld{hc % 2}", f"slabrd{hc % 2}")
                    v.tensor_scalar(sgb, sgb, -128.0, None, ALU.add)
                    x.s_v.inc(v.tensor_scalar(sub, sub, -128.0, None, ALU.add))
                rec(f"deq{hc % 2}", x.s_v)

                sg3 = sgb.rearrange("p (a c) -> p a c", a=NDC)
                su3 = sub.rearrange("p (a c) -> p a c", a=NDC)

                @blk.tensor
                def _(t, hc=hc, sg3=sg3, su3=su3, pg=pg, pu=pu):
                    wt(t, f"deq{hc % 2}", "xgld", f"gu{hc % 2}")
                    if hc < 2:
                        wt(t, "pdrd0", "pdrd1")
                    for g2 in range(2):
                        for dc in range(NDC):
                            t.matmul(pg[:, g2 * G:(g2 + 1) * G], sg3[:, dc, :],
                                     xg3[:, dc, g2 * G:(g2 + 1) * G],
                                     start=(dc == 0), stop=(dc == NDC - 1))
                    for g2 in range(2):
                        for dc in range(NDC):
                            ins = t.matmul(pu[:, g2 * G:(g2 + 1) * G], su3[:, dc, :],
                                           xg3[:, dc, g2 * G:(g2 + 1) * G],
                                           start=(dc == 0), stop=(dc == NDC - 1))
                    x.s_pe.inc(ins)
                rec(f"slabrd{hc % 2}", x.s_pe)
                rec(f"gupe{hc}", x.s_pe)
                if hc == NHC - 1:
                    rec("xgrd", x.s_pe)

                @blk.scalar
                def _(sc, hc=hc, pg=pg):
                    wt(sc, f"gupe{hc}", "actrd")
                    x.s_s.inc(sc.activation(actt[:, :], pg, AF.Silu, bias=0.0,
                                            scale=x.sg_sc[:, hc:hc + 1]))
                rec(f"sil{hc}", x.s_s)

                @blk.vector
                def _(v, hc=hc, pu=pu):
                    wt(v, f"sil{hc}", "hidrd")
                    x.s_v.inc(v.scalar_tensor_tensor(hid3[:, hc, :], pu,
                                                     x.su_sc[:, hc:hc + 1],
                                                     actt[:, :], ALU.mult, ALU.mult))
                rec(f"gu{hc % 2}", x.s_v)
                rec("actrd", x.s_v)
                rec(f"hid{hc}", x.s_v)

            for dc in range(NDC):
                b = dc % 2
                wdb = wds[:, b * NHC * 128:(b + 1) * NHC * 128]
                pd = PA[:, b * 1024:(b + 1) * 1024]
                x2b = x2t[:, b * GT:(b + 1) * GT]
                otb = ot[:, b * GT:(b + 1) * GT]

                @blk.gpsimd
                def _(gp, dc=dc, wdb=wdb):
                    wt(gp, f"wdrd{dc % 2}")
                    x.s_gd.inc(gp.dma_start(out=wdb, in_=x.wd_t[dc]), 16)
                rec(f"wdld{dc % 2}", x.s_gd)

                @blk.sync
                def _(sy, dc=dc, t0=t0, x2b=x2b):
                    wt(sy, f"x2rd{dc % 2}")
                    x.s_ld.inc(sy.dma_start(out=x2b,
                                            in_=x.x2_scr[dc, :, t0:t0 + GT]), 16)
                rec(f"x2ld{dc % 2}", x.s_ld)

                @blk.vector
                def _(v, dc=dc, wdb=wdb):
                    wt(v, f"wdld{dc % 2}", f"wdrd{dc % 2}")
                    x.s_v.inc(v.tensor_scalar(wdb, wdb, -128.0, None, ALU.add))
                rec(f"wdeq{dc % 2}", x.s_v)

                wd3 = wdb.rearrange("p (a c) -> p a c", a=NHC)

                @blk.tensor
                def _(t, dc=dc, wd3=wd3, pd=pd):
                    wt(t, f"wdeq{dc % 2}", f"pdrd{dc % 2}")
                    if dc < 2:
                        wt(t, f"sil{NHC - 2}", f"sil{NHC - 1}")
                    for g2 in range(2):
                        for hc in range(NHC):
                            ins = t.matmul(pd[:, g2 * G:(g2 + 1) * G], wd3[:, hc, :],
                                           hid3[:, hc, g2 * G:(g2 + 1) * G],
                                           start=(hc == 0), stop=(hc == NHC - 1))
                    x.s_pe.inc(ins)
                rec(f"wdrd{dc % 2}", x.s_pe)
                rec(f"dpe{dc}", x.s_pe)
                if dc == NDC - 1:
                    rec("hidrd", x.s_pe)

                @blk.vector
                def _(v, dc=dc, pd=pd, x2b=x2b, otb=otb):
                    wt(v, f"dpe{dc}", f"x2ld{dc % 2}", f"otwr{dc % 2}")
                    x.s_v.inc(v.scalar_tensor_tensor(otb, pd, x.sd_sc[:, dc:dc + 1],
                                                     x2b, ALU.mult, ALU.add))
                rec(f"otV{dc % 2}", x.s_v)
                rec(f"x2rd{dc % 2}", x.s_v)
                rec(f"pdrd{dc % 2}", x.s_v)

                @blk.scalar
                def _(sc, dc=dc, t0=t0, otb=otb):
                    wt(sc, f"otV{dc % 2}")
                    x.s_wr.inc(sc.dma_start(
                        out=x.out_fm[dc * 128:(dc + 1) * 128, t0:t0 + GT],
                        in_=otb), 16)
                rec(f"otwr{dc % 2}", x.s_wr)


_CACHE = {}


def _tile_slabs(w2d):
    r, c = w2d.shape
    return np.ascontiguousarray(
        w2d.reshape(r // 128, 128, c // 128, 128).transpose(2, 1, 0, 3)
        .reshape(c // 128, 128, (r // 128) * 128))


def _lowbyte(a_int32):
    import sys
    a = np.ascontiguousarray(np.asarray(a_int32))
    return a.view(np.uint8).reshape(a.shape + (4,))[..., 0 if sys.byteorder == "little" else 3]


def _host_inputs(inputs):
    x = np.asarray(inputs["L0_attn_hidden_states"])
    wpj = np.concatenate([_tile_slabs(np.asarray(inputs[k]))
                          for k in ("L0_attn_Wq", "L0_attn_Wk", "L0_attn_Ws",
                                    "L0_attn_Wv", "L0_attn_Wg")], axis=0)
    shared = dict(
        wpj=wpj,
        wo_t=_tile_slabs(np.asarray(inputs["L0_attn_Wo"])),
        wg_t=_tile_slabs(np.ascontiguousarray(_lowbyte(inputs["L0_ffn_wq_gate"]).T)),
        wu_t=_tile_slabs(np.ascontiguousarray(_lowbyte(inputs["L0_ffn_wq_up"]).T)),
        wd_t=_tile_slabs(np.ascontiguousarray(_lowbyte(inputs["L0_ffn_wq_down"]).T)),
        sgate=np.ascontiguousarray(np.asarray(inputs["L0_ffn_s_gate"]).reshape(NHC, 128).T),
        sup=np.ascontiguousarray(np.asarray(inputs["L0_ffn_s_up"]).reshape(NHC, 128).T),
        sdown=np.ascontiguousarray(np.asarray(inputs["L0_ffn_s_down"]).reshape(NDC, 128).T),
        gnw=np.ascontiguousarray(np.asarray(inputs["L0_attn_g_norm_weight"]).reshape(128, 1)),
        m1u=np.ascontiguousarray(np.triu(np.ones((128, 128))).astype(_BF)),
        m1u8=np.ascontiguousarray(np.tile(np.triu(np.ones((128, 128))).astype(_BF), (1, 8))),
        ident=np.eye(128).astype(_BF),
        cst=np.ascontiguousarray(np.tile(np.array([CONSTS], np.float32), (128, 1))),
        onescol=np.ones((128, 1), dtype=_BF),
        onesrow=np.ones((1, 128), dtype=np.float32),
    )
    per_core = []
    for c in range(NCORES):
        b, qt = divmod(c, 4)
        mk = np.zeros((128, NCORES), np.float32)
        for r in range(b * 4, c):
            mk[:, r] = 1.0
        m = dict(shared)
        m["x_fm"] = np.ascontiguousarray(x[b, qt * T:(qt + 1) * T, :].T)
        m["maskpre"] = mk
        per_core.append(m)
    return per_core


def kernel(**inputs):
    if "nc" not in _CACHE:
        _CACHE["nc"] = build()
    nc = _CACHE["nc"]
    in_maps = _host_inputs(inputs)
    res = run_bass_kernel_spmd(nc, in_maps, core_ids=list(range(NCORES)))
    out = np.empty((2, 8192, D), np.float32)
    for c in range(NCORES):
        b, qt = divmod(c, 4)
        out[b, qt * T:(qt + 1) * T, :] = res.results[c]["out_fm"].T
    return out


# version 11
# speedup vs baseline: 1.1656x; 1.1228x over previous
import os
import numpy as np
import ml_dtypes
import concourse.bass as bass
import concourse.mybir as mybir
from concourse.bass_utils import run_bass_kernel_spmd

F32 = mybir.dt.float32
BF16 = mybir.dt.bfloat16
I32 = mybir.dt.int32
U8 = mybir.dt.uint8
AF = mybir.ActivationFunctionType
ALU = mybir.AluOpType
AX = mybir.AxisListType.X

D = 2048
T = 2048
H = 16
DK = 64
DV = 128
M = 64
C = 128
NCH = T // C
G = 512
NG = T // G
HID = 5632
NHC = HID // 128
NDC = D // 128
NCORES = 8

CONSTS = [0.0, 1.0, -1.0, -0.5, -32.0, 1e-5, 1.0 / 128.0, 0.125, 1.0 / 2048.0]
_BF = ml_dtypes.bfloat16

_KPHASE = os.environ.get("KPHASE", "all")
_KDBG = os.environ.get("KDBG", "0") == "1"
_PHASES = ["b1", "b2", "b3", "b4", "b5", "b6"]


def _upto(ph):
    return _KPHASE == "all" or _PHASES.index(ph) <= _PHASES.index(_KPHASE)


class Sem:
    def __init__(self, h):
        self.h = h
        self.n = 0

    def inc(self, instr, amt=1):
        instr.then_inc(self.h, amt)
        self.n += amt
        return instr


class Ctx:
    pass


def build():
    nc = bass.Bass("TRN2", target_bir_lowering=False)
    x = Ctx()
    x.nc = nc
    scr_kind = "ExternalOutput" if _KDBG else "Internal"

    x.x_fm = nc.dram_tensor("x_fm", [D, T], F32, kind="ExternalInput").ap()
    x.wpj = nc.dram_tensor("wpj", [56, 128, NDC * 128], F32, kind="ExternalInput").ap()
    x.wo_t = nc.dram_tensor("wo_t", [NDC, 128, NDC * 128], F32, kind="ExternalInput").ap()
    x.wg_t = nc.dram_tensor("wg_t", [NHC, 128, NDC * 128], U8, kind="ExternalInput").ap()
    x.wu_t = nc.dram_tensor("wu_t", [NHC, 128, NDC * 128], U8, kind="ExternalInput").ap()
    x.wd_t = nc.dram_tensor("wd_t", [NDC, 128, NHC * 128], U8, kind="ExternalInput").ap()
    x.sgate = nc.dram_tensor("sgate", [128, NHC], F32, kind="ExternalInput").ap()
    x.sup = nc.dram_tensor("sup", [128, NHC], F32, kind="ExternalInput").ap()
    x.sdown = nc.dram_tensor("sdown", [128, NDC], F32, kind="ExternalInput").ap()
    x.gnw = nc.dram_tensor("gnw", [128, 1], F32, kind="ExternalInput").ap()
    x.m1u_i = nc.dram_tensor("m1u", [128, 128], BF16, kind="ExternalInput").ap()
    x.m1u8_i = nc.dram_tensor("m1u8", [128, 8 * 128], BF16, kind="ExternalInput").ap()
    x.ident_i = nc.dram_tensor("ident", [128, 128], BF16, kind="ExternalInput").ap()
    x.cst_i = nc.dram_tensor("cst", [128, len(CONSTS)], F32, kind="ExternalInput").ap()
    x.mask_i = nc.dram_tensor("maskpre", [128, NCORES], F32, kind="ExternalInput").ap()
    x.onescol_i = nc.dram_tensor("onescol", [128, 1], BF16, kind="ExternalInput").ap()
    x.onesrow_i = nc.dram_tensor("onesrow", [1, 128], F32, kind="ExternalInput").ap()

    x.out_fm = nc.dram_tensor("out_fm", [D, T], F32, kind="ExternalOutput").ap()

    x.q_scr = nc.dram_tensor("q_scr", [H, DK, T], BF16, kind=scr_kind).ap()
    x.k_scr = nc.dram_tensor("k_scr", [H, DK, T], BF16, kind=scr_kind).ap()
    x.e_scr = nc.dram_tensor("e_scr", [H, M, T], BF16, kind=scr_kind).ap()
    x.v_scr = nc.dram_tensor("v_scr", [H, DV, T], BF16, kind=scr_kind).ap()
    x.g_scr = nc.dram_tensor("g_scr", [H, DV, T], BF16, kind=scr_kind).ap()
    x.x2_scr = nc.dram_tensor("x2_scr", [NDC, 128, T], F32, kind=scr_kind).ap()
    x.hid_scr = nc.dram_tensor("hid_scr", [NHC, 128, T], BF16, kind=scr_kind).ap()
    x.xln2_scr = nc.dram_tensor("xln2_scr", [NDC, 128, T], BF16, kind=scr_kind).ap()
    x.seg_d = nc.dram_tensor("seg_d", [64, 4096], F32).ap()
    if _KDBG:
        x.dbg_denr = nc.dram_tensor("dbg_denr", [2, 128, 1024], F32, kind="ExternalOutput").ap()
        x.dbg_og = nc.dram_tensor("dbg_og", [128, H * 1024], mybir.dt.bfloat16, kind="ExternalOutput").ap()
        x.dbg_ov = nc.dram_tensor("dbg_ov", [128, H * DV], F32, kind="ExternalOutput").ap()
        x.dbg_pvt = nc.dram_tensor("dbg_pvt", [128, H * M], mybir.dt.bfloat16, kind="ExternalOutput").ap()
        x.dbg_rs = nc.dram_tensor("dbg_rs", [128, 16], F32, kind="ExternalOutput").ap()
        x.dbg_pvf = nc.dram_tensor("dbg_pvf", [64, H * C], mybir.dt.bfloat16, kind="ExternalOutput").ap()
        x.dbg_w2 = nc.dram_tensor("dbg_w2", [128, H * C], mybir.dt.bfloat16, kind="ExternalOutput").ap()
    x.gath_d = nc.dram_tensor("gath_d", [NCORES * 64, 4096], F32).ap()

    x.e_rows = x.e_scr.rearrange("h m t -> (h m) t")
    x.k_rows = x.k_scr.rearrange("h m t -> (h m) t")
    x.v_rows = x.v_scr.rearrange("h m t -> (h m) t")

    from contextlib import ExitStack
    es = ExitStack()
    with es:
        x.cst = es.enter_context(nc.sbuf_tensor("cst_sb", [128, len(CONSTS)], F32))
        x.m1u = es.enter_context(nc.sbuf_tensor("m1u_sb", [128, 128], BF16))
        x.m1u8 = es.enter_context(nc.sbuf_tensor("m1u8_sb", [128, 8 * 128], BF16))
        x.ident = es.enter_context(nc.sbuf_tensor("ident_sb", [128, 128], BF16))
        x.gnw_sb = es.enter_context(nc.sbuf_tensor("gnw_sb", [128, 1], F32))
        x.mask_sb = es.enter_context(nc.sbuf_tensor("mask_sb", [128, NCORES], F32))
        x.onescol = es.enter_context(nc.sbuf_tensor("onescol_sb", [128, 1], BF16))
        x.onesrow = es.enter_context(nc.sbuf_tensor("onesrow_sb", [1, 128], F32))
        x.sg_sc = es.enter_context(nc.sbuf_tensor("sg_sc", [128, NHC], F32))
        x.su_sc = es.enter_context(nc.sbuf_tensor("su_sc", [128, NHC], F32))
        x.sd_sc = es.enter_context(nc.sbuf_tensor("sd_sc", [128, NDC], F32))

        for i, vv in enumerate(CONSTS):
            nc.const_aps.register(F32, vv, x.cst[:, i:i + 1])

        x.s_ld = Sem(es.enter_context(nc.semaphore("s_ld")))
        x.s_wr = Sem(es.enter_context(nc.semaphore("s_wr")))
        x.s_gd = Sem(es.enter_context(nc.semaphore("s_gd")))
        x.s_cc = Sem(es.enter_context(nc.semaphore("s_cc")))
        x.s_pe = Sem(es.enter_context(nc.semaphore("s_pe")))
        x.s_v = Sem(es.enter_context(nc.semaphore("s_v")))
        x.s_s = Sem(es.enter_context(nc.semaphore("s_s")))
        x.s_g = Sem(es.enter_context(nc.semaphore("s_g")))

        with nc.Block() as blk:
            @blk.sync
            def _(sy):
                for dst, src in [(x.cst, x.cst_i), (x.m1u, x.m1u_i), (x.m1u8, x.m1u8_i),
                                 (x.ident, x.ident_i), (x.gnw_sb, x.gnw),
                                 (x.mask_sb, x.mask_i), (x.onescol, x.onescol_i),
                                 (x.onesrow, x.onesrow_i), (x.sg_sc, x.sgate),
                                 (x.su_sc, x.sup), (x.sd_sc, x.sdown)]:
                    x.s_ld.inc(sy.dma_start(out=dst[:, :], in_=src), 16)

            @blk.vector
            def _(v):
                v.wait_ge(x.s_ld.h, x.s_ld.n)

        if _upto("b1"):
            with nc.sbuf_tensor("xln", [128, NDC * T], BF16) as _xln_flat, \
                 nc.Block() as blk:
                xln = _xln_flat
                _emit_ln(x, blk, x.x_fm, xln, "ln1")
                _emit_proj(x, blk, xln)

        if _upto("b2"):
            with nc.sbuf_tensor("acc", [64, 4096], F32) as acc, \
                 nc.sbuf_tensor("hkb2", [64, 2 * H * M], BF16) as hkb2, \
                 nc.sbuf_tensor("hvb2", [64, 2 * H * DV], BF16) as hvb2, \
                 nc.sbuf_tensor("wsnap", [1, 2 * H * M], F32) as wsnap:
                x.acc, x.hkb2, x.hvb2, x.wsnap = acc, hkb2, hvb2, wsnap
                with nc.Block() as blk:
                    _emit_states_exchange(x, blk)
                if _upto("b3"):
                    with nc.Block() as blk:
                        _emit_attn(x, blk)

        if _upto("b4"):
            with nc.sbuf_tensor("xln2", [128, NDC * T], BF16) as xln2:
                x.xln2 = xln2
                with nc.Block() as blk:
                    _emit_ln(x, blk, x.x2_scr.rearrange("a p t -> (a p) t"), x.xln2, "ln2")

                    @blk.scalar
                    def _(sc):
                        sc.wait_ge(x.s_v.h, x.s_v.n)
                        sc.wait_ge(x.s_g.h, x.s_g.n)
                        for dc in range(NDC):
                            x.s_wr.inc(sc.dma_start(
                                out=x.xln2_scr[dc],
                                in_=_v3(x.xln2, NDC, T)[:, dc, :]), 16)
            if _upto("b5"):
                with nc.Block() as blk:
                    _emit_ffn_fused(x, blk)

        with nc.Block() as blk:
            @blk.sync
            def _(sy):
                sy.wait_ge(x.s_wr.h, x.s_wr.n)
                sy.wait_ge(x.s_ld.h, x.s_ld.n)
                sy.wait_ge(x.s_gd.h, x.s_gd.n)

    return nc


def _v3(t2d, a, b):
    return t2d[:, :].rearrange("p (a b) -> p a b", a=a)


def _emit_ln(x, blk, src_fm, dst, tag):
    nc = x.nc
    G2 = 256
    NG2 = T // G2
    src3 = src_fm.rearrange("(a p) t -> p a t", p=128)
    dst3 = _v3(dst, NDC, T)
    with nc.sbuf_tensor(f"{tag}_xg", [128, 2 * NDC * G2], F32) as xg4, \
         nc.sbuf_tensor(f"{tag}_xb", [128, 2 * NDC * G2], BF16) as xb4, \
         nc.sbuf_tensor(f"{tag}_sq", [128, 2 * NDC * G2], BF16) as sq4, \
         nc.sbuf_tensor(f"{tag}_rows", [1, 2 * 6 * G2], F32) as rows, \
         nc.sbuf_tensor(f"{tag}_bc", [128, 2 * 2 * G2], F32) as bc4, \
         nc.psum_tensor(f"{tag}_psS", [1, 2 * G2], F32) as psS, \
         nc.psum_tensor(f"{tag}_psQ", [1, 2 * G2], F32) as psQ, \
         nc.psum_tensor(f"{tag}_psB", [128, 4 * G2], F32) as psB:
        ev = {}

        def rec(key, sem):
            ev[key] = (sem, sem.n)

        def wt(eng, *keys):
            for k in keys:
                if k in ev:
                    sm, n = ev[k]
                    eng.wait_ge(sm.h, n)

        for g in range(NG2):
            p = g % 2
            xg2f = xg4[:, p * NDC * G2:(p + 1) * NDC * G2]
            xb2f = xb4[:, p * NDC * G2:(p + 1) * NDC * G2]
            sq2f = sq4[:, p * NDC * G2:(p + 1) * NDC * G2]
            xg = _v3(xg2f, NDC, G2)
            xb = _v3(xb2f, NDC, G2)
            sq = _v3(sq2f, NDC, G2)
            mu, msq, tt, var, rstd, mup = (
                rows[0:1, (6 * p + i) * G2:(6 * p + i + 1) * G2] for i in range(6))
            bc = bc4[:, p * 2 * G2:(p + 1) * 2 * G2]
            pS = psS[:, p * G2:(p + 1) * G2]
            pQ = psQ[:, p * G2:(p + 1) * G2]
            pB = psB[:, p * 2 * G2:(p + 1) * 2 * G2]

            @blk.sync
            def _(sy, g=g, p=p, xg=xg):
                if tag == "ln2" and g == 0:
                    sy.wait_ge(x.s_v.h, x.s_v.n)
                    sy.wait_ge(x.s_g.h, x.s_g.n)
                    sy.wait_ge(x.s_wr.h, x.s_wr.n)
                wt(sy, f"applyV{p}", f"applyG{p}")
                x.s_ld.inc(sy.dma_start(out=xg[:, :, :],
                                        in_=src3[:, :, g * G2:(g + 1) * G2]), 16)
            rec(f"ld{p}", x.s_ld)

            @blk.vector
            def _(v, g=g, p=p, xg2f=xg2f, xb2f=xb2f, sq2f=sq2f):
                wt(v, f"ld{p}", f"stat{p}")
                v.tensor_copy(xb2f, xg2f)
                x.s_v.inc(v.tensor_tensor(sq2f, xb2f, xb2f, ALU.mult))
            rec(f"xbw{p}", x.s_v)

            @blk.tensor
            def _(t, g=g, p=p, xb=xb, sq=sq, pS=pS, pQ=pQ):
                wt(t, f"xbw{p}", f"muv{p}")
                for dc in range(NDC):
                    t.matmul(pS, x.onescol[:, :], xb[:, dc, :],
                             start=(dc == 0), stop=(dc == NDC - 1))
                for dc in range(NDC):
                    ins = t.matmul(pQ, x.onescol[:, :], sq[:, dc, :],
                                   start=(dc == 0), stop=(dc == NDC - 1))
                x.s_pe.inc(ins)
            rec(f"stat{p}", x.s_pe)

            @blk.vector
            def _(v, g=g, p=p, pS=pS, pQ=pQ, mu=mu, msq=msq, tt=tt, var=var):
                wt(v, f"stat{p}", f"rstd{p}")
                v.tensor_scalar(mu, pS, 1.0 / D, None, ALU.mult)
                v.tensor_scalar(msq, pQ, 1.0 / D, None, ALU.mult)
                v.tensor_tensor(tt, mu, mu, ALU.mult)
                x.s_v.inc(v.tensor_tensor(var, msq, tt, ALU.subtract))
            rec(f"muv{p}", x.s_v)

            @blk.scalar
            def _(sc, g=g, p=p, tt=tt, var=var, rstd=rstd):
                wt(sc, f"muv{p}", f"bpe{p}")
                sc.activation(tt, var, AF.Ln, bias=1e-5, scale=1.0)
                x.s_s.inc(sc.activation(rstd, tt, AF.Exp, bias=0.0, scale=-0.5))
            rec(f"rstd{p}", x.s_s)

            @blk.vector
            def _(v, g=g, p=p, mu=mu, mup=mup, rstd=rstd):
                wt(v, f"rstd{p}", f"bpe{p}")
                x.s_v.inc(v.scalar_tensor_tensor(mup, mu, -1.0, rstd, ALU.mult, ALU.mult))
            rec(f"mup{p}", x.s_v)

            @blk.tensor
            def _(t, g=g, p=p, rstd=rstd, mup=mup, pB=pB):
                wt(t, f"mup{p}", f"bcc{p}")
                t.matmul(pB[:, 0:G2], x.onesrow[:, :], rstd, start=True, stop=True)
                ins = t.matmul(pB[:, G2:2 * G2], x.onesrow[:, :], mup, start=True, stop=True)
                x.s_pe.inc(ins)
            rec(f"bpe{p}", x.s_pe)

            @blk.vector
            def _(v, g=g, p=p, pB=pB, bc=bc):
                wt(v, f"bpe{p}", f"applyV{p}", f"applyG{p}")
                x.s_v.inc(v.tensor_copy(bc, pB))
            rec(f"bcc{p}", x.s_v)

            NPOOL = 6

            @blk.gpsimd
            def _(gp, g=g, p=p, xg=xg, bc=bc):
                wt(gp, f"bcc{p}")
                for dc in range(NDC - NPOOL, NDC):
                    gp.tensor_tensor(xg[:, dc, :], xg[:, dc, :], bc[:, 0:G2], ALU.mult)
                    ins = gp.tensor_tensor(dst3[:, dc, g * G2:(g + 1) * G2],
                                           xg[:, dc, :], bc[:, G2:2 * G2], ALU.add)
                x.s_g.inc(ins)
            rec(f"applyG{p}", x.s_g)

            @blk.vector
            def _(v, g=g, p=p, xg=xg, bc=bc):
                for dc in range(NDC - NPOOL):
                    v.tensor_tensor(xg[:, dc, :], xg[:, dc, :], bc[:, 0:G2], ALU.mult)
                    ins = v.tensor_tensor(dst3[:, dc, g * G2:(g + 1) * G2],
                                          xg[:, dc, :], bc[:, G2:2 * G2], ALU.add)
                x.s_v.inc(ins)
            rec(f"applyV{p}", x.s_v)


def _emit_proj(x, blk, xln):
    nc = x.nc
    xln3 = _v3(xln, NDC, T)
    jobs = []
    for kind, ncc, scr in [("q", 8, x.q_scr), ("k", 8, x.k_scr),
                           ("s", 8, x.e_scr), ("v", 16, x.v_scr),
                           ("g", 16, x.g_scr)]:
        for cc in range(ncc):
            jobs.append((kind, cc, scr))

    with nc.sbuf_tensor("pj_slab", [128, 2 * NDC * 128], BF16) as slab, \
         nc.sbuf_tensor("pj_ev", [128, 2 * T], BF16) as ev, \
         nc.sbuf_tensor("pj_sf", [128, T], F32) as sf, \
         nc.psum_tensor("pj_psA", [128, T], F32) as psA, \
         nc.psum_tensor("pj_psB", [128, T], F32) as psB:
        pss = [psA, psB]
        pe_evt, ev_evt, wr_evt, ld_evt = [], [], [], []

        for j, (kind, cc, scr) in enumerate(jobs):
            ps = pss[j % 2]
            slab_b = slab[:, (j % 2) * NDC * 128:((j % 2) + 1) * NDC * 128]
            sl3 = slab_b.rearrange("p (a c) -> p a c", a=NDC)
            ev_b = ev[:, (j % 2) * T:((j % 2) + 1) * T]

            @blk.gpsimd
            def _(gp, j=j, slab_b=slab_b):
                if j >= 2:
                    gp.wait_ge(x.s_pe.h, pe_evt[j - 2])
                ins = gp.dma_start(out=slab_b, in_=x.wpj[j])
                x.s_gd.inc(ins, 16)
            ld_evt.append(x.s_gd.n)

            @blk.tensor
            def _(t, j=j, ps=ps, sl3=sl3):
                t.wait_ge(x.s_gd.h, ld_evt[j])
                if j >= 2:
                    t.wait_ge(x.s_v.h, ev_evt[j - 2][0])
                    t.wait_ge(x.s_s.h, ev_evt[j - 2][1])
                if j == 0:
                    t.wait_ge(x.s_v.h, x.s_v.n)
                    t.wait_ge(x.s_g.h, x.s_g.n)
                for g in range(NG):
                    for dc in range(NDC):
                        ins = t.matmul(ps[:, g * G:(g + 1) * G], sl3[:, dc, :],
                                       xln3[:, dc, g * G:(g + 1) * G],
                                       start=(dc == 0), stop=(dc == NDC - 1))
                x.s_pe.inc(ins)
            pe_evt.append(x.s_pe.n)

            if kind in ("k", "v"):
                @blk.vector
                def _(v, j=j, ps=ps, ev_b=ev_b):
                    v.wait_ge(x.s_pe.h, pe_evt[j])
                    if j >= 2:
                        v.wait_ge(x.s_wr.h, wr_evt[j - 2])
                    x.s_v.inc(v.tensor_copy(ev_b, ps[:, :]))
            elif kind == "g":
                @blk.scalar
                def _(sc, j=j, ps=ps, ev_b=ev_b):
                    sc.wait_ge(x.s_pe.h, pe_evt[j])
                    if j >= 2:
                        sc.wait_ge(x.s_wr.h, wr_evt[j - 2])
                    x.s_s.inc(sc.activation(ev_b, ps[:, :], AF.Silu,
                                            bias=0.0, scale=1.0))
            elif kind == "q":
                @blk.scalar
                def _(sc, j=j, ps=ps, ev_b=ev_b):
                    sc.wait_ge(x.s_pe.h, pe_evt[j])
                    if j >= 2:
                        sc.wait_ge(x.s_wr.h, wr_evt[j - 2])
                    x.s_s.inc(sc.activation(ev_b, ps[:, :], AF.Copy,
                                            bias=0.0, scale=0.125))
            else:
                @blk.vector
                def _(v, j=j, ps=ps):
                    v.wait_ge(x.s_pe.h, pe_evt[j])
                    v.wait_ge(x.s_s.h, x.s_s.n)
                    x.s_v.inc(v.tensor_scalar(sf[:, :], ps[:, :], 32.0, -32.0,
                                              ALU.min, ALU.max))

                @blk.scalar
                def _(sc, j=j, ev_b=ev_b):
                    sc.wait_ge(x.s_v.h, x.s_v.n)
                    if j >= 2:
                        sc.wait_ge(x.s_wr.h, wr_evt[j - 2])
                    x.s_s.inc(sc.activation(ev_b, sf[:, :], AF.Exp,
                                            bias=-32.0, scale=1.0))
            ev_evt.append((x.s_v.n, x.s_s.n))

            @blk.scalar
            def _(sc, j=j, cc=cc, scr=scr, ev_b=ev_b):
                sc.wait_ge(x.s_v.h, ev_evt[j][0])
                sc.wait_ge(x.s_s.h, ev_evt[j][1])
                if scr.shape[1] == 64:
                    x.s_wr.inc(sc.dma_start(out=scr[2 * cc], in_=ev_b[0:64, :]), 16)
                    x.s_wr.inc(sc.dma_start(out=scr[2 * cc + 1], in_=ev_b[64:128, :]), 16)
                else:
                    x.s_wr.inc(sc.dma_start(out=scr[cc], in_=ev_b[:, :]), 16)
            wr_evt.append(x.s_wr.n)
        x.pj_wr_evt = wr_evt


def _emit_states_exchange(x, blk):
    nc = x.nc
    with nc.sbuf_tensor("seg", [64, 4096], F32) as seg, \
         nc.sbuf_tensor("b2_etm", [128, 2 * H * M], BF16) as etm2, \
         nc.sbuf_tensor("b2_ktm", [128, 2 * H * DK], BF16) as ktm2, \
         nc.sbuf_tensor("b2_vtm", [128, 2 * H * DV], BF16) as vtm2, \
         nc.sbuf_tensor("b2_slab", [64, 2 * 4096], F32) as slabr, \
         nc.psum_tensor("b2_psF", [64, H * M], F32) as psF, \
         nc.psum_tensor("b2_psG1", [64, 8 * DV], F32) as psG1, \
         nc.psum_tensor("b2_psG2", [64, 8 * DV], F32) as psG2, \
         nc.psum_tensor("b2_psH", [1, H * M], F32) as psH:

        b2pe, b2ld = [], []
        for c in range(NCH):
            b = c % 2
            etm = etm2[:, b * H * M:(b + 1) * H * M]
            ktm = ktm2[:, b * H * DK:(b + 1) * H * DK]
            vtm = vtm2[:, b * H * DV:(b + 1) * H * DV]

            @blk.sync
            def _(sy, c=c, etm=etm, ktm=ktm, vtm=vtm):
                if c >= 2:
                    sy.wait_ge(x.s_pe.h, b2pe[c - 2])
                else:
                    sy.wait_ge(x.s_wr.h, x.pj_wr_evt[39])
                x.s_ld.inc(sy.dma_start(out=etm, in_=x.e_rows[:, c * C:(c + 1) * C],
                                        transpose=True), 16)
                x.s_ld.inc(sy.dma_start(out=ktm, in_=x.k_rows[:, c * C:(c + 1) * C],
                                        transpose=True), 16)
                x.s_ld.inc(sy.dma_start(out=vtm, in_=x.v_rows[:, c * C:(c + 1) * C],
                                        transpose=True), 16)
            b2ld.append(x.s_ld.n)

            @blk.tensor
            def _(t, c=c, etm=etm, ktm=ktm, vtm=vtm):
                t.wait_ge(x.s_ld.h, b2ld[c])
                if c == 0:
                    t.wait_ge(x.s_v.h, x.s_v.n)
                    t.wait_ge(x.s_s.h, x.s_s.n)
                st, sp = c == 0, c == NCH - 1
                t.matmul(psH[0:1, 0:512], x.onescol[:, :], etm[:, 0:512],
                         start=st, stop=sp, skip_group_check=True)
                t.matmul(psH[0:1, 512:1024], x.onescol[:, :], etm[:, 512:1024],
                         start=st, stop=sp, skip_group_check=True)
                for h in range(H):
                    t.matmul(psF[0:64, h * M:(h + 1) * M], ktm[:, h * DK:(h + 1) * DK],
                             etm[:, h * M:(h + 1) * M], start=st, stop=sp,
                             skip_group_check=True)
                for h in range(H):
                    ps = psG1 if h < 8 else psG2
                    ins = t.matmul(ps[0:64, (h % 8) * DV:((h % 8) + 1) * DV],
                                   etm[:, h * M:(h + 1) * M],
                                   vtm[:, h * DV:(h + 1) * DV], start=st, stop=sp,
                                   skip_group_check=True)
                x.s_pe.inc(ins)
            b2pe.append(x.s_pe.n)

        @blk.vector
        def _(v):
            v.wait_ge(x.s_pe.h, b2pe[NCH - 1])
            v.tensor_copy(seg[:, 0:H * M], psF[:, :])
            v.tensor_copy(seg[:, 1024:2048], psG1[:, :])
            v.tensor_copy(seg[:, 2048:3072], psG2[:, :])
            x.s_v.inc(v.tensor_copy(seg[0:1, 3072:4096], psH[:, :]))

        @blk.gpsimd
        def _(g):
            g.wait_ge(x.s_v.h, x.s_v.n)
            x.s_gd.inc(g.dma_start(out=x.seg_d, in_=seg[:, :]), 16)
            g.wait_ge(x.s_gd.h, x.s_gd.n)
            x.s_cc.inc(g.collective_compute(
                "AllGather", ALU.bypass,
                replica_groups=[list(range(NCORES))],
                ins=[x.seg_d.opt()],
                outs=[x.gath_d.opt()]))
            g.wait_ge(x.s_cc.h, x.s_cc.n)

        @blk.vector
        def _(v):
            x.s_v.inc(v.memset(x.acc[:, :], 0.0))

        for r in range(NCORES):
            sb = slabr[:, (r % 2) * 4096:((r % 2) + 1) * 4096]

            @blk.gpsimd
            def _(g, r=r, sb=sb):
                if r >= 2:
                    g.wait_ge(x.s_v.h, x.s_v.n)
                x.s_gd.inc(g.dma_start(out=sb, in_=x.gath_d[r * 64:(r + 1) * 64, :]), 16)

            @blk.vector
            def _(v, r=r, sb=sb):
                v.wait_ge(x.s_gd.h, x.s_gd.n)
                x.s_v.inc(v.scalar_tensor_tensor(x.acc[:, :], sb,
                                                 x.mask_sb[0:64, r:r + 1], x.acc[:, :],
                                                 ALU.mult, ALU.add))

        @blk.vector
        def _(v):
            v.tensor_copy(x.hkb2[:, 0:H * M], x.acc[0:64, 0:H * M])
            v.tensor_copy(x.hvb2[:, 0:H * DV], x.acc[0:64, 1024:1024 + H * DV])
            x.s_v.inc(v.tensor_copy(x.wsnap[:, 0:H * M], x.acc[0:1, 3072:3072 + H * M]))


def _emit_attn(x, blk):
    nc = x.nc
    from contextlib import ExitStack
    with ExitStack() as aes:
        def sb(name, shape, dt):
            return aes.enter_context(nc.sbuf_tensor(name, shape, dt))

        qc = sb("at_qc", [64, 2 * H * C], BF16)
        kc = sb("at_kc", [64, 2 * H * C], BF16)
        efc = sb("at_efc", [64, 2 * H * C], BF16)
        gc = sb("at_gc", [128, 2 * H * C], BF16)
        etm2 = sb("at_etm", [128, 2 * H * M], BF16)
        ktm2 = sb("at_ktm", [128, 2 * H * DK], BF16)
        vtm2 = sb("at_vtm", [128, 2 * H * DV], BF16)
        denr = sb("at_denr", [128, 2 * H * M], F32)
        sqkm = sb("at_sqk", [128, H * C], BF16)
        w2m = sb("at_w2", [128, H * C], BF16)
        t1 = sb("at_t1", [128, H * M], F32)
        eo = sb("at_eo", [128, H * M], F32)
        rs = sb("at_rs", [128, 16], F32)
        rs2 = sb("at_rs2", [128, 16], F32)
        pvt = sb("at_pvt", [128, H * M], BF16)
        pvf = sb("at_pvf", [64, H * C], BF16)
        ovsb = sb("at_ov", [128, H * DV], F32)
        sq_sb = sb("at_sq", [128, H * DV], BF16)
        rrow = sb("at_rrow", [1, H * C], F32)
        rg_sb = sb("at_rg", [128, H * DV], F32)
        on_sb = sb("at_on", [128, H * DV], F32)
        og2 = sb("at_og", [128, H * 1024], BF16)
        wos = sb("at_wos", [128, 2 * NDC * 128], BF16)
        xt = sb("at_xt", [128, 2 * G], F32)
        x2t = sb("at_x2", [128, 2 * G], F32)
        P1 = aes.enter_context(nc.psum_tensor("at_P1", [128, 2048], F32))
        P2 = aes.enter_context(nc.psum_tensor("at_P2", [128, 1024], F32))
        P3b = aes.enter_context(nc.psum_tensor("at_P3", [128, 2048], BF16))
        P3f = P3b[:, :].bitcast(F32)

        og3 = og2[:, :].rearrange("p (h t) -> p h t", h=H)

        ev = {}

        def rec(key, sem):
            ev[key] = (sem, sem.n)

        def wt(eng, *keys):
            for k in keys:
                if k in ev:
                    sm, n = ev[k]
                    eng.wait_ge(sm.h, n)

        def tile_loads(c):
            p = c % 2
            sl = slice(c * C, (c + 1) * C)

            @blk.sync
            def _(sy, c=c, p=p, sl=sl):
                wt(sy, f"tilerd{p}")
                x.s_ld.inc(sy.dma_start(out=etm2[:, p * H * M:(p + 1) * H * M],
                                        in_=x.e_rows[:, sl], transpose=True), 16)
                x.s_ld.inc(sy.dma_start(out=ktm2[:, p * H * DK:(p + 1) * H * DK],
                                        in_=x.k_rows[:, sl], transpose=True), 16)
                x.s_ld.inc(sy.dma_start(out=vtm2[:, p * H * DV:(p + 1) * H * DV],
                                        in_=x.v_rows[:, sl], transpose=True), 16)

            @blk.gpsimd
            def _(gp, c=c, p=p, sl=sl):
                gp.wait_ge(x.s_wr.h, x.pj_wr_evt[-1])
                wt(gp, f"qkrd{p}")
                x.s_ld.inc(gp.dma_start(
                    out=qc[:, p * H * C:(p + 1) * H * C].rearrange("p (h t) -> p h t", h=H),
                    in_=x.q_scr.rearrange("h m t -> m h t")[:, :, sl]), 16)
                x.s_ld.inc(gp.dma_start(
                    out=kc[:, p * H * C:(p + 1) * H * C].rearrange("p (h t) -> p h t", h=H),
                    in_=x.k_scr.rearrange("h m t -> m h t")[:, :, sl]), 16)
                x.s_ld.inc(gp.dma_start(
                    out=efc[:, p * H * C:(p + 1) * H * C].rearrange("p (h t) -> p h t", h=H),
                    in_=x.e_scr.rearrange("h m t -> m h t")[:, :, sl]), 16)
                x.s_ld.inc(gp.dma_start(
                    out=gc[:, p * H * C:(p + 1) * H * C].rearrange("p (h t) -> p h t", h=H),
                    in_=x.g_scr.rearrange("h m t -> m h t")[:, :, sl]), 16)
            rec(f"tileld{p}", x.s_ld)

        def emit_denom(c):
            p = c % 2

            @blk.tensor
            def _(t, c=c, p=p):
                wt(t, "P3", f"tileld{p}", f"wsnap{p}")
                for half in range(2):
                    sl = slice(half * 512, (half + 1) * 512)
                    t.matmul(P3f[:, sl], x.m1u[:, :],
                             etm2[:, p * H * M + half * 512: p * H * M + half * 512 + 512],
                             start=True, stop=False)
                    ins = t.matmul(P3f[:, sl], x.onesrow[:, :],
                                   x.wsnap[:, p * H * M + half * 512: p * H * M + half * 512 + 512],
                                   start=False, stop=True)
                x.s_pe.inc(ins)
            rec("CPE", x.s_pe)

            @blk.scalar
            def _(sc, c=c, p=p):
                wt(sc, "CPE", "t1rd")
                sc.activation(t1[:, :], P3f[:, :], AF.Ln, bias=0.0, scale=1.0)
                x.s_s.inc(sc.activation(denr[:, p * H * M:(p + 1) * H * M],
                                        t1[:, :], AF.Exp, bias=0.0, scale=-1.0))
            rec("P3", x.s_s)
            rec(f"denr{p}", x.s_s)
            rec("t1wr", x.s_s)
            if _KDBG and c < 2:
                @blk.scalar
                def _(sc, c=c, p=p):
                    sc.wait_ge(x.s_s.h, x.s_s.n)
                    x.s_wr.inc(sc.dma_start(out=x.dbg_denr[c],
                                            in_=denr[:, p * H * M:(p + 1) * H * M]), 16)

        tile_loads(0)
        emit_denom(0)
        tile_loads(1)

        for c in range(NCH):
            p = c % 2
            np_ = 1 - p
            etm = etm2[:, p * H * M:(p + 1) * H * M]
            ktm = ktm2[:, p * H * DK:(p + 1) * H * DK]
            vtm = vtm2[:, p * H * DV:(p + 1) * H * DV]
            q3 = qc[:, p * H * C:(p + 1) * H * C]
            k3 = kc[:, p * H * C:(p + 1) * H * C]
            ef3 = efc[:, p * H * C:(p + 1) * H * C]
            g3 = gc[:, p * H * C:(p + 1) * H * C]
            dnr = denr[:, p * H * M:(p + 1) * H * M]
            acc_hk = x.acc[0:64, 0:H * M]
            acc_hv1 = x.acc[0:64, 1024:1024 + 8 * DV]
            acc_hv2 = x.acc[0:64, 1024 + 8 * DV:1024 + 16 * DV]
            cst = (c % 8) * C

            @blk.tensor
            def _(t, c=c, p=p, etm=etm, ktm=ktm, vtm=vtm):
                wt(t, f"tileld{p}", "P1free", "P2free", "P3", f"shadrd{1 - p}")
                for h in range(H):
                    t.matmul(P2[0:64, h * M:(h + 1) * M], ktm[:, h * DK:(h + 1) * DK],
                             etm[:, h * M:(h + 1) * M], start=True, stop=True)
                for h in range(H):
                    ps = P1[0:64, 0:1024] if h < 8 else P1[0:64, 1024:2048]
                    t.matmul(ps[:, (h % 8) * DV:((h % 8) + 1) * DV],
                             etm[:, h * M:(h + 1) * M],
                             vtm[:, h * DV:(h + 1) * DV], start=True, stop=True)
                t.matmul(P3f[0:1, 0:512], x.onescol[:, :], etm[:, 0:512],
                         start=True, stop=True)
                ins = t.matmul(P3f[0:1, 512:1024], x.onescol[:, :], etm[:, 512:1024],
                               start=True, stop=True)
                x.s_pe.inc(ins)
            rec("UPE", x.s_pe)

            @blk.vector
            def _(v, c=c, p=p, np_=np_):
                wt(v, "UPE")
                v.tensor_tensor(x.acc[0:64, 0:H * M], x.acc[0:64, 0:H * M],
                                P2[0:64, :], ALU.add)
                v.tensor_tensor(x.acc[0:64, 1024:1024 + 8 * DV],
                                x.acc[0:64, 1024:1024 + 8 * DV],
                                P1[0:64, 0:1024], ALU.add)
                v.tensor_tensor(x.acc[0:64, 1024 + 8 * DV:1024 + 16 * DV],
                                x.acc[0:64, 1024 + 8 * DV:1024 + 16 * DV],
                                P1[0:64, 1024:2048], ALU.add)
                v.tensor_tensor(x.wsnap[:, np_ * H * M:(np_ + 1) * H * M],
                                x.wsnap[:, p * H * M:(p + 1) * H * M],
                                P3f[0:1, 0:H * M], ALU.add)
                v.tensor_copy(x.hkb2[:, np_ * H * M:(np_ + 1) * H * M],
                              x.acc[0:64, 0:H * M])
                ins = v.tensor_copy(x.hvb2[:, np_ * H * DV:(np_ + 1) * H * DV],
                                    x.acc[0:64, 1024:1024 + H * DV])
                x.s_v.inc(ins)
            rec("UV", x.s_v)
            rec(f"wsnap{np_}", x.s_v)
            rec("P2free_afterF", x.s_v)

            @blk.tensor
            def _(t, c=c, p=p, k3=k3, q3=q3):
                wt(t, "UV", f"qkld{p}")
                for h in range(H):
                    ins = t.matmul(P1[:, h * C:(h + 1) * C],
                                   k3[:, h * C:(h + 1) * C], q3[:, h * C:(h + 1) * C],
                                   start=True, stop=True)
                x.s_pe.inc(ins)
            rec("APE", x.s_pe)

            @blk.vector
            def _(v, c=c):
                wt(v, "APE", "sqkmrd")
                v.tensor_tensor(sqkm[:, 0:1024], P1[:, 0:1024], x.m1u8[:, :], ALU.mult)
                ins = v.tensor_tensor(sqkm[:, 1024:2048], P1[:, 1024:2048],
                                      x.m1u8[:, :], ALU.mult)
                x.s_v.inc(ins)
            rec("sqkmwr", x.s_v)
            rec("P1free", x.s_v)

            @blk.tensor
            def _(t, c=c, p=p, etm=etm, q3=q3):
                wt(t, "sqkmwr", "P2free_afterF")
                for h in range(H):
                    t.matmul(P2[:, h * M:(h + 1) * M], sqkm[:, h * C:(h + 1) * C],
                             etm[:, h * M:(h + 1) * M], start=True, stop=False)
                    ins = t.matmul(P2[:, h * M:(h + 1) * M], q3[:, h * C:(h + 1) * C],
                                   x.hkb2[:, p * H * M + h * M: p * H * M + (h + 1) * M],
                                   start=False, stop=True)
                x.s_pe.inc(ins)
            rec("BPE", x.s_pe)
            rec("sqkmrd", x.s_pe)

            @blk.vector
            def _(v, c=c, p=p, dnr=dnr):
                wt(v, "BPE", f"denr{p}", "t1wr")
                x.s_v.inc(v.tensor_tensor(t1[:, :], P2[:, :], dnr, ALU.mult))
            rec("okV", x.s_v)

            @blk.scalar
            def _(sc, c=c):
                wt(sc, "okV", "eord")
                x.s_s.inc(sc.activation(eo[:, :], t1[:, :], AF.Exp, bias=0.0, scale=1.0))
            rec("eoS", x.s_s)
            rec("t1rd", x.s_s)

            @blk.vector
            def _(v, c=c, p=p, dnr=dnr):
                wt(v, "eoS", "pvtrd")
                v.tensor_reduce(rs2[:, :], eo[:, :].rearrange("p (i m) -> p i m", i=H),
                                AX, ALU.add)
                v.reciprocal(rs[:, 0:8], rs2[:, 0:8])
                v.reciprocal(rs[:, 8:16], rs2[:, 8:16])
                v.tensor_tensor(t1[:, :], eo[:, :], dnr, ALU.mult)
                ins = v.tensor_tensor(
                    pvt[:, :].rearrange("p (i m) -> p i m", i=H),
                    t1[:, :].rearrange("p (i m) -> p i m", i=H),
                    rs[:, :].rearrange("p (i o) -> p i o", o=1).broadcast_to([128, H, M]),
                    ALU.mult)
                x.s_v.inc(ins)
            rec("pvtwr", x.s_v)
            rec("P2free", x.s_v)
            rec("eord", x.s_v)

            if _KDBG and c == 0:
                @blk.scalar
                def _(sc):
                    sc.wait_ge(x.s_v.h, x.s_v.n)
                    x.s_wr.inc(sc.dma_start(out=x.dbg_pvt, in_=pvt[:, :]), 16)
                    x.s_wr.inc(sc.dma_start(out=x.dbg_rs, in_=rs[:, :]), 16)

            @blk.tensor
            def _(t, c=c):
                wt(t, "pvtwr", "P3")
                for i in range(H):
                    ins = t.transpose(P3b[0:64, i * C:(i + 1) * C],
                                      pvt[:, i * M:(i + 1) * M], x.ident[:, :])
                x.s_pe.inc(ins)
            rec("TPE", x.s_pe)
            rec("pvtrd", x.s_pe)

            @blk.vector
            def _(v, c=c):
                wt(v, "TPE", "pvfrd")
                x.s_v.inc(v.tensor_copy(pvf[:, :], P3b[0:64, :]))
            rec("pvfwr", x.s_v)
            rec("P3", x.s_v)

            if _KDBG and c == 0:
                @blk.scalar
                def _(sc):
                    sc.wait_ge(x.s_v.h, x.s_v.n)
                    x.s_wr.inc(sc.dma_start(out=x.dbg_pvf, in_=pvf[:, :]), 16)

            @blk.tensor
            def _(t, c=c, ef3=ef3):
                wt(t, "pvfwr", "P1free")
                for h in range(H):
                    ins = t.matmul(P1[:, h * C:(h + 1) * C], ef3[:, h * C:(h + 1) * C],
                                   pvf[:, h * C:(h + 1) * C], start=True, stop=True)
                x.s_pe.inc(ins)
            rec("DPE", x.s_pe)

            @blk.vector
            def _(v, c=c):
                wt(v, "DPE", "w2mrd")
                v.tensor_tensor(w2m[:, 0:1024], P1[:, 0:1024], x.m1u8[:, :], ALU.mult)
                ins = v.tensor_tensor(w2m[:, 1024:2048], P1[:, 1024:2048],
                                      x.m1u8[:, :], ALU.mult)
                x.s_v.inc(ins)
            rec("w2mwr", x.s_v)
            rec("P1free", x.s_v)

            if _KDBG and c == 0:
                @blk.scalar
                def _(sc):
                    sc.wait_ge(x.s_v.h, x.s_v.n)
                    x.s_wr.inc(sc.dma_start(out=x.dbg_w2, in_=w2m[:, :]), 16)

            @blk.tensor
            def _(t, c=c, p=p, vtm=vtm):
                wt(t, "w2mwr", "P1free", "ovrd")
                for h in range(H):
                    t.matmul(P1[:, h * C:(h + 1) * C], vtm[:, h * DV:(h + 1) * DV],
                             w2m[:, h * C:(h + 1) * C], start=True, stop=False)
                    ins = t.matmul(P1[:, h * C:(h + 1) * C],
                                   x.hvb2[:, p * H * DV + h * DV: p * H * DV + (h + 1) * DV],
                                   pvf[:, h * C:(h + 1) * C], start=False, stop=True)
                x.s_pe.inc(ins)
            rec("EPE", x.s_pe)
            rec("w2mrd", x.s_pe)
            rec("pvfrd", x.s_pe)
            rec(f"tilerd{p}", x.s_pe)
            rec(f"shadrd{p}", x.s_pe)

            @blk.vector
            def _(v, c=c):
                wt(v, "EPE")
                x.s_v.inc(v.tensor_copy(ovsb[:, :], P1[:, :]))
            rec("ovwr", x.s_v)
            rec("P1free", x.s_v)

            if _KDBG and c == 0:
                @blk.scalar
                def _(sc):
                    sc.wait_ge(x.s_v.h, x.s_v.n)
                    x.s_wr.inc(sc.dma_start(out=x.dbg_ov, in_=ovsb[:, :]), 16)

            @blk.scalar
            def _(sc, c=c):
                wt(sc, "ovwr", "sqrd")
                x.s_s.inc(sc.activation(sq_sb[:, :], ovsb[:, :], AF.Square,
                                        bias=0.0, scale=1.0))
            rec("sqwr", x.s_s)

            @blk.tensor
            def _(t, c=c):
                wt(t, "sqwr", "P2free", "P3")
                for i in range(8):
                    t.matmul(P2[0:1, i * C:(i + 1) * C], x.onescol[:, :],
                             sq_sb[:, i * DV:(i + 1) * DV], start=True, stop=True)
                for i in range(8):
                    ins = t.matmul(P3f[0:1, i * 128:(i + 1) * 128], x.onescol[:, :],
                                   sq_sb[:, (8 + i) * DV:(9 + i) * DV],
                                   start=True, stop=True)
                x.s_pe.inc(ins)
            rec("RPE", x.s_pe)
            rec("sqrd", x.s_pe)

            @blk.scalar
            def _(sc, c=c):
                wt(sc, "RPE")
                sc.activation(rrow[:, 0:1024], P2[0:1, :], AF.Ln, bias=1e-5, scale=1.0 / DV)
                sc.activation(rrow[:, 1024:2048], P3f[0:1, :], AF.Ln, bias=1e-5, scale=1.0 / DV)
                x.s_s.inc(sc.activation(rrow[:, :], rrow[:, :], AF.Exp, bias=0.0, scale=-0.5))
            rec("rrowS", x.s_s)
            rec("P2free", x.s_s)
            rec("P3", x.s_s)

            @blk.tensor
            def _(t, c=c):
                wt(t, "rrowS", "P1free")
                for i in range(H):
                    ins = t.matmul(P1[:, i * C:(i + 1) * C], x.onesrow[:, :],
                                   rrow[:, i * C:(i + 1) * C], start=True, stop=True)
                x.s_pe.inc(ins)
            rec("RBPE", x.s_pe)

            @blk.vector
            def _(v, c=c, g3=g3, cst=cst):
                wt(v, "RBPE", "rgrd", "ogrd")
                v.tensor_scalar(rg_sb[:, :], P1[:, :], x.gnw_sb[:, 0:1], None, ALU.mult)
                v.tensor_tensor(on_sb[:, :], ovsb[:, :], rg_sb[:, :], ALU.mult)
                ins = v.tensor_tensor(
                    og3[:, :, cst:cst + C],
                    on_sb[:, :].rearrange("p (h t) -> p h t", h=H),
                    g3.rearrange("p (h t) -> p h t", h=H), ALU.mult)
                x.s_v.inc(ins)
            rec("ogV", x.s_v)
            rec("ovrd", x.s_v)
            rec("rgrd", x.s_v)
            rec("P1free", x.s_v)
            rec(f"qkld{p}", x.s_v)
            rec(f"qkrd{p}", x.s_v)

            if _KDBG and c == 7:
                @blk.scalar
                def _(sc):
                    sc.wait_ge(x.s_v.h, x.s_v.n)
                    x.s_wr.inc(sc.dma_start(out=x.dbg_og, in_=og2[:, :]), 16)

            if c + 2 < NCH:
                tile_loads(c + 2)

            if c + 1 < NCH:
                emit_denom(c + 1)

            if c % 8 == 7:
                _emit_wo(x, blk, c // 8, og3, wos, xt, x2t, P1, P2, rec, wt)


def _emit_wo(x, blk, sup, og3, wos, xt, x2t, P1, P2, rec, wt):
    nc = x.nc
    pe_evt = []
    for dc in range(NDC):
        bb = dc % 2
        wslab = wos[:, bb * NDC * 128:(bb + 1) * NDC * 128]
        slw = wslab.rearrange("p (a c) -> p a c", a=NDC)

        @blk.gpsimd
        def _(gp, dc=dc, wslab=wslab):
            if dc >= 2:
                gp.wait_ge(x.s_pe.h, pe_evt[dc - 2])
            else:
                gp.wait_ge(x.s_pe.h, x.s_pe.n)
            ins = gp.dma_start(out=wslab, in_=x.wo_t[dc])
            x.s_gd.inc(ins, 16)

        for g2 in range(2):
            tok0 = sup * 1024 + g2 * G
            xtb = xt[:, bb * G:(bb + 1) * G]
            x2b = x2t[:, bb * G:(bb + 1) * G]
            psl = P1[:, (bb * 2 + g2) * G % 2048:((bb * 2 + g2) * G % 2048) + G]

            @blk.sync
            def _(sy, xtb=xtb, dc=dc, tok0=tok0):
                sy.wait_ge(x.s_v.h, x.s_v.n)
                x.s_ld.inc(sy.dma_start(out=xtb,
                                        in_=x.x_fm[dc * 128:(dc + 1) * 128,
                                                   tok0:tok0 + G]), 16)

            @blk.tensor
            def _(t, dc=dc, g2=g2, psl=psl, slw=slw):
                t.wait_ge(x.s_gd.h, x.s_gd.n)
                if dc < 2 and g2 == 0:
                    wt(t, "P1free", "ogV")
                t.wait_ge(x.s_v.h, x.s_v.n)
                for vd in range(H):
                    ins = t.matmul(psl, slw[:, vd, :],
                                   og3[:, vd, g2 * G:(g2 + 1) * G],
                                   start=(vd == 0), stop=(vd == H - 1))
                x.s_pe.inc(ins)
            if g2 == 1:
                pe_evt.append(x.s_pe.n)

            @blk.vector
            def _(v, dc=dc, g2=g2, psl=psl, xtb=xtb, x2b=x2b):
                v.wait_ge(x.s_pe.h, x.s_pe.n)
                v.wait_ge(x.s_ld.h, x.s_ld.n)
                v.wait_ge(x.s_wr.h, x.s_wr.n)
                x.s_v.inc(v.tensor_tensor(x2b, psl, xtb, ALU.add))

            @blk.scalar
            def _(sc, dc=dc, x2b=x2b, tok0=tok0):
                sc.wait_ge(x.s_v.h, x.s_v.n)
                x.s_wr.inc(sc.dma_start(out=x.x2_scr[dc, :, tok0:tok0 + G],
                                        in_=x2b), 16)
    rec("P1free", x.s_v)
    rec("ogrd", x.s_pe)


def _emit_ffn_fused(x, blk):
    nc = x.nc
    GT = 1024
    with nc.sbuf_tensor("ff_x", [128, NDC * GT], BF16) as xg2, \
         nc.sbuf_tensor("ff_hid", [128, NHC * GT], BF16) as hid2, \
         nc.sbuf_tensor("ff_sg", [128, 2 * NDC * 128], BF16) as slg, \
         nc.sbuf_tensor("ff_su", [128, 2 * NDC * 128], BF16) as slu, \
         nc.sbuf_tensor("ff_wd", [128, 2 * NHC * 128], BF16) as wds, \
         nc.sbuf_tensor("ff_sgu", [128, 2 * NDC * 128], U8) as sgu, \
         nc.sbuf_tensor("ff_suu", [128, 2 * NDC * 128], U8) as suu, \
         nc.sbuf_tensor("ff_wdu", [128, 2 * NHC * 128], U8) as wdu, \
         nc.sbuf_tensor("ff_act", [128, GT], F32) as actt, \
         nc.sbuf_tensor("ff_x2", [128, 2 * GT], F32) as x2t, \
         nc.sbuf_tensor("ff_o", [128, 2 * GT], F32) as ot, \
         nc.psum_tensor("ff_PA", [128, 2048], F32) as PA, \
         nc.psum_tensor("ff_PB", [128, 2048], F32) as PB:
        xg3 = xg2[:, :].rearrange("p (a t) -> p a t", a=NDC)
        hid3 = hid2[:, :].rearrange("p (h t) -> p h t", h=NHC)
        ev = {}

        def rec(key, sem):
            ev[key] = (sem, sem.n)

        def wt(eng, *keys):
            for k in keys:
                if k in ev:
                    sm, n = ev[k]
                    eng.wait_ge(sm.h, n)

        for grp in range(2):
            t0 = grp * GT

            @blk.sync
            def _(sy, grp=grp, t0=t0):
                wt(sy, "xgrd")
                sy.wait_ge(x.s_wr.h, x.s_wr.n)
                x.s_ld.inc(sy.dma_start(
                    out=xg3[:, :, :],
                    in_=x.xln2_scr.rearrange("a p t -> p a t")[:, :, t0:t0 + GT]), 16)
            rec("xgld", x.s_ld)

            def gu_dma(hc):
                b = hc % 2
                sgu_b = sgu[:, b * NDC * 128:(b + 1) * NDC * 128]
                suu_b = suu[:, b * NDC * 128:(b + 1) * NDC * 128]

                @blk.sync
                def _(sy, hc=hc, sgu_b=sgu_b, suu_b=suu_b):
                    wt(sy, f"slabu{hc % 2}")
                    x.s_ld.inc(sy.dma_start(out=sgu_b, in_=x.wg_t[hc]), 16)
                    x.s_ld.inc(sy.dma_start(out=suu_b, in_=x.wu_t[hc]), 16)
                rec(f"slabld{hc % 2}", x.s_ld)

            def gu_deq(hc):
                b = hc % 2
                sgb = slg[:, b * NDC * 128:(b + 1) * NDC * 128]
                sub = slu[:, b * NDC * 128:(b + 1) * NDC * 128]
                sgu_b = sgu[:, b * NDC * 128:(b + 1) * NDC * 128]
                suu_b = suu[:, b * NDC * 128:(b + 1) * NDC * 128]

                @blk.vector
                def _(v, hc=hc, sgb=sgb, sub=sub, sgu_b=sgu_b, suu_b=suu_b):
                    wt(v, f"slabld{hc % 2}", f"slabrd{hc % 2}")
                    v.tensor_scalar(sgb, sgu_b, -128.0, None, ALU.add)
                    x.s_v.inc(v.tensor_scalar(sub, suu_b, -128.0, None, ALU.add))
                rec(f"deq{hc % 2}", x.s_v)
                rec(f"slabu{hc % 2}", x.s_v)

            gu_dma(0)
            gu_dma(1)
            gu_deq(0)
            for hc in range(NHC):
                b = hc % 2
                sgb = slg[:, b * NDC * 128:(b + 1) * NDC * 128]
                sub = slu[:, b * NDC * 128:(b + 1) * NDC * 128]
                pg = PA[:, b * 1024:(b + 1) * 1024]
                pu = PB[:, b * 1024:(b + 1) * 1024]
                if hc + 2 < NHC:
                    gu_dma(hc + 2)
                if hc + 1 < NHC:
                    gu_deq(hc + 1)

                sg3 = sgb.rearrange("p (a c) -> p a c", a=NDC)
                su3 = sub.rearrange("p (a c) -> p a c", a=NDC)

                @blk.tensor
                def _(t, hc=hc, sg3=sg3, su3=su3, pg=pg, pu=pu):
                    wt(t, f"deq{hc % 2}", "xgld", f"gu{hc % 2}")
                    if hc < 2:
                        wt(t, "pdrd0", "pdrd1")
                    for g2 in range(2):
                        for dc in range(NDC):
                            t.matmul(pg[:, g2 * G:(g2 + 1) * G], sg3[:, dc, :],
                                     xg3[:, dc, g2 * G:(g2 + 1) * G],
                                     start=(dc == 0), stop=(dc == NDC - 1))
                    for g2 in range(2):
                        for dc in range(NDC):
                            ins = t.matmul(pu[:, g2 * G:(g2 + 1) * G], su3[:, dc, :],
                                           xg3[:, dc, g2 * G:(g2 + 1) * G],
                                           start=(dc == 0), stop=(dc == NDC - 1))
                    x.s_pe.inc(ins)
                rec(f"slabrd{hc % 2}", x.s_pe)
                rec(f"gupe{hc}", x.s_pe)
                if hc == NHC - 1:
                    rec("xgrd", x.s_pe)

                @blk.scalar
                def _(sc, hc=hc, pg=pg):
                    wt(sc, f"gupe{hc}", "actrd")
                    x.s_s.inc(sc.activation(actt[:, :], pg, AF.Silu, bias=0.0,
                                            scale=x.sg_sc[:, hc:hc + 1]))
                rec(f"sil{hc}", x.s_s)

                @blk.vector
                def _(v, hc=hc, pu=pu):
                    wt(v, f"sil{hc}", "hidrd")
                    x.s_v.inc(v.scalar_tensor_tensor(hid3[:, hc, :], pu,
                                                     x.su_sc[:, hc:hc + 1],
                                                     actt[:, :], ALU.mult, ALU.mult))
                rec(f"gu{hc % 2}", x.s_v)
                rec("actrd", x.s_v)
                rec(f"hid{hc}", x.s_v)

            def wd_dma(dc):
                b = dc % 2
                wdu_b = wdu[:, b * NHC * 128:(b + 1) * NHC * 128]

                @blk.sync
                def _(sy, dc=dc, wdu_b=wdu_b):
                    wt(sy, f"wdu{dc % 2}")
                    x.s_ld.inc(sy.dma_start(out=wdu_b, in_=x.wd_t[dc]), 16)
                rec(f"wdld{dc % 2}", x.s_ld)

            def wd_deq(dc):
                b = dc % 2
                wdb = wds[:, b * NHC * 128:(b + 1) * NHC * 128]
                wdu_b = wdu[:, b * NHC * 128:(b + 1) * NHC * 128]

                @blk.vector
                def _(v, dc=dc, wdb=wdb, wdu_b=wdu_b):
                    wt(v, f"wdld{dc % 2}", f"wdrd{dc % 2}")
                    x.s_v.inc(v.tensor_scalar(wdb, wdu_b, -128.0, None, ALU.add))
                rec(f"wdeq{dc % 2}", x.s_v)
                rec(f"wdu{dc % 2}", x.s_v)

            wd_dma(0)
            wd_dma(1)
            wd_deq(0)
            for dc in range(NDC):
                b = dc % 2
                wdb = wds[:, b * NHC * 128:(b + 1) * NHC * 128]
                pd = PA[:, b * 1024:(b + 1) * 1024]
                x2b = x2t[:, b * GT:(b + 1) * GT]
                otb = ot[:, b * GT:(b + 1) * GT]
                if dc + 2 < NDC:
                    wd_dma(dc + 2)
                if dc + 1 < NDC:
                    wd_deq(dc + 1)

                @blk.sync
                def _(sy, dc=dc, t0=t0, x2b=x2b):
                    wt(sy, f"x2rd{dc % 2}")
                    x.s_ld.inc(sy.dma_start(out=x2b,
                                            in_=x.x2_scr[dc, :, t0:t0 + GT]), 16)
                rec(f"x2ld{dc % 2}", x.s_ld)

                wd3 = wdb.rearrange("p (a c) -> p a c", a=NHC)

                @blk.tensor
                def _(t, dc=dc, wd3=wd3, pd=pd):
                    wt(t, f"wdeq{dc % 2}", f"pdrd{dc % 2}")
                    if dc < 2:
                        wt(t, f"sil{NHC - 2}", f"sil{NHC - 1}")
                    for g2 in range(2):
                        for hc in range(NHC):
                            ins = t.matmul(pd[:, g2 * G:(g2 + 1) * G], wd3[:, hc, :],
                                           hid3[:, hc, g2 * G:(g2 + 1) * G],
                                           start=(hc == 0), stop=(hc == NHC - 1))
                    x.s_pe.inc(ins)
                rec(f"wdrd{dc % 2}", x.s_pe)
                rec(f"dpe{dc}", x.s_pe)
                if dc == NDC - 1:
                    rec("hidrd", x.s_pe)

                @blk.vector
                def _(v, dc=dc, pd=pd, x2b=x2b, otb=otb):
                    wt(v, f"dpe{dc}", f"x2ld{dc % 2}", f"otwr{dc % 2}")
                    x.s_v.inc(v.scalar_tensor_tensor(otb, pd, x.sd_sc[:, dc:dc + 1],
                                                     x2b, ALU.mult, ALU.add))
                rec(f"otV{dc % 2}", x.s_v)
                rec(f"x2rd{dc % 2}", x.s_v)
                rec(f"pdrd{dc % 2}", x.s_v)

                @blk.scalar
                def _(sc, dc=dc, t0=t0, otb=otb):
                    wt(sc, f"otV{dc % 2}")
                    x.s_wr.inc(sc.dma_start(
                        out=x.out_fm[dc * 128:(dc + 1) * 128, t0:t0 + GT],
                        in_=otb), 16)
                rec(f"otwr{dc % 2}", x.s_wr)


_CACHE = {}


def _tile_slabs(w2d):
    r, c = w2d.shape
    return np.ascontiguousarray(
        w2d.reshape(r // 128, 128, c // 128, 128).transpose(2, 1, 0, 3)
        .reshape(c // 128, 128, (r // 128) * 128))


def _lowbyte(a_int32):
    import sys
    a = np.ascontiguousarray(np.asarray(a_int32))
    return a.view(np.uint8).reshape(a.shape + (4,))[..., 0 if sys.byteorder == "little" else 3]


def _host_inputs(inputs):
    x = np.asarray(inputs["L0_attn_hidden_states"])
    wpj = np.concatenate([_tile_slabs(np.asarray(inputs[k]))
                          for k in ("L0_attn_Wq", "L0_attn_Wk", "L0_attn_Ws",
                                    "L0_attn_Wv", "L0_attn_Wg")], axis=0)
    shared = dict(
        wpj=wpj,
        wo_t=_tile_slabs(np.asarray(inputs["L0_attn_Wo"])),
        wg_t=_tile_slabs(np.ascontiguousarray(_lowbyte(inputs["L0_ffn_wq_gate"]).T)),
        wu_t=_tile_slabs(np.ascontiguousarray(_lowbyte(inputs["L0_ffn_wq_up"]).T)),
        wd_t=_tile_slabs(np.ascontiguousarray(_lowbyte(inputs["L0_ffn_wq_down"]).T)),
        sgate=np.ascontiguousarray(np.asarray(inputs["L0_ffn_s_gate"]).reshape(NHC, 128).T),
        sup=np.ascontiguousarray(np.asarray(inputs["L0_ffn_s_up"]).reshape(NHC, 128).T),
        sdown=np.ascontiguousarray(np.asarray(inputs["L0_ffn_s_down"]).reshape(NDC, 128).T),
        gnw=np.ascontiguousarray(np.asarray(inputs["L0_attn_g_norm_weight"]).reshape(128, 1)),
        m1u=np.ascontiguousarray(np.triu(np.ones((128, 128))).astype(_BF)),
        m1u8=np.ascontiguousarray(np.tile(np.triu(np.ones((128, 128))).astype(_BF), (1, 8))),
        ident=np.eye(128).astype(_BF),
        cst=np.ascontiguousarray(np.tile(np.array([CONSTS], np.float32), (128, 1))),
        onescol=np.ones((128, 1), dtype=_BF),
        onesrow=np.ones((1, 128), dtype=np.float32),
    )
    per_core = []
    for c in range(NCORES):
        b, qt = divmod(c, 4)
        mk = np.zeros((128, NCORES), np.float32)
        for r in range(b * 4, c):
            mk[:, r] = 1.0
        m = dict(shared)
        m["x_fm"] = np.ascontiguousarray(x[b, qt * T:(qt + 1) * T, :].T)
        m["maskpre"] = mk
        per_core.append(m)
    return per_core


def kernel(**inputs):
    if "nc" not in _CACHE:
        _CACHE["nc"] = build()
    nc = _CACHE["nc"]
    in_maps = _host_inputs(inputs)
    res = run_bass_kernel_spmd(nc, in_maps, core_ids=list(range(NCORES)))
    out = np.empty((2, 8192, D), np.float32)
    for c in range(NCORES):
        b, qt = divmod(c, 4)
        out[b, qt * T:(qt + 1) * T, :] = res.results[c]["out_fm"].T
    return out
